# version 06
# speedup vs baseline: 1.3911x; 1.0445x over previous
import numpy as np
import ml_dtypes
from contextlib import ExitStack
import concourse.bass as bass
import concourse.mybir as mybir
from concourse.bass import IndirectOffsetOnAxis
from concourse.bass_utils import run_bass_kernel_spmd
from concourse.alu_op_type import AluOpType

F32 = mybir.dt.float32
BF16 = mybir.dt.bfloat16
U32 = mybir.dt.uint32
AF = mybir.ActivationFunctionType
OP = AluOpType

NCORES = 8
T = 2048
D = 1024
NCH = 8
RC = 1.5 * (2.0 ** 23)
BF = ml_dtypes.bfloat16


class Tok:
    __slots__ = ("eng", "cum", "dma", "key", "val")
    def __init__(self, eng, cum, dma, key=None, val=None):
        self.eng, self.cum, self.dma = eng, cum, dma
        self.key = key if key is not None else eng
        self.val = val if val is not None else cum

NDMASEM = 48

class TB:
    ENGS = ("tensor", "vector", "scalar", "gpsimd", "sync")

    def __init__(self, nc, ctx):
        self.nc, self.ctx = nc, ctx
        self.sems = {e: ctx.enter_context(nc.semaphore(f"S_{e}")) for e in self.ENGS}
        for i in range(NDMASEM):
            self.sems[f"dma{i}"] = ctx.enter_context(nc.semaphore(f"SD_{i}"))
        with nc.Block() as blk0:
            @blk0.gpsimd
            def _(g):
                for k in self.sems:
                    g.sem_clear(self.sems[k])
        self.cum = {e: 0 for e in self.ENGS}
        self.dmause = [0] * NDMASEM
        self.dmaidx = 0
        self.ops = {e: [] for e in self.ENGS}
        self.mem = {}
        self.width = {}

    def sbuf(self, name, shape, dt):
        t = self.ctx.enter_context(self.nc.sbuf_tensor(name, shape, dt))
        self.width[name] = shape[1] if len(shape) == 2 else int(np.prod(shape[1:]))
        return t

    def psum(self, name, shape, dt=F32):
        t = self.ctx.enter_context(self.nc.psum_tensor(name, shape, dt))
        self.width[name] = shape[1] if len(shape) == 2 else int(np.prod(shape[1:]))
        return t

    def dram(self, name, shape, dt, kind=None):
        t = (self.nc.dram_tensor(name, shape, dt, kind=kind) if kind
             else self.nc.dram_tensor(name, shape, dt))
        self.width[name] = shape[-1]
        return t

    def _region(self, ap):
        try:
            name = ap.name
            shp = ap.shape
            pb = ap.base_partition()
            if len(shp) == 2 and name in self.width:
                w = self.width[name]
                c0 = ap.offset - pb * w
                if 0 <= c0 and c0 + shp[1] <= w:
                    return (name, pb, shp[0], c0, shp[1], True)
            return (name, pb, shp[0], 0, 1 << 30, False)
        except Exception:
            return (getattr(ap, "name", "?"), 0, 1 << 30, 0, 1 << 30, False)

    @staticmethod
    def _olap(a, b):
        if a[0] != b[0]:
            return False
        if not (a[1] < b[1] + b[2] and b[1] < a[1] + a[2]):
            return False
        if a[5] and b[5]:
            return a[3] < b[3] + b[4] and b[3] < a[3] + a[4]
        return True

    def _deps_for(self, reads, writes):
        deps = []
        for ap in reads:
            r = self._region(ap)
            for rec in self.mem.get(r[0], ()):
                if rec[2] and self._olap(r, rec[0]):
                    deps.append((rec[1], False))
        for ap in writes:
            r = self._region(ap)
            for rec in self.mem.get(r[0], ()):
                if self._olap(r, rec[0]):
                    deps.append((rec[1], not rec[2]))
        return deps

    def _record(self, reads, writes, tok):
        for ap, isw in [(a, False) for a in reads] + [(a, True) for a in writes]:
            r = self._region(ap)
            lst = self.mem.setdefault(r[0], [])
            lst.append((r, tok, isw))
            if len(lst) > 240:
                old, keep = lst[:120], lst[120:]
                best = {}
                for rr, tk, _ in old:
                    if tk.key not in best or tk.val > best[tk.key].val:
                        best[tk.key] = tk
                name = r[0]
                lst[:] = [((name, 0, 1 << 30, 0, 1 << 30, False), tk, True)
                          for tk in best.values()] + keep

    def _needs(self, eng, alldeps, extra_need=None):
        need = dict(extra_need or {})
        for d, is_war in alldeps:
            if d.eng == eng and not d.dma:
                if is_war:
                    continue
                if eng == "tensor":
                    continue
            if d.key in need:
                need[d.key] = max(need[d.key], d.val)
            else:
                need[d.key] = d.val
        return need

    def op(self, eng, fn, reads=(), writes=(), deps=(), inc=1):
        alldeps = [(d, False) for d in deps if d is not None] + \
                  self._deps_for(reads, writes)
        self.cum[eng] += inc
        tok = Tok(eng, self.cum[eng], False)
        need = self._needs(eng, alldeps)
        self.ops[eng].append((fn, need, (eng, inc)))
        self._record(reads, writes, tok)
        return tok

    def _dmasem(self, queue):
        if queue == "gpsimd":
            self.gidx = getattr(self, "gidx", 0)
            j = self.gidx % 4
            self.gidx += 1
        else:
            j = 4 + self.dmaidx % (NDMASEM - 4)
            self.dmaidx += 1
        return j

    def dma(self, queue, out, in_, reads=(), writes=(), deps=(), **kw):
        if "single_packet" not in kw:
            try:
                nb = min(out.nbytes, in_.nbytes)
            except Exception:
                nb = 1 << 30
            if nb <= 12288:
                kw["single_packet"] = True
        alldeps = [(d, False) for d in deps if d is not None] + \
                  self._deps_for(list(reads) + [in_], list(writes) + [out])
        j = self._dmasem(queue)
        self.dmause[j] += 1
        key = f"dma{j}"
        tok = Tok(queue, 0, True, key, 16 * self.dmause[j])
        extra = {key: 16 * (self.dmause[j] - 1)} if self.dmause[j] > 1 else None
        need = self._needs(queue, alldeps, extra)
        sem = self.sems[key]
        self.ops[queue].append(
            (lambda e, out=out, in_=in_, kw=kw: e.dma_start(out=out, in_=in_, **kw),
             need, (key, 16)))
        self._record(list(reads) + [in_], list(writes) + [out], tok)
        return tok

    def idma(self, out, in_, offset_ap, side, bounds, reads=(), writes=(), deps=()):
        alldeps = [(d, False) for d in deps if d is not None] + \
                  self._deps_for(list(reads) + [in_, offset_ap], list(writes) + [out])
        j = self._dmasem("gpsimd")
        self.dmause[j] += 1
        key = f"dma{j}"
        tok = Tok("gpsimd", 0, True, key, 16 * self.dmause[j])
        extra = {key: 16 * (self.dmause[j] - 1)} if self.dmause[j] > 1 else None
        need = self._needs("gpsimd", alldeps, extra)
        ioff = IndirectOffsetOnAxis(ap=offset_ap, axis=0)
        if side == "out":
            fn = lambda e: e.indirect_dma_start(out=out, out_offset=ioff, in_=in_,
                                                in_offset=None, bounds_check=bounds,
                                                oob_is_err=False)
        else:
            fn = lambda e: e.indirect_dma_start(out=out, out_offset=None, in_=in_,
                                                in_offset=ioff, bounds_check=bounds,
                                                oob_is_err=False)
        self.ops["gpsimd"].append((fn, need, (key, 16)))
        self._record(list(reads) + [in_, offset_ap], list(writes) + [out], tok)
        return tok

    def emit(self):
        block = self.ctx.enter_context(self.nc.Block())
        for eng in self.ENGS:
            ops = self.ops[eng]
            if not ops:
                continue
            def make(eng, ops):
                def body(proxy):
                    water = {}
                    for fn, need, (inckey, inc) in ops:
                        for sk in sorted(need):
                            if need[sk] > water.get(sk, 0):
                                proxy.wait_ge(self.sems[sk], need[sk])
                                water[sk] = need[sk]
                        fn(proxy).then_inc(self.sems[inckey], inc)
                return body
            getattr(block, eng)(make(eng, list(ops)))

    def finalize(self):
        fin = {e: self.cum[e] for e in self.ENGS if self.cum[e] > 0 and e != "gpsimd"}
        for j in range(NDMASEM):
            if self.dmause[j]:
                fin[f"dma{j}"] = 16 * self.dmause[j]
        def fn(proxy):
            for k in sorted(fin):
                proxy.wait_ge(self.sems[k], fin[k])
            return proxy.nop()
        self.cum["gpsimd"] += 1
        self.ops["gpsimd"].append((fn, {}, ("gpsimd", 1)))
        self.emit()



def h_round_clip(b, out_ap, in_ap, inv_scale_col, lo, hi, extra_reads=()):
    b.op("vector", lambda v: v.tensor_scalar(
        out=out_ap, in0=in_ap, scalar1=inv_scale_col, scalar2=RC,
        op0=OP.mult, op1=OP.add),
        reads=[in_ap, inv_scale_col] + list(extra_reads), writes=[out_ap])
    b.op("vector", lambda v: v.tensor_scalar(
        out=out_ap, in0=out_ap, scalar1=-RC, scalar2=float(hi),
        op0=OP.add, op1=OP.min), reads=[out_ap], writes=[out_ap])
    return b.op("vector", lambda v: v.tensor_scalar(
        out=out_ap, in0=out_ap, scalar1=float(lo), scalar2=None,
        op0=OP.max), reads=[out_ap], writes=[out_ap])


def h_bcast_row(b, ps, dst, src_row, ones_row, n):
    for cs in range(0, n, 512):
        w = min(512, n - cs)
        b.op("tensor", lambda t, cs=cs, w=w: t.matmul(
            ps[:, 0:w], ones_row[:], src_row[0:1, cs:cs + w], start=True, stop=True),
            reads=[ones_row[:], src_row[0:1, cs:cs + w]], writes=[ps[:, 0:w]])
        b.op("vector", lambda v, cs=cs, w=w: v.tensor_copy(dst[:, cs:cs + w], ps[:, 0:w]),
             reads=[ps[:, 0:w]], writes=[dst[:, cs:cs + w]])



def build_L1():
    nc = bass.Bass("TRN2", target_bir_lowering=False)
    with ExitStack() as ctx:
        b = TB(nc, ctx)
        x_in = b.dram("x_fm", [D, T], F32, kind="ExternalInput").ap()
        w_in = b.dram("wqkd_fm", [D, 3 * D], F32, kind="ExternalInput").ap()
        ident_in = b.dram("ident", [128, 128], F32, kind="ExternalInput").ap()
        xn_out = b.dram("xn_fm", [D, T], F32, kind="ExternalOutput").ap()
        qkd_out = b.dram("qkd_bf", [3 * D, T], BF16, kind="ExternalOutput").ap()
        mx_out = b.dram("maxabs_xn", [1, 1], F32, kind="ExternalOutput").ap()

        x = [b.sbuf(f"x{i}", [128, T], F32) for i in range(NCH)]
        xnb = [b.sbuf(f"xnb{i}", [128, T], BF16) for i in range(NCH)]
        sq = b.sbuf("sq", [128, 512], F32)
        wcol = [b.sbuf(f"wcol{i}", [128, D], BF16) for i in range(2)]
        ones_col = b.sbuf("ones_col", [128, 1], F32)
        ones_row = b.sbuf("ones_row", [1, 128], F32)
        sx_row = b.sbuf("sx_row", [1, T], F32)
        sq_row = b.sbuf("sq_row", [1, T], F32)
        mubc = b.sbuf("mubc", [128, T], F32)
        rstdbc = b.sbuf("rstdbc", [128, T], F32)
        mxcol = b.sbuf("mxcol", [128, NCH], F32)
        mx1 = b.sbuf("mx1", [128, 1], F32)
        mxr = b.sbuf("mxr", [1, 128], F32)
        ident = b.sbuf("identsb", [128, 128], F32)
        otile = [b.sbuf(f"ot{i}", [128, T], BF16) for i in range(2)]
        ps_stat_t = b.psum("ps_stat_t", [128, 512], F32)
        ps_stat2_t = b.psum("ps_stat2_t", [128, 512], F32)
        ps_stat = ps_stat_t[0:1, :]
        ps_stat2 = ps_stat2_t[0:1, :]
        ps_bc = b.psum("ps_bc", [128, 512], F32)
        ps_tr = b.psum("ps_tr", [128, 512], F32)
        ps_mm = [b.psum(f"ps_mm{i}", [128, 512], F32) for i in range(2)]

        for i in range(NCH):
            b.dma("sync", x[i][:], x_in[128 * i:128 * (i + 1), :])
        b.dma("sync", ident[:], ident_in)
        b.op("vector", lambda v: v.memset(ones_col[:], 1.0), writes=[ones_col[:]])
        b.op("vector", lambda v: v.memset(ones_row[:], 1.0), writes=[ones_row[:]])

        for cs in range(4):
            sl = slice(512 * cs, 512 * (cs + 1))
            for i in range(NCH):
                b.op("tensor", lambda t, i=i, sl=sl: t.matmul(
                    ps_stat, ones_col[:], x[i][:, sl],
                    start=(i == 0), stop=(i == NCH - 1)),
                    reads=[x[i][:, sl], ones_col[:]], writes=[ps_stat])
            b.op("vector", lambda v, sl=sl: v.tensor_copy(sx_row[0:1, sl], ps_stat),
                 reads=[ps_stat], writes=[sx_row[0:1, sl]])
            for i in range(NCH):
                b.op("scalar", lambda s, i=i, sl=sl: s.activation(
                    out=sq[:, :], in_=x[i][:, sl], func=AF.Square),
                    reads=[x[i][:, sl]], writes=[sq[:, :]])
                b.op("tensor", lambda t, i=i: t.matmul(
                    ps_stat2, ones_col[:], sq[:, :],
                    start=(i == 0), stop=(i == NCH - 1)),
                    reads=[sq[:, :], ones_col[:]], writes=[ps_stat2])
            b.op("vector", lambda v, sl=sl: v.tensor_copy(sq_row[0:1, sl], ps_stat2),
                 reads=[ps_stat2], writes=[sq_row[0:1, sl]])
        b.op("vector", lambda v: v.tensor_scalar(out=sx_row[:], in0=sx_row[:],
             scalar1=1.0 / D, scalar2=None, op0=OP.mult),
             reads=[sx_row[:]], writes=[sx_row[:]])
        h_bcast_row(b, ps_bc, mubc, sx_row, ones_row, T)
        b.op("vector", lambda v: v.tensor_tensor(out=sx_row[:], in0=sx_row[:],
             in1=sx_row[:], op=OP.mult), reads=[sx_row[:]], writes=[sx_row[:]])
        b.op("vector", lambda v: v.tensor_scalar(out=sq_row[:], in0=sq_row[:],
             scalar1=1.0 / D, scalar2=None, op0=OP.mult),
             reads=[sq_row[:]], writes=[sq_row[:]])
        b.op("vector", lambda v: v.tensor_tensor(out=sq_row[:], in0=sq_row[:],
             in1=sx_row[:], op=OP.subtract), reads=[sq_row[:], sx_row[:]], writes=[sq_row[:]])
        b.op("vector", lambda v: v.tensor_scalar(out=sq_row[:], in0=sq_row[:],
             scalar1=1e-5, scalar2=None, op0=OP.add), reads=[sq_row[:]], writes=[sq_row[:]])
        b.op("scalar", lambda s: s.activation(out=sq_row[:], in_=sq_row[:],
             func=AF.Sqrt), reads=[sq_row[:]], writes=[sq_row[:]])
        b.op("vector", lambda v: v.reciprocal(out=sq_row[:], in_=sq_row[:]),
             reads=[sq_row[:]], writes=[sq_row[:]])
        h_bcast_row(b, ps_bc, rstdbc, sq_row, ones_row, T)

        for i in range(NCH):
            b.op("vector", lambda v, i=i: v.tensor_tensor(out=x[i][:], in0=x[i][:],
                 in1=mubc[:], op=OP.subtract), reads=[x[i][:], mubc[:]], writes=[x[i][:]])
            b.op("vector", lambda v, i=i: v.tensor_tensor(out=x[i][:], in0=x[i][:],
                 in1=rstdbc[:], op=OP.mult), reads=[x[i][:], rstdbc[:]], writes=[x[i][:]])
            b.op("vector", lambda v, i=i: v.reduce_max(out=mxcol[:, i:i + 1], in_=x[i][:],
                 axis=mybir.AxisListType.X, apply_absolute_value=True),
                 reads=[x[i][:]], writes=[mxcol[:, i:i + 1]])
            b.op("vector", lambda v, i=i: v.tensor_copy(xnb[i][:], x[i][:]),
                 reads=[x[i][:]], writes=[xnb[i][:]])
            b.dma("sync", xn_out[128 * i:128 * (i + 1), :], x[i][:])
        b.op("vector", lambda v: v.reduce_max(out=mx1[:], in_=mxcol[:],
             axis=mybir.AxisListType.X), reads=[mxcol[:]], writes=[mx1[:]])
        b.op("tensor", lambda t: t.transpose(ps_tr[0:1, 0:128], mx1[:], ident[:]),
             reads=[mx1[:], ident[:]], writes=[ps_tr[0:1, 0:128]])
        b.op("vector", lambda v: v.tensor_copy(mxr[:], ps_tr[0:1, 0:128]),
             reads=[ps_tr[0:1, 0:128]], writes=[mxr[:]])
        b.op("vector", lambda v: v.reduce_max(out=mxr[0:1, 0:1], in_=mxr[:],
             axis=mybir.AxisListType.X), reads=[mxr[:]], writes=[mxr[:]])
        b.dma("sync", mx_out, mxr[0:1, 0:1])

        for w in range(3):
            for ot in range(8):
                wc = wcol[(w * 8 + ot) % 2]
                col0 = D * w + 128 * ot
                src = w_in[:, col0:col0 + 128].rearrange("(c p) j -> p c j", p=128)
                b.dma("gpsimd", wc[:].rearrange("p (c j) -> p c j", c=NCH), src)
                obuf = otile[(w * 8 + ot) % 2]
                for cs in range(4):
                    ps = ps_mm[cs % 2]
                    sl = slice(512 * cs, 512 * (cs + 1))
                    for i in range(NCH):
                        b.op("tensor", lambda t, i=i, sl=sl, ps=ps, wc=wc: t.matmul(
                            ps[:, :], wc[:, 128 * i:128 * (i + 1)],
                            xnb[i][:, sl], start=(i == 0), stop=(i == NCH - 1)),
                            reads=[wc[:, 128 * i:128 * (i + 1)], xnb[i][:, sl]],
                            writes=[ps[:, :]])
                    b.op("scalar", lambda s, sl=sl, ps=ps, obuf=obuf: s.activation(
                        out=obuf[:, sl], in_=ps[:, :], func=AF.Copy),
                        reads=[ps[:, :]], writes=[obuf[:, sl]])
                b.dma("sync", qkd_out[col0:col0 + 128, :], obuf[:])
        b.finalize()
    return nc



S_FULL = 8192

import os
L2_PARTS = os.environ.get("L2_PARTS", "AB")

def build_L2():
    nc = bass.Bass("TRN2", target_bir_lowering=False)
    with ExitStack() as ctx:
        b = TB(nc, ctx)
        qh_in = b.dram("qh_bf", [256, S_FULL], BF16, kind="ExternalInput").ap()
        kh_in = b.dram("kh_bf", [256, S_FULL], BF16, kind="ExternalInput").ap()
        xn_in = b.dram("xn_fm", [D, T], F32, kind="ExternalInput").ap()
        mx_in = b.dram("maxxn_all", [1, NCORES], F32, kind="ExternalInput").ap()
        clip_in = b.dram("clip_x", [1, 1], F32, kind="ExternalInput").ap()
        wg_in = b.dram("wg_t", [D, 2816], F32, kind="ExternalInput").ap()
        wu_in = b.dram("wu_t", [D, 2816], F32, kind="ExternalInput").ap()
        ident_in = b.dram("ident", [128, 128], F32, kind="ExternalInput").ap()
        diag_out = b.dram("attn_diag", [512, 64], F32, kind="ExternalOutput").ap()
        h_out = b.dram("h_fm", [2816, T], F32, kind="ExternalOutput").ap()
        mxh_out = b.dram("maxh", [1, 1], F32, kind="ExternalOutput").ap()
        wq_dram = b.dram("wq_scratch", [2 * D, 2816], BF16).ap()

        qsc = b.sbuf("qsc", [64, S_FULL], BF16)
        ksc = b.sbuf("ksc", [64, S_FULL], BF16)
        xq = [b.sbuf(f"xq{i}", [128, T], BF16) for i in range(NCH)]
        xbuf = [b.sbuf(f"xb{i}", [128, T], F32) for i in range(2)]
        wstr = [b.sbuf(f"wstr{i}", [128, 2816], F32) for i in range(2)]
        wacc = b.sbuf("wacc", [128, 2816], F32)
        rbc = b.sbuf("rbc", [128, 2816], F32)
        wqtile = [b.sbuf(f"wqt{i}", [128, 2816], BF16) for i in range(2)]
        wcol = [b.sbuf(f"wcol{i}", [128, D], BF16) for i in range(2)]
        ident = b.sbuf("identsb", [128, 128], F32)
        ones_col = b.sbuf("ones_col", [128, 1], F32)
        ones_row = b.sbuf("ones_row", [1, 128], F32)
        rrow = b.sbuf("rrow", [1, 2816], F32)
        scale_g = b.sbuf("scale_g", [128, 22], F32)
        scale_u = b.sbuf("scale_u", [128, 22], F32)
        comb_g = b.sbuf("comb_g", [128, 22], F32)
        comb_u = b.sbuf("comb_u", [128, 22], F32)
        r22 = b.sbuf("r22", [22, 128], F32)
        sc1 = b.sbuf("sc1", [1, NCORES], F32)
        sxcol = b.sbuf("sxcol", [128, 1], F32)
        invsx = b.sbuf("invsx", [128, 1], F32)
        gsil = b.sbuf("gsil", [128, T], F32)
        htile = b.sbuf("htile", [128, T], F32)
        mxh_cols = b.sbuf("mxh_cols", [128, 22], F32)
        mx1 = b.sbuf("mx1", [128, 1], F32)
        mxr = b.sbuf("mxr", [1, 128], F32)
        dscr = b.sbuf("dscr", [64, 128], F32)
        escr = b.sbuf("escr", [128, 128], F32)
        escr2 = b.sbuf("escr2", [128, 128], F32)
        eth = [b.sbuf(f"eth{i}", [128, 512], F32) for i in range(2)]
        accb = b.sbuf("accb", [128, 18], F32)
        dcol = b.sbuf("dcol", [128, 1], F32)
        dent = b.sbuf("dent", [128, 1], F32)
        dat = b.sbuf("dat", [128, 64], F32)
        scr_dram = b.dram("scr22", [22, 128], F32).ap()

        ps_mm = [b.psum(f"ps_mm{i}", [128, 512], F32) for i in range(2)]
        ps_bc = b.psum("ps_bc", [128, 512], F32)
        ps_tr = b.psum("ps_tr", [128, 512], F32)
        ps_sc = [b.psum(f"ps_sc{i}", [128, 512], F32) for i in range(2)]
        ps_dg = b.psum("ps_dg", [128, 512], F32)

        b.dma("sync", ident[:], ident_in)
        b.op("vector", lambda v: v.memset(ones_col[:], 1.0), writes=[ones_col[:]])
        b.op("vector", lambda v: v.memset(ones_row[:], 1.0), writes=[ones_row[:]])


        gmax1 = b.sbuf("gmax1", [1, 1], F32)
        snap = b.sbuf("snap", [1, NCORES], F32)
        b.dma("sync", sc1[0:1, 0:NCORES], mx_in)
        b.op("vector", lambda v: v.tensor_copy(snap[:], sc1[0:1, 0:NCORES]),
             reads=[sc1[:]], writes=[snap[:]])
        b.op("vector", lambda v: v.reduce_max(out=gmax1[:], in_=sc1[0:1, 0:NCORES],
             axis=mybir.AxisListType.X), reads=[sc1[:]], writes=[gmax1[:]])
        cl = b.sbuf("cl", [1, 1], F32)
        b.dma("sync", cl[:], clip_in)
        b.op("scalar", lambda s: s.activation(out=cl[:], in_=cl[:], func=AF.Sigmoid),
             reads=[cl[:]], writes=[cl[:]])
        b.op("vector", lambda v: v.tensor_tensor(out=cl[:], in0=cl[:], in1=gmax1[:],
             op=OP.mult), reads=[cl[:], gmax1[:]], writes=[cl[:]])
        b.op("vector", lambda v: v.tensor_scalar(out=cl[:], in0=cl[:], scalar1=1.0 / 127.0,
             scalar2=None, op0=OP.mult), reads=[cl[:]], writes=[cl[:]])
        b.op("tensor", lambda t: t.matmul(ps_bc[:, 0:1], ones_row[:], cl[:],
             start=True, stop=True), reads=[ones_row[:], cl[:]], writes=[ps_bc[:, 0:1]])
        b.op("vector", lambda v: v.tensor_copy(sxcol[:], ps_bc[:, 0:1]),
             reads=[ps_bc[:, 0:1]], writes=[sxcol[:]])
        b.op("vector", lambda v: v.reciprocal(out=invsx[:], in_=sxcol[:]),
             reads=[sxcol[:]], writes=[invsx[:]])

        for i in (range(NCH) if "A" in L2_PARTS else range(0)):
            xb = xbuf[i % 2]
            b.dma("sync", xb[:], xn_in[128 * i:128 * (i + 1), :])
            b.op("vector", lambda v, xb=xb: v.tensor_scalar(
                out=xb[:], in0=xb[:], scalar1=invsx[:], scalar2=RC,
                op0=OP.mult, op1=OP.add), reads=[xb[:], invsx[:]], writes=[xb[:]])
            b.op("vector", lambda v, xb=xb: v.tensor_scalar(
                out=xb[:], in0=xb[:], scalar1=-RC, scalar2=127.0,
                op0=OP.add, op1=OP.min), reads=[xb[:]], writes=[xb[:]])
            b.op("vector", lambda v, xb=xb, i=i: v.tensor_scalar(
                out=xq[i][:], in0=xb[:], scalar1=-128.0, scalar2=None,
                op0=OP.max), reads=[xb[:]], writes=[xq[i][:]])

        def wquant(w_in_ap, scale_cols, wq_row0):
            for i in range(NCH):
                ws = wstr[i % 2]
                b.dma("sync", ws[:], w_in_ap[128 * i:128 * (i + 1), :])
                b.op("scalar", lambda s, ws=ws: s.activation(out=ws[:], in_=ws[:],
                     func=AF.Abs), reads=[ws[:]], writes=[ws[:]])
                if i == 0:
                    b.op("vector", lambda v, ws=ws: v.tensor_copy(wacc[:], ws[:]),
                         reads=[ws[:]], writes=[wacc[:]])
                else:
                    b.op("vector", lambda v, ws=ws: v.tensor_tensor(out=wacc[:], in0=wacc[:],
                         in1=ws[:], op=OP.max), reads=[wacc[:], ws[:]], writes=[wacc[:]])
            for blk in range(22):
                b.op("tensor", lambda t, blk=blk: t.transpose(
                    ps_tr[:, 0:128], wacc[:, 128 * blk:128 * (blk + 1)], ident[:]),
                    reads=[wacc[:], ident[:]], writes=[ps_tr[:, 0:128]])
                b.op("vector", lambda v, blk=blk: v.reduce_max(
                    out=scale_cols[:, blk:blk + 1], in_=ps_tr[:, 0:128],
                    axis=mybir.AxisListType.X),
                    reads=[ps_tr[:, 0:128]], writes=[scale_cols[:, blk:blk + 1]])
            rcols = b.sbuf(f"rcols{wq_row0}", [128, 22], F32)
            b.op("vector", lambda v, rcols=rcols: v.reciprocal(out=rcols[:], in_=scale_cols[:]),
                 reads=[scale_cols[:]], writes=[rcols[:]])
            b.op("vector", lambda v, rcols=rcols: v.tensor_scalar(out=rcols[:], in0=rcols[:],
                 scalar1=127.0, scalar2=None, op0=OP.mult), reads=[rcols[:]], writes=[rcols[:]])
            b.op("tensor", lambda t, rcols=rcols: t.transpose(ps_tr[0:22, 0:128], rcols[:], ident[:]),
                 reads=[rcols[:], ident[:]], writes=[ps_tr[0:22, 0:128]])
            b.op("vector", lambda v: v.tensor_copy(r22[:], ps_tr[0:22, 0:128]),
                 reads=[ps_tr[:, :]], writes=[r22[:]])
            b.dma("sync", scr_dram, r22[:])
            b.dma("sync", rrow[:], scr_dram.rearrange("a b -> (a b)").unsqueeze(0))
            h_bcast_row(b, ps_bc, rbc, rrow, ones_row, 2816)
            for i in range(NCH):
                ws = wstr[i % 2]
                wqt = wqtile[i % 2]
                b.dma("sync", ws[:], w_in_ap[128 * i:128 * (i + 1), :])
                b.op("vector", lambda v, ws=ws: v.tensor_tensor(out=ws[:], in0=ws[:],
                     in1=rbc[:], op=OP.mult), reads=[ws[:], rbc[:]], writes=[ws[:]])
                b.op("vector", lambda v, ws=ws: v.tensor_scalar(out=ws[:], in0=ws[:],
                     scalar1=RC, scalar2=-RC, op0=OP.add, op1=OP.add),
                     reads=[ws[:]], writes=[ws[:]])
                b.op("vector", lambda v, ws=ws, wqt=wqt: v.tensor_scalar(out=wqt[:], in0=ws[:],
                     scalar1=127.0, scalar2=-127.0, op0=OP.min, op1=OP.max),
                     reads=[ws[:]], writes=[wqt[:]])
                b.dma("sync", wq_dram[wq_row0 + 128 * i: wq_row0 + 128 * (i + 1), :], wqt[:])
        if "A" in L2_PARTS:
            wquant(wg_in, scale_g, 0)
            wquant(wu_in, scale_u, D)
        for sc_, cb_ in ([(scale_g, comb_g), (scale_u, comb_u)] if "A" in L2_PARTS else []):
            b.op("vector", lambda v, sc_=sc_, cb_=cb_: v.tensor_scalar(
                out=cb_[:], in0=sc_[:], scalar1=sxcol[:], scalar2=1.0 / 127.0,
                op0=OP.mult, op1=OP.mult), reads=[sc_[:], sxcol[:]], writes=[cb_[:]])

        for it in (range(22) if "A" in L2_PARTS else range(0)):
            for w in range(2):
                wc = wcol[w]
                col0 = 128 * it
                src = wq_dram[D * w:D * w + D, col0:col0 + 128].rearrange(
                    "(c p) j -> p c j", p=128)
                b.dma("sync", wc[:].rearrange("p (c j) -> p c j", c=NCH), src)
                for cs in range(4):
                    ps = ps_mm[cs % 2]
                    sl = slice(512 * cs, 512 * (cs + 1))
                    for i in range(NCH):
                        b.op("tensor", lambda t, i=i, sl=sl, ps=ps, wc=wc: t.matmul(
                            ps[:, :], wc[:, 128 * i:128 * (i + 1)], xq[i][:, sl],
                            start=(i == 0), stop=(i == NCH - 1)),
                            reads=[wc[:, 128 * i:128 * (i + 1)], xq[i][:, sl]],
                            writes=[ps[:, :]])
                    if w == 0:
                        b.op("scalar", lambda s, sl=sl, ps=ps, it=it: s.activation(
                            out=gsil[:, sl], in_=ps[:, :], func=AF.Silu,
                            scale=comb_g[:, it:it + 1]),
                            reads=[ps[:, :], comb_g[:, it:it + 1]], writes=[gsil[:, sl]])
                    else:
                        b.op("vector", lambda v, sl=sl, ps=ps, it=it: v.scalar_tensor_tensor(
                            out=htile[:, sl], in0=ps[:, :], scalar=comb_u[:, it:it + 1],
                            in1=gsil[:, sl], op0=OP.mult, op1=OP.mult),
                            reads=[ps[:, :], comb_u[:, it:it + 1], gsil[:, sl]],
                            writes=[htile[:, sl]])
            b.op("vector", lambda v, it=it: v.reduce_max(out=mxh_cols[:, it:it + 1],
                 in_=htile[:], axis=mybir.AxisListType.X, apply_absolute_value=True),
                 reads=[htile[:]], writes=[mxh_cols[:, it:it + 1]])
            b.dma("sync", h_out[128 * it:128 * (it + 1), :], htile[:])
        if "A" not in L2_PARTS:
            b.op("vector", lambda v: v.memset(mxh_cols[:], 0.0), writes=[mxh_cols[:]])
            b.op("vector", lambda v: v.memset(dat[:], 0.0), writes=[dat[:]])
            for u in range(4):
                b.dma("sync", diag_out[128 * u:128 * (u + 1), :], dat[:])
            for it in range(22):
                b.op("vector", lambda v: v.memset(htile[:], 0.0), writes=[htile[:]])
                b.dma("sync", h_out[128 * it:128 * (it + 1), :], htile[:])
        b.op("vector", lambda v: v.reduce_max(out=mx1[:], in_=mxh_cols[:],
             axis=mybir.AxisListType.X), reads=[mxh_cols[:]], writes=[mx1[:]])
        b.op("tensor", lambda t: t.transpose(ps_tr[0:1, 0:128], mx1[:], ident[:]),
             reads=[mx1[:], ident[:]], writes=[ps_tr[0:1, 0:128]])
        b.op("vector", lambda v: v.tensor_copy(mxr[:], ps_tr[0:1, 0:128]),
             reads=[ps_tr[:, :]], writes=[mxr[:]])
        b.op("vector", lambda v: v.reduce_max(out=mxr[0:1, 0:1], in_=mxr[:],
             axis=mybir.AxisListType.X), reads=[mxr[:]], writes=[mxr[:]])
        b.dma("sync", mxh_out, mxr[0:1, 0:1])

        NU = int(os.environ.get("L2B_UNITS", "4"))
        NR = int(os.environ.get("L2B_ROWS", "64"))
        SKIPREG = os.environ.get("L2B_SKIPREG", "") == "1"
        SKIPDIAG = os.environ.get("L2B_SKIPDIAG", "") == "1"
        tri_in = b.dram("trimask", [128, 128], F32, kind="ExternalInput").ap()
        trimask = b.sbuf("trimask_sb", [128, 128], F32)
        b.dma("sync", trimask[:], tri_in)
        accb2 = [accb, b.sbuf("accb_b", [128, 18], F32)]
        dcol2 = [dcol, b.sbuf("dcol_b", [128, 1], F32)]
        dent2 = [dent, b.sbuf("dent_b", [128, 1], F32)]
        escrE = [escr, escr2]
        esum = [b.sbuf(f"esum{i}", [128, 128], F32) for i in range(2)]
        eth4 = [b.sbuf(f"ethb{i}", [128, 512], BF16) for i in range(4)]
        ps_dg2 = [ps_dg, ps_bc]
        ps_sc4 = ps_sc + ps_mm
        for u in (range(NU) if "B" in L2_PARTS else range(0)):
            b.dma("sync", qsc[:], qh_in[64 * u:64 * (u + 1), :])
            b.dma("sync", ksc[:], kh_in[64 * u:64 * (u + 1), :])
            qt, kt = qsc, ksc
            pb = 0
            for r in range(NR):
                par = r % 2
                rs = slice(128 * r, 128 * (r + 1))
                ksl = kt[pb:pb + 64, rs]
                pd = ps_dg2[par]
                ac = accb2[par]
                b.op("tensor", lambda t, ksl=ksl, qt=qt, pb=pb, rs=rs, pd=pd: t.matmul(
                    pd[:, 0:128], ksl, qt[pb:pb + 64, rs], start=True, stop=True),
                    reads=[ksl, qt[pb:pb + 64, rs]], writes=[pd[:, 0:128]])
                b.op("vector", lambda v, pd=pd, par=par: v.tensor_tensor(
                    out=esum[par][:], in0=pd[:, 0:128], in1=trimask[:], op=OP.add),
                    reads=[pd[:, 0:128], trimask[:]], writes=[esum[par][:]])
                b.op("scalar", lambda s, par=par, ac=ac: s.activation(
                    out=escrE[par][:], in_=esum[par][:], func=AF.Exp, scale=0.125,
                    accum_out=ac[:, 0:1]),
                    reads=[esum[par][:]], writes=[escrE[par][:], ac[:, 0:1]])
                b.op("vector", lambda v, par=par: v.tensor_tensor(
                    out=esum[par][:], in0=escrE[par][:], in1=ident[:], op=OP.mult),
                    reads=[escrE[par][:], ident[:]], writes=[esum[par][:]])
                b.op("vector", lambda v, par=par: v.reduce_sum(
                    out=dcol2[par][:], in_=esum[par][:], axis=mybir.AxisListType.X),
                    reads=[esum[par][:]], writes=[dcol2[par][:]])
                c0 = 128 * (r + 1)
                ntile = 0
                while c0 < S_FULL and not SKIPREG:
                    cw = min(512, S_FULL - c0)
                    ps = ps_sc4[ntile % 4]
                    et = eth4[ntile % 4]
                    b.op("tensor", lambda t, ksl=ksl, qt=qt, pb=pb, c0=c0, cw=cw, ps=ps:
                         t.matmul(ps[:, 0:cw], ksl, qt[pb:pb + 64, c0:c0 + cw],
                                  start=True, stop=True),
                         reads=[ksl, qt[pb:pb + 64, c0:c0 + cw]], writes=[ps[:, 0:cw]])
                    b.op("scalar", lambda s, ps=ps, cw=cw, et=et: s.activation(
                         out=et[:, 0:cw], in_=ps[:, 0:cw], func=AF.Exp, scale=0.125),
                         reads=[ps[:, 0:cw]], writes=[et[:, 0:cw]])
                    b.op("vector", lambda v, et=et, cw=cw, nt=ntile, ac=ac: v.reduce_sum(
                         out=ac[:, 1 + nt:2 + nt], in_=et[:, 0:cw],
                         axis=mybir.AxisListType.X),
                         reads=[et[:, 0:cw]], writes=[ac[:, 1 + ntile:2 + ntile]])
                    c0 += cw
                    ntile += 1
                b.op("vector", lambda v, nt=ntile, par=par, ac=ac: v.reduce_sum(
                     out=dent2[par][:], in_=ac[:, 0:nt + 1], axis=mybir.AxisListType.X),
                     reads=[ac[:]], writes=[dent2[par][:]])
                b.op("vector", lambda v, par=par: v.reciprocal(
                     out=dent2[par][:], in_=dent2[par][:]),
                     reads=[dent2[par][:]], writes=[dent2[par][:]])
                b.op("vector", lambda v, r=r, par=par: v.tensor_tensor(
                     out=dat[:, r:r + 1], in0=dcol2[par][:], in1=dent2[par][:],
                     op=OP.mult),
                     reads=[dcol2[par][:], dent2[par][:]], writes=[dat[:, r:r + 1]])
            b.dma("sync", diag_out[128 * u:128 * (u + 1), :], dat[:])
        b.finalize()
    return nc



def build_L3a():
    nc = bass.Bass("TRN2", target_bir_lowering=False)
    with ExitStack() as ctx:
        b = TB(nc, ctx)
        x_in = b.dram("x_fm", [D, T], F32, kind="ExternalInput").ap()
        xn_in = b.dram("xn_fm", [D, T], F32, kind="ExternalInput").ap()
        h_in = b.dram("h_fm", [2816, T], F32, kind="ExternalInput").ap()
        mxh_in = b.dram("maxh_all", [1, NCORES], F32, kind="ExternalInput").ap()
        clip_in = b.dram("clip_down", [1, 1], F32, kind="ExternalInput").ap()
        diag_in = b.dram("diag16", [16, T], F32, kind="ExternalInput").ap()
        dvd_in = b.dram("dvdown_bf", [D, T], BF16, kind="ExternalInput").ap()
        wd_in = b.dram("wd_t", [2816, D], F32, kind="ExternalInput").ap()
        wvu_in = b.dram("wvu_fm", [D, D], F32, kind="ExternalInput").ap()
        m0_in = b.dram("mask0", [1, 128], F32, kind="ExternalInput").ap()
        m1_in = b.dram("mask1", [1, 128], F32, kind="ExternalInput").ap()
        ident_in = b.dram("ident", [128, 128], F32, kind="ExternalInput").ap()
        x1_out = b.dram("x1_fm", [D, T], F32, kind="ExternalOutput").ap()
        xn1_out = b.dram("xn1_fm", [D, T], F32, kind="ExternalOutput").ap()
        xr1_out = b.dram("xr1_bf", [D, T], BF16, kind="ExternalOutput").ap()
        mm_out = b.dram("minmax_xn1", [1, 2], F32, kind="ExternalOutput").ap()
        wdq_dram = b.dram("wdq_scr", [2816, D], BF16).ap()
        scr_dram = b.dram("scr8", [8, 128], F32).ap()

        st1 = [b.sbuf(f"st1_{i}", [128, T], F32) for i in range(2)]
        st2 = [b.sbuf(f"st2_{i}", [128, 512], F32) for i in range(2)]
        hqsb = [b.sbuf(f"hqsb{i}", [128, T], BF16) for i in range(22)]
        wvusb = [b.sbuf(f"wvusb{i}", [128, D], BF16) for i in range(8)]
        wstr = [b.sbuf("wstr0", [128, D], F32)] * 2
        wacc = b.sbuf("wacc", [128, D], F32)
        rbc = wacc
        wqt = [b.sbuf(f"wqt{i}", [128, D], BF16) for i in range(2)]
        wcol = [b.sbuf(f"wcol{i}", [128, 2816], BF16) for i in range(2)]
        vd = b.sbuf("vd", [128, 8 * 512], BF16)
        g0 = b.sbuf("g0", [128, 8 * 512], BF16)
        s0b = [b.sbuf(f"s0b{i}", [128, 512], F32) for i in range(2)]
        diagrow = b.sbuf("diagrow", [1, 1024], BF16)
        ident = b.sbuf("identsb", [128, 128], F32)
        ones_col = b.sbuf("ones_col", [128, 1], F32)
        ones_row = b.sbuf("ones_row", [1, 128], F32)
        m0 = b.sbuf("m0", [1, 128], BF16)
        m1 = b.sbuf("m1", [1, 128], BF16)
        cl = b.sbuf("cl", [1, 1], F32)
        sc1 = b.sbuf("sc1", [1, NCORES], F32)
        gmax1 = b.sbuf("gmax1", [1, 1], F32)
        invsd = b.sbuf("invsd", [128, 1], F32)
        sdcol = b.sbuf("sdcol", [128, 1], F32)
        scale_w = b.sbuf("scale_w", [128, 8], F32)
        comb_w = b.sbuf("comb_w", [128, 8], F32)
        r8 = b.sbuf("r8", [8, 128], F32)
        rrow = b.sbuf("rrow", [1, D], F32)
        sx_row = b.sbuf("sx_row", [1, T], F32)
        sq_row = b.sbuf("sq_row", [1, T], F32)
        mubc = b.sbuf("mubc", [128, T], F32)
        rstdbc = b.sbuf("rstdbc", [128, T], F32)
        sqs = st2[0]
        mncol = b.sbuf("mncol", [128, NCH], F32)
        mxcol = b.sbuf("mxcol", [128, NCH], F32)
        mm2 = b.sbuf("mm2", [1, 128], F32)
        mmo = b.sbuf("mmo", [1, 2], F32)
        nsc = scale_w
        ns_in = b.dram("norm1_scale_col", [128, 8], F32, kind="ExternalInput").ap()

        PS = [b.psum(f"PS{i}", [128, 512], F32) for i in range(8)]

        b.dma("sync", ident[:], ident_in)
        b.dma("gpsimd", m0[:], m0_in)
        b.dma("gpsimd", m1[:], m1_in)
        b.op("vector", lambda v: v.memset(ones_col[:], 1.0), writes=[ones_col[:]])
        b.op("vector", lambda v: v.memset(ones_row[:], 1.0), writes=[ones_row[:]])

        b.dma("sync", sc1[:], mxh_in)
        b.dma("sync", cl[:], clip_in)
        b.op("vector", lambda v: v.reduce_max(out=gmax1[:], in_=sc1[:],
             axis=mybir.AxisListType.X), reads=[sc1[:]], writes=[gmax1[:]])
        b.op("scalar", lambda s: s.activation(out=cl[:], in_=cl[:], func=AF.Sigmoid),
             reads=[cl[:]], writes=[cl[:]])
        b.op("vector", lambda v: v.tensor_tensor(out=cl[:], in0=cl[:], in1=gmax1[:],
             op=OP.mult), reads=[cl[:], gmax1[:]], writes=[cl[:]])
        b.op("vector", lambda v: v.tensor_scalar(out=cl[:], in0=cl[:], scalar1=1.0 / 127.0,
             scalar2=None, op0=OP.mult), reads=[cl[:]], writes=[cl[:]])
        b.op("tensor", lambda t: t.matmul(PS[0][:, 0:1], ones_row[:], cl[:],
             start=True, stop=True), reads=[ones_row[:], cl[:]], writes=[PS[0][:, 0:1]])
        b.op("vector", lambda v: v.tensor_copy(sdcol[:], PS[0][:, 0:1]),
             reads=[PS[0][:, 0:1]], writes=[sdcol[:]])
        b.op("vector", lambda v: v.reciprocal(out=invsd[:], in_=sdcol[:]),
             reads=[sdcol[:]], writes=[invsd[:]])

        for i in range(22):
            sb = st1[i % 2]
            b.dma("sync", sb[:], h_in[128 * i:128 * (i + 1), :])
            b.op("vector", lambda v, sb=sb: v.tensor_scalar(
                out=sb[:], in0=sb[:], scalar1=invsd[:], scalar2=RC,
                op0=OP.mult, op1=OP.add), reads=[sb[:], invsd[:]], writes=[sb[:]])
            b.op("vector", lambda v, sb=sb: v.tensor_scalar(
                out=sb[:], in0=sb[:], scalar1=-RC, scalar2=127.0,
                op0=OP.add, op1=OP.min), reads=[sb[:]], writes=[sb[:]])
            b.op("vector", lambda v, sb=sb, i=i: v.tensor_scalar(
                out=hqsb[i][:], in0=sb[:], scalar1=-128.0, scalar2=None,
                op0=OP.max), reads=[sb[:]], writes=[hqsb[i][:]])
        for rt in range(8):
            b.dma("gpsimd", wvusb[rt][:], wvu_in[128 * rt:128 * (rt + 1), :])

        for i in range(22):
            ws = wstr[i % 2]
            b.dma("sync", ws[:], wd_in[128 * i:128 * (i + 1), :])
            b.op("scalar", lambda s, ws=ws: s.activation(out=ws[:], in_=ws[:],
                 func=AF.Abs), reads=[ws[:]], writes=[ws[:]])
            if i == 0:
                b.op("vector", lambda v, ws=ws: v.tensor_copy(wacc[:], ws[:]),
                     reads=[ws[:]], writes=[wacc[:]])
            else:
                b.op("vector", lambda v, ws=ws: v.tensor_tensor(out=wacc[:], in0=wacc[:],
                     in1=ws[:], op=OP.max), reads=[wacc[:], ws[:]], writes=[wacc[:]])
        for blk in range(8):
            b.op("tensor", lambda t, blk=blk: t.transpose(
                PS[1][:, 0:128], wacc[:, 128 * blk:128 * (blk + 1)], ident[:]),
                reads=[wacc[:], ident[:]], writes=[PS[1][:, 0:128]])
            b.op("vector", lambda v, blk=blk: v.reduce_max(
                out=scale_w[:, blk:blk + 1], in_=PS[1][:, 0:128],
                axis=mybir.AxisListType.X),
                reads=[PS[1][:, 0:128]], writes=[scale_w[:, blk:blk + 1]])
        rcols = comb_w
        b.op("vector", lambda v: v.reciprocal(out=rcols[:], in_=scale_w[:]),
             reads=[scale_w[:]], writes=[rcols[:]])
        b.op("vector", lambda v: v.tensor_scalar(out=rcols[:], in0=rcols[:],
             scalar1=127.0, scalar2=None, op0=OP.mult), reads=[rcols[:]], writes=[rcols[:]])
        b.op("tensor", lambda t: t.transpose(PS[1][0:8, 0:128], rcols[:], ident[:]),
             reads=[rcols[:], ident[:]], writes=[PS[1][0:8, 0:128]])
        b.op("vector", lambda v: v.tensor_copy(r8[:], PS[1][0:8, 0:128]),
             reads=[PS[1][0:8, 0:128]], writes=[r8[:]])
        b.dma("sync", scr_dram, r8[:])
        b.dma("sync", rrow[:], scr_dram.rearrange("a b -> (a b)").unsqueeze(0))
        h_bcast_row(b, PS[2], rbc, rrow, ones_row, D)
        for i in range(22):
            ws = wstr[i % 2]
            wq_ = wqt[i % 2]
            b.dma("sync", ws[:], wd_in[128 * i:128 * (i + 1), :])
            b.op("vector", lambda v, ws=ws: v.tensor_tensor(out=ws[:], in0=ws[:],
                 in1=rbc[:], op=OP.mult), reads=[ws[:], rbc[:]], writes=[ws[:]])
            b.op("vector", lambda v, ws=ws: v.tensor_scalar(out=ws[:], in0=ws[:],
                 scalar1=RC, scalar2=-RC, op0=OP.add, op1=OP.add),
                 reads=[ws[:]], writes=[ws[:]])
            b.op("vector", lambda v, ws=ws, wq_=wq_: v.tensor_scalar(out=wq_[:], in0=ws[:],
                 scalar1=127.0, scalar2=-127.0, op0=OP.min, op1=OP.max),
                 reads=[ws[:]], writes=[wq_[:]])
            b.dma("sync", wdq_dram[128 * i:128 * (i + 1), :], wq_[:])
        b.op("vector", lambda v: v.tensor_scalar(
            out=comb_w[:], in0=scale_w[:], scalar1=sdcol[:], scalar2=1.0 / 127.0,
            op0=OP.mult, op1=OP.mult), reads=[scale_w[:], sdcol[:]], writes=[comb_w[:]])

        for cs in range(4):
            tsl = slice(512 * cs, 512 * (cs + 1))
            for ot in range(8):
                wc = wcol[ot % 2]
                src = wdq_dram[:, 128 * ot:128 * (ot + 1)].rearrange(
                    "(c p) j -> p c j", p=128)
                b.dma("sync", wc[:].rearrange("p (c j) -> p c j", c=22), src)
                ps = PS[ot % 2]
                for i in range(22):
                    b.op("tensor", lambda t, wc=wc, i=i, ps=ps, tsl=tsl: t.matmul(
                        ps[:, :], wc[:, 128 * i:128 * (i + 1)], hqsb[i][:, tsl],
                        start=(i == 0), stop=(i == 21)),
                        reads=[wc[:, 128 * i:128 * (i + 1)], hqsb[i][:, tsl]],
                        writes=[ps[:, :]])
                b.op("scalar", lambda s, ps=ps, ot=ot: s.activation(
                    out=g0[:, 512 * ot:512 * (ot + 1)], in_=ps[:, :], func=AF.Silu,
                    scale=comb_w[:, ot:ot + 1]),
                    reads=[ps[:, :], comb_w[:, ot:ot + 1]],
                    writes=[g0[:, 512 * ot:512 * (ot + 1)]])
            for rt in range(8):
                b.dma("gpsimd", diagrow[0:1, 0:512], diag_in[2 * rt:2 * rt + 1, tsl])
                b.dma("gpsimd", diagrow[0:1, 512:1024], diag_in[2 * rt + 1:2 * rt + 2, tsl])
                ps = PS[2]
                b.op("tensor", lambda t, ps=ps: t.matmul(ps[:, :], m0[:], diagrow[0:1, 0:512],
                     start=True, stop=False), reads=[m0[:], diagrow[:]], writes=[ps[:, :]])
                b.op("tensor", lambda t, ps=ps: t.matmul(ps[:, :], m1[:], diagrow[0:1, 512:1024],
                     start=False, stop=True), reads=[m1[:], diagrow[:]], writes=[ps[:, :]])
                dbuf = st1[0]
                b.dma("sync", dbuf[:, 0:512].bitcast(BF16) if False else vd[:, 512 * rt:512 * (rt + 1)],
                      dvd_in[128 * rt:128 * (rt + 1), tsl])
                b.op("vector", lambda v, rt=rt, ps=ps: v.tensor_tensor(
                    out=vd[:, 512 * rt:512 * (rt + 1)], in0=vd[:, 512 * rt:512 * (rt + 1)],
                    in1=ps[:, :], op=OP.mult),
                    reads=[vd[:, 512 * rt:512 * (rt + 1)], ps[:, :]],
                    writes=[vd[:, 512 * rt:512 * (rt + 1)]])
            for et in range(8):
                ps = PS[3 + et % 2]
                for rt in range(8):
                    wv = wvusb[rt]
                    b.op("tensor", lambda t, wv=wv, rt=rt, et=et, ps=ps: t.matmul(
                        ps[:, :], wv[:, 128 * et:128 * (et + 1)],
                        vd[:, 512 * rt:512 * (rt + 1)],
                        start=(rt == 0), stop=(rt == 7)),
                        reads=[wv[:, 128 * et:128 * (et + 1)],
                               vd[:, 512 * rt:512 * (rt + 1)]], writes=[ps[:, :]])
                xnb_ = st2[et % 2]
                s0c = s0b[et % 2]
                b.dma("sync", xnb_[:, 0:512], xn_in[128 * et:128 * (et + 1), tsl])
                b.op("vector", lambda v, ps=ps, s0c=s0c, xnb_=xnb_: v.scalar_tensor_tensor(
                    out=s0c[:], in0=ps[:, :], scalar=1.0 / 16.0,
                    in1=xnb_[:, 0:512], op0=OP.mult, op1=OP.add),
                    reads=[ps[:, :], xnb_[:, 0:512]], writes=[s0c[:]])
                b.op("vector", lambda v, et=et, s0c=s0c: v.tensor_tensor(
                    out=s0c[:], in0=s0c[:], in1=g0[:, 512 * et:512 * (et + 1)],
                    op=OP.mult),
                    reads=[s0c[:], g0[:, 512 * et:512 * (et + 1)]], writes=[s0c[:]])
                xb = st1[et % 2]
                b.dma("sync", xb[:, 0:512], x_in[128 * et:128 * (et + 1), tsl])
                b.op("vector", lambda v, s0c=s0c, xb=xb: v.tensor_tensor(
                    out=s0c[:], in0=s0c[:], in1=xb[:, 0:512], op=OP.add),
                    reads=[s0c[:], xb[:, 0:512]], writes=[s0c[:]])
                b.dma("sync", x1_out[128 * et:128 * (et + 1), tsl], s0c[:])

        for cs in range(4):
            sl = slice(512 * cs, 512 * (cs + 1))
            for i in range(NCH):
                xb = st1[i % 2]
                b.dma("sync", xb[:, 0:512], x1_out[128 * i:128 * (i + 1), sl])
                b.op("tensor", lambda t, i=i, xb=xb: t.matmul(
                    PS[6][0:1, :], ones_col[:], xb[:, 0:512],
                    start=(i == 0), stop=(i == NCH - 1)),
                    reads=[xb[:, 0:512], ones_col[:]], writes=[PS[6][0:1, :]])
                b.op("scalar", lambda s, xb=xb: s.activation(
                    out=sqs[:], in_=xb[:, 0:512], func=AF.Square),
                    reads=[xb[:, 0:512]], writes=[sqs[:]])
                b.op("tensor", lambda t, i=i: t.matmul(
                    PS[7][0:1, :], ones_col[:], sqs[:],
                    start=(i == 0), stop=(i == NCH - 1)),
                    reads=[sqs[:], ones_col[:]], writes=[PS[7][0:1, :]])
            b.op("vector", lambda v, sl=sl: v.tensor_copy(sx_row[0:1, sl], PS[6][0:1, :]),
                 reads=[PS[6][0:1, :]], writes=[sx_row[0:1, sl]])
            b.op("vector", lambda v, sl=sl: v.tensor_copy(sq_row[0:1, sl], PS[7][0:1, :]),
                 reads=[PS[7][0:1, :]], writes=[sq_row[0:1, sl]])
        b.op("vector", lambda v: v.tensor_scalar(out=sx_row[:], in0=sx_row[:],
             scalar1=1.0 / D, scalar2=None, op0=OP.mult),
             reads=[sx_row[:]], writes=[sx_row[:]])
        h_bcast_row(b, PS[0], mubc, sx_row, ones_row, T)
        b.op("vector", lambda v: v.tensor_tensor(out=sx_row[:], in0=sx_row[:],
             in1=sx_row[:], op=OP.mult), reads=[sx_row[:]], writes=[sx_row[:]])
        b.op("vector", lambda v: v.tensor_scalar(out=sq_row[:], in0=sq_row[:],
             scalar1=1.0 / D, scalar2=None, op0=OP.mult),
             reads=[sq_row[:]], writes=[sq_row[:]])
        b.op("vector", lambda v: v.tensor_tensor(out=sq_row[:], in0=sq_row[:],
             in1=sx_row[:], op=OP.subtract), reads=[sq_row[:], sx_row[:]], writes=[sq_row[:]])
        b.op("vector", lambda v: v.tensor_scalar(out=sq_row[:], in0=sq_row[:],
             scalar1=1e-5, scalar2=None, op0=OP.add), reads=[sq_row[:]], writes=[sq_row[:]])
        b.op("scalar", lambda s: s.activation(out=sq_row[:], in_=sq_row[:],
             func=AF.Sqrt), reads=[sq_row[:]], writes=[sq_row[:]])
        b.op("vector", lambda v: v.reciprocal(out=sq_row[:], in_=sq_row[:]),
             reads=[sq_row[:]], writes=[sq_row[:]])
        h_bcast_row(b, PS[1], rstdbc, sq_row, ones_row, T)

        for i in range(NCH):
            xb = st1[i % 2]
            b.dma("sync", xb[:], x1_out[128 * i:128 * (i + 1), :])
            b.op("vector", lambda v, xb=xb: v.tensor_tensor(out=xb[:], in0=xb[:],
                 in1=mubc[:], op=OP.subtract), reads=[xb[:], mubc[:]], writes=[xb[:]])
            b.op("vector", lambda v, xb=xb: v.tensor_tensor(out=xb[:], in0=xb[:],
                 in1=rstdbc[:], op=OP.mult), reads=[xb[:], rstdbc[:]], writes=[xb[:]])
            b.op("vector", lambda v, i=i, xb=xb: v.reduce_max(out=mxcol[:, i:i + 1],
                 in_=xb[:], axis=mybir.AxisListType.X),
                 reads=[xb[:]], writes=[mxcol[:, i:i + 1]])
            b.op("vector", lambda v, i=i, xb=xb: v.tensor_reduce(out=mncol[:, i:i + 1],
                 in_=xb[:], axis=mybir.AxisListType.X, op=OP.min),
                 reads=[xb[:]], writes=[mncol[:, i:i + 1]])
            b.dma("sync", xn1_out[128 * i:128 * (i + 1), :], xb[:])
            for cs in range(4):
                sl = slice(512 * cs, 512 * (cs + 1))
                b.op("scalar", lambda s, xb=xb, sl=sl: s.activation(
                    out=sqs[:], in_=xb[:, sl], func=AF.Square),
                    reads=[xb[:, sl]], writes=[sqs[:]])
                b.op("tensor", lambda t, i=i, cs=cs: t.matmul(
                    PS[2 + cs][0:1, :], ones_col[:], sqs[:],
                    start=(i == 0), stop=(i == NCH - 1)),
                    reads=[sqs[:], ones_col[:]], writes=[PS[2 + cs][0:1, :]])
        for cs in range(4):
            sl = slice(512 * cs, 512 * (cs + 1))
            b.op("vector", lambda v, sl=sl, cs=cs: v.tensor_copy(
                sx_row[0:1, sl], PS[2 + cs][0:1, :]),
                reads=[PS[2 + cs][0:1, :]], writes=[sx_row[0:1, sl]])
        b.op("vector", lambda v: v.tensor_scalar(out=sx_row[:], in0=sx_row[:],
             scalar1=1.0 / D, scalar2=None, op0=OP.mult), reads=[sx_row[:]], writes=[sx_row[:]])
        b.op("scalar", lambda s: s.activation(out=sx_row[:], in_=sx_row[:],
             func=AF.Sqrt), reads=[sx_row[:]], writes=[sx_row[:]])
        b.op("vector", lambda v: v.reciprocal(out=sx_row[:], in_=sx_row[:]),
             reads=[sx_row[:]], writes=[sx_row[:]])
        h_bcast_row(b, PS[0], rstdbc, sx_row, ones_row, T)
        b.op("vector", lambda v: v.reduce_max(out=mxcol[:, 0:1], in_=mxcol[:],
             axis=mybir.AxisListType.X), reads=[mxcol[:]], writes=[mxcol[:, 0:1]])
        b.op("vector", lambda v: v.tensor_reduce(out=mncol[:, 0:1], in_=mncol[:],
             axis=mybir.AxisListType.X, op=OP.min), reads=[mncol[:]], writes=[mncol[:, 0:1]])
        b.op("tensor", lambda t: t.transpose(PS[1][0:1, 0:128], mxcol[:, 0:1], ident[:]),
             reads=[mxcol[:], ident[:]], writes=[PS[1][0:1, 0:128]])
        b.op("vector", lambda v: v.tensor_copy(mm2[:], PS[1][0:1, 0:128]),
             reads=[PS[1][0:1, 0:128]], writes=[mm2[:]])
        b.op("vector", lambda v: v.reduce_max(out=mmo[0:1, 1:2], in_=mm2[:],
             axis=mybir.AxisListType.X), reads=[mm2[:]], writes=[mmo[:]])
        b.op("tensor", lambda t: t.transpose(PS[1][0:1, 0:128], mncol[:, 0:1], ident[:]),
             reads=[mncol[:], ident[:]], writes=[PS[1][0:1, 0:128]])
        b.op("vector", lambda v: v.tensor_copy(mm2[:], PS[1][0:1, 0:128]),
             reads=[PS[1][0:1, 0:128]], writes=[mm2[:]])
        b.op("vector", lambda v: v.tensor_reduce(out=mmo[0:1, 0:1], in_=mm2[:],
             axis=mybir.AxisListType.X, op=OP.min), reads=[mm2[:]], writes=[mmo[:]])
        b.dma("sync", mm_out, mmo[:])

        b.dma("sync", nsc[:], ns_in)
        qtile = hqsb[0]
        for i in range(NCH):
            xb = st1[i % 2]
            b.dma("sync", xb[:], xn1_out[128 * i:128 * (i + 1), :])
            b.op("vector", lambda v, xb=xb: v.tensor_tensor(out=xb[:], in0=xb[:],
                 in1=rstdbc[:], op=OP.mult), reads=[xb[:], rstdbc[:]], writes=[xb[:]])
            b.op("vector", lambda v, i=i, xb=xb: v.tensor_scalar(out=qtile[:], in0=xb[:],
                 scalar1=nsc[:, i:i + 1], scalar2=None, op0=OP.mult),
                 reads=[xb[:], nsc[:, i:i + 1]], writes=[qtile[:]])
            b.dma("sync", xr1_out[128 * i:128 * (i + 1), :], qtile[:])
        b.finalize()
    return nc



def build_L3b():
    nc = bass.Bass("TRN2", target_bir_lowering=False)
    with ExitStack() as ctx:
        b = TB(nc, ctx)
        xr_in = b.dram("xr1_bf", [D, T], BF16, kind="ExternalInput").ap()
        w_in = b.dram("wqkvg", [D, 4 * D], F32, kind="ExternalInput").ap()
        bq_in = b.dram("bq_col", [128, 8], F32, kind="ExternalInput").ap()
        bkvg_in = b.dram("bkvg_rows", [1, 3 * D], F32, kind="ExternalInput").ap()
        cbig_in = b.dram("CBIG", [128, T], F32, kind="ExternalInput").ap()
        sbig_in = b.dram("SBIG", [128, T], F32, kind="ExternalInput").ap()
        cosk_in = b.dram("cosK", [T, 32], F32, kind="ExternalInput").ap()
        sink_in = b.dram("sinK", [T, 32], F32, kind="ExternalInput").ap()
        pswap_in = b.dram("PSWAP", [128, 128], F32, kind="ExternalInput").ap()
        q_out = b.dram("qro_bf", [D, T], BF16, kind="ExternalOutput").ap()
        k_out = b.dram("k_tm", [T, D], BF16, kind="ExternalOutput").ap()
        v_out = b.dram("v_tm", [T, D], BF16, kind="ExternalOutput").ap()
        g_out = b.dram("g_tm", [T, D], BF16, kind="ExternalOutput").ap()
        ds_out = b.dram("ds_tot", [64, D], F32, kind="ExternalOutput").ap()
        dbg_kt = b.dram("dbg_kt", [128, D], F32, kind="ExternalOutput").ap()
        dbg_kr = b.dram("dbg_kr", [128, D], F32, kind="ExternalOutput").ap()
        z_out = b.dram("z_tot", [64, 16], F32, kind="ExternalOutput").ap()

        xr = [b.sbuf(f"xr{i}", [128, T], BF16) for i in range(NCH)]
        wq_sb = [b.sbuf(f"wqsb{i}", [128, D], BF16) for i in range(NCH)]
        wk_sb = [b.sbuf(f"wksb{i}", [128, D], BF16) for i in range(NCH)]
        wv_sb = [b.sbuf(f"wvsb{i}", [128, D], BF16) for i in range(NCH)]
        wg_sb = [b.sbuf(f"wgsb{i}", [128, D], BF16) for i in range(NCH)]
        bqc = b.sbuf("bqc", [128, 8], F32)
        bkbc = b.sbuf("bkbc", [128, D], F32)
        bvbc = b.sbuf("bvbc", [128, D], F32)
        bgbc = b.sbuf("bgbc", [128, D], F32)
        brow = b.sbuf("brow", [1, 3 * D], F32)
        cbig = b.sbuf("cbig", [128, T], F32)
        sbig = b.sbuf("sbig", [128, T], F32)
        cost = b.sbuf("cost", [128, 512], F32)
        sint = b.sbuf("sint", [128, 512], F32)
        pswap = b.sbuf("pswap", [128, 128], BF16)
        pswap32 = b.sbuf("pswap32", [128, 128], F32)
        ones_row = b.sbuf("ones_row", [1, 128], F32)
        ones_col = b.sbuf("ones_col", [128, 1], F32)
        ones_col_bf = b.sbuf("ones_col_bf", [128, 1], BF16)
        qt = b.sbuf("qt", [128, T], F32)
        qr = b.sbuf("qr", [128, T], F32)
        qtmp = b.sbuf("qtmp", [128, T], F32)
        qbf = b.sbuf("qbf", [128, T], BF16)
        kt = b.sbuf("kt", [128, D], F32)
        vt = b.sbuf("vt", [128, D], F32)
        gt = b.sbuf("gt", [128, D], F32)
        krot = b.sbuf("krot", [128, D], F32)
        ktmp = b.sbuf("ktmp", [128, D], F32)
        ktile = b.sbuf("ktile", [128, D], BF16)
        vtile = b.sbuf("vtile", [128, D], BF16)
        gtile = b.sbuf("gtile", [128, D], BF16)
        dssb = b.sbuf("dssb", [64, D], F32)
        zsb = b.sbuf("zsb", [64, 16], F32)
        PS = [b.psum(f"PS{i}", [128, 512], F32) for i in range(8)]

        for i in range(NCH):
            b.dma("sync", xr[i][:], xr_in[128 * i:128 * (i + 1), :])
            b.dma("gpsimd", wq_sb[i][:], w_in[128 * i:128 * (i + 1), 0:D])
            b.dma("gpsimd", wk_sb[i][:], w_in[128 * i:128 * (i + 1), D:2 * D])
            b.dma("gpsimd", wv_sb[i][:], w_in[128 * i:128 * (i + 1), 2 * D:3 * D])
            b.dma("gpsimd", wg_sb[i][:], w_in[128 * i:128 * (i + 1), 3 * D:4 * D])
        b.dma("sync", bqc[:], bq_in)
        b.dma("sync", brow[:], bkvg_in)
        b.dma("sync", cbig[:], cbig_in)
        b.dma("sync", sbig[:], sbig_in)
        b.dma("sync", cost[:].rearrange("p (t c) -> p t c", c=32),
              cosk_in.rearrange("(t p) c -> p t c", p=128))
        b.dma("sync", sint[:].rearrange("p (t c) -> p t c", c=32),
              sink_in.rearrange("(t p) c -> p t c", p=128))
        b.dma("gpsimd", pswap[:], pswap_in)
        b.dma("sync", pswap32[:], pswap_in)
        b.op("vector", lambda v: v.memset(ones_row[:], 1.0), writes=[ones_row[:]])
        b.op("vector", lambda v: v.memset(ones_col[:], 1.0), writes=[ones_col[:]])
        b.op("vector", lambda v: v.memset(ones_col_bf[:], 1.0), writes=[ones_col_bf[:]])
        h_bcast_row(b, PS[0], bkbc, brow[0:1, 0:D], ones_row, D)
        h_bcast_row(b, PS[0], bvbc, brow[0:1, D:2 * D], ones_row, D)
        h_bcast_row(b, PS[0], bgbc, brow[0:1, 2 * D:3 * D], ones_row, D)

        for ot in range(8):
            for cs in range(4):
                ps = PS[cs % 2]
                sl = slice(512 * cs, 512 * (cs + 1))
                for i in range(NCH):
                    b.op("tensor", lambda t, i=i, ot=ot, sl=sl, ps=ps: t.matmul(
                        ps[:, :], wq_sb[i][:, 128 * ot:128 * (ot + 1)], xr[i][:, sl],
                        start=(i == 0), stop=(i == NCH - 1)),
                        reads=[wq_sb[i][:], xr[i][:, sl]], writes=[ps[:, :]])
                b.op("vector", lambda v, sl=sl, ps=ps, ot=ot: v.tensor_scalar(
                    out=qt[:, sl], in0=ps[:, :], scalar1=bqc[:, ot:ot + 1], scalar2=None,
                    op0=OP.add), reads=[ps[:, :], bqc[:, ot:ot + 1]], writes=[qt[:, sl]])
            for cs in range(4):
                ps = PS[2 + cs % 2]
                sl = slice(512 * cs, 512 * (cs + 1))
                b.op("tensor", lambda t, sl=sl, ps=ps: t.matmul(
                    ps[:, :], pswap32[:], qt[:, sl], start=True, stop=True),
                    reads=[pswap32[:], qt[:, sl]], writes=[ps[:, :]])
                b.op("vector", lambda v, sl=sl: v.tensor_tensor(
                    out=qr[:, sl], in0=qt[:, sl], in1=cbig[:, sl], op=OP.mult),
                    reads=[qt[:, sl], cbig[:, sl]], writes=[qr[:, sl]])
                b.op("vector", lambda v, sl=sl, ps=ps: v.tensor_tensor(
                    out=qtmp[:, sl], in0=ps[:, :], in1=sbig[:, sl], op=OP.mult),
                    reads=[ps[:, :], sbig[:, sl]], writes=[qtmp[:, sl]])
                b.op("vector", lambda v, sl=sl: v.tensor_tensor(
                    out=qr[:, sl], in0=qr[:, sl], in1=qtmp[:, sl], op=OP.add),
                    reads=[qr[:, sl], qtmp[:, sl]], writes=[qr[:, sl]])
            b.op("vector", lambda v: v.tensor_scalar(out=qtmp[:], in0=qr[:],
                 scalar1=0.0, scalar2=None, op0=OP.min), reads=[qr[:]], writes=[qtmp[:]])
            b.op("scalar", lambda s: s.activation(out=qtmp[:], in_=qtmp[:],
                 func=AF.Exp), reads=[qtmp[:]], writes=[qtmp[:]])
            b.op("vector", lambda v: v.tensor_scalar(out=qr[:], in0=qr[:],
                 scalar1=0.0, scalar2=None, op0=OP.max), reads=[qr[:]], writes=[qr[:]])
            b.op("vector", lambda v: v.tensor_tensor(out=qr[:], in0=qr[:],
                 in1=qtmp[:], op=OP.add), reads=[qr[:], qtmp[:]], writes=[qr[:]])
            b.op("vector", lambda v: v.tensor_copy(qbf[:], qr[:]),
                 reads=[qr[:]], writes=[qbf[:]])
            b.dma("sync", q_out[128 * ot:128 * (ot + 1), :], qbf[:])

        for tt in range(16):
            tsl = slice(128 * tt, 128 * (tt + 1))
            for w_i, (wsb, dst) in enumerate([(wk_sb, kt), (wv_sb, vt), (wg_sb, gt)]):
                for half in range(2):
                    ps = PS[2 * w_i + half]
                    osl = slice(512 * half, 512 * (half + 1))
                    for i in range(NCH):
                        b.op("tensor", lambda t, i=i, ps=ps, osl=osl, tsl=tsl, wsb=wsb:
                             t.matmul(ps[:, :], xr[i][:, tsl], wsb[i][:, osl],
                                      start=(i == 0), stop=(i == NCH - 1)),
                             reads=[xr[i][:, tsl], wsb[i][:, osl]], writes=[ps[:, :]])
                    bbc = [bkbc, bvbc, bgbc][w_i]
                    b.op("vector", lambda v, ps=ps, osl=osl, dst=dst, bbc=bbc:
                         v.tensor_tensor(out=dst[:, osl], in0=ps[:, :],
                                         in1=bbc[:, osl], op=OP.add),
                         reads=[ps[:, :], bbc[:, osl]], writes=[dst[:, osl]])
            if tt == 15:
                b.dma("sync", dbg_kt, kt[:])
            c3 = cost[:, 32 * tt:32 * (tt + 1)].unsqueeze(1).broadcast_to([128, 16, 32])
            s3 = sint[:, 32 * tt:32 * (tt + 1)].unsqueeze(1).broadcast_to([128, 16, 32])
            ktv = kt[:].rearrange("p (h a c) -> p h a c", h=16, a=2)
            krv = krot[:].rearrange("p (h a c) -> p h a c", h=16, a=2)
            ktm = ktmp[:].rearrange("p (h a c) -> p h a c", h=16, a=2)
            fh_k, sh_k = ktv[:, :, 0:1, :], ktv[:, :, 1:2, :]
            fh_r, sh_r = krv[:, :, 0:1, :], krv[:, :, 1:2, :]
            fh_t, sh_t = ktm[:, :, 0:1, :], ktm[:, :, 1:2, :]
            c4 = cost[:, 32 * tt:32 * (tt + 1)].unsqueeze(1).unsqueeze(2).broadcast_to([128, 16, 1, 32])
            s4 = sint[:, 32 * tt:32 * (tt + 1)].unsqueeze(1).unsqueeze(2).broadcast_to([128, 16, 1, 32])
            b.op("vector", lambda v, c4=c4: v.tensor_tensor(out=fh_r, in0=fh_k, in1=c4,
                 op=OP.mult), reads=[kt[:], cost[:]], writes=[krot[:]])
            b.op("vector", lambda v, s4=s4: v.tensor_tensor(out=fh_t, in0=sh_k, in1=s4,
                 op=OP.mult), reads=[kt[:], sint[:]], writes=[ktmp[:]])
            b.op("vector", lambda v: v.tensor_tensor(out=fh_r, in0=fh_r, in1=fh_t,
                 op=OP.subtract), reads=[krot[:], ktmp[:]], writes=[krot[:]])
            b.op("vector", lambda v, c4=c4: v.tensor_tensor(out=sh_r, in0=sh_k, in1=c4,
                 op=OP.mult), reads=[kt[:], cost[:]], writes=[krot[:]])
            b.op("vector", lambda v, s4=s4: v.tensor_tensor(out=sh_t, in0=fh_k, in1=s4,
                 op=OP.mult), reads=[kt[:], sint[:]], writes=[ktmp[:]])
            b.op("vector", lambda v: v.tensor_tensor(out=sh_r, in0=sh_r, in1=sh_t,
                 op=OP.add), reads=[krot[:], ktmp[:]], writes=[krot[:]])
            b.op("vector", lambda v: v.tensor_scalar(out=ktmp[:], in0=krot[:],
                 scalar1=0.0, scalar2=None, op0=OP.min), reads=[krot[:]], writes=[ktmp[:]])
            b.op("scalar", lambda s: s.activation(out=ktmp[:], in_=ktmp[:],
                 func=AF.Exp), reads=[ktmp[:]], writes=[ktmp[:]])
            b.op("vector", lambda v: v.tensor_scalar(out=krot[:], in0=krot[:],
                 scalar1=0.0, scalar2=None, op0=OP.max), reads=[krot[:]], writes=[krot[:]])
            b.op("vector", lambda v: v.tensor_tensor(out=krot[:], in0=krot[:],
                 in1=ktmp[:], op=OP.add), reads=[krot[:], ktmp[:]], writes=[krot[:]])
            b.op("vector", lambda v: v.tensor_copy(ktile[:], krot[:]),
                 reads=[krot[:]], writes=[ktile[:]])
            if tt == 15:
                b.dma("sync", dbg_kr, krot[:])
            b.dma("sync", k_out[tsl, :], ktile[:])
            b.op("vector", lambda v: v.tensor_copy(vtile[:], vt[:]),
                 reads=[vt[:]], writes=[vtile[:]])
            b.dma("sync", v_out[tsl, :], vtile[:])
            b.op("scalar", lambda s: s.activation(out=gtile[:], in_=gt[:],
                 func=AF.Sigmoid), reads=[gt[:]], writes=[gtile[:]])
            b.dma("sync", g_out[tsl, :], gtile[:])
            for h in range(16):
                psd = PS[6] if h < 8 else PS[7]
                col = 64 * (h % 8)
                b.op("tensor", lambda t, h=h, psd=psd, col=col: t.matmul(
                    psd[0:64, col:col + 64], ktile[:, 64 * h:64 * (h + 1)],
                    vtile[:, 64 * h:64 * (h + 1)], start=True, stop=True),
                    reads=[ktile[:, 64 * h:64 * (h + 1)], vtile[:, 64 * h:64 * (h + 1)]],
                    writes=[psd[0:64, col:col + 64]])
                psz = PS[5]
                b.op("tensor", lambda t, h=h, psz=psz: t.matmul(
                    psz[0:64, h:h + 1], ktile[:, 64 * h:64 * (h + 1)],
                    ones_col_bf[:], start=True, stop=True),
                    reads=[ktile[:, 64 * h:64 * (h + 1)], ones_col_bf[:]],
                    writes=[psz[0:64, h:h + 1]])
            if tt == 0:
                b.op("vector", lambda v: v.tensor_copy(dssb[:, 0:512], PS[6][0:64, :]),
                     reads=[PS[6][0:64, :]], writes=[dssb[:, 0:512]])
                b.op("vector", lambda v: v.tensor_copy(dssb[:, 512:1024], PS[7][0:64, :]),
                     reads=[PS[7][0:64, :]], writes=[dssb[:, 512:1024]])
                b.op("vector", lambda v: v.tensor_copy(zsb[:], PS[5][0:64, 0:16]),
                     reads=[PS[5][0:64, 0:16]], writes=[zsb[:]])
            else:
                b.op("vector", lambda v: v.tensor_tensor(out=dssb[:, 0:512],
                     in0=dssb[:, 0:512], in1=PS[6][0:64, :], op=OP.add),
                     reads=[dssb[:, 0:512], PS[6][0:64, :]], writes=[dssb[:, 0:512]])
                b.op("vector", lambda v: v.tensor_tensor(out=dssb[:, 512:1024],
                     in0=dssb[:, 512:1024], in1=PS[7][0:64, :], op=OP.add),
                     reads=[dssb[:, 512:1024], PS[7][0:64, :]], writes=[dssb[:, 512:1024]])
                b.op("vector", lambda v: v.tensor_tensor(out=zsb[:],
                     in0=zsb[:], in1=PS[5][0:64, 0:16], op=OP.add),
                     reads=[zsb[:], PS[5][0:64, 0:16]], writes=[zsb[:]])
        b.dma("sync", ds_out, dssb[:])
        b.dma("sync", z_out, zsb[:])
        b.finalize()
    return nc



def build_L4a():
    nc = bass.Bass("TRN2", target_bir_lowering=False)
    with ExitStack() as ctx:
        b = TB(nc, ctx)
        q_in = b.dram("qro_bf", [D, T], BF16, kind="ExternalInput").ap()
        k_in = b.dram("k_tm", [T, D], BF16, kind="ExternalInput").ap()
        v_in = b.dram("v_tm", [T, D], BF16, kind="ExternalInput").ap()
        g_in = b.dram("g_tm", [T, D], BF16, kind="ExternalInput").ap()
        dsp_in = b.dram("ds_prev", [3, 64 * D], F32, kind="ExternalInput").ap()
        zp_in = b.dram("z_prev", [3, 64 * 16], F32, kind="ExternalInput").ap()
        xn1_in = b.dram("xn1_fm", [D, T], F32, kind="ExternalInput").ap()
        wo_in = b.dram("wo", [D, D], F32, kind="ExternalInput").ap()
        bo_in = b.dram("bo_col", [128, 8], F32, kind="ExternalInput").ap()
        ns2_in = b.dram("norm2_scale_col", [128, 8], F32, kind="ExternalInput").ap()
        ident_in = b.dram("ident", [128, 128], F32, kind="ExternalInput").ap()
        identb_in = b.dram("identb", [128, 128], BF16, kind="ExternalInput").ap()
        x2_out = b.dram("x2_fm", [D, T], F32, kind="ExternalOutput").ap()
        xr2_out = b.dram("xr2_bf", [D, T], BF16, kind="ExternalOutput").ap()
        at_dram = b.dram("attn_tm", [T, D], BF16).ap()
        dbg_S = b.dram("dbg_S", [64, D], F32, kind="ExternalOutput").ap()
        dbg_den = b.dram("dbg_den", [64, 16], F32, kind="ExternalOutput").ap()
        dbg_q = b.dram("dbg_q", [64, 2048], F32, kind="ExternalOutput").ap()
        dbg_att = b.dram("dbg_att", [64, D], F32, kind="ExternalOutput").ap()
        dbg_g = b.dram("dbg_g", [64, 2048], F32, kind="ExternalOutput").ap()
        dbg_num = b.dram("dbg_num", [64, D], F32, kind="ExternalOutput").ap()

        qsc = [b.sbuf(f"qsc{i}", [64, 2048], BF16) for i in range(2)]
        ksc = [b.sbuf(f"ksc{i}", [64, 2048], BF16) for i in range(2)]
        vsc = [b.sbuf(f"vsc{i}", [64, 2048], BF16) for i in range(2)]
        gsc = [b.sbuf(f"gsc{i}", [64, 2048], BF16) for i in range(2)]
        S_sb = b.sbuf("S_sb", [64, D], BF16)
        S32 = b.sbuf("S32", [64, D], F32)
        Z_sb = b.sbuf("Z_sb", [64, 16], BF16)
        Z32 = b.sbuf("Z32", [64, 16], F32)
        dent = b.sbuf("dent", [64, 16], F32)
        att = b.sbuf("att", [64, D], F32)
        attb = b.sbuf("attb", [64, D], BF16)
        atile = b.sbuf("atile", [128, D], BF16)
        afm = b.sbuf("afm", [128, 4096], BF16)
        wosb = [b.sbuf(f"wosb{i}", [128, D], BF16) for i in range(NCH)]
        boc = b.sbuf("boc", [128, 8], F32)
        ns2 = b.sbuf("ns2", [128, 8], F32)
        ident = b.sbuf("identsb", [128, 128], F32)
        identb = b.sbuf("identbb", [128, 128], BF16)
        ones_col = b.sbuf("ones_col", [128, 1], F32)
        x2t = b.sbuf("x2t", [128, 512], F32)
        sqs = b.sbuf("sqs", [128, 512], F32)
        sx_row = b.sbuf("sx_row", [1, T], F32)
        rstdbc = b.sbuf("rstdbc", [128, T], F32)
        st1 = [b.sbuf(f"st1_{i}", [128, T], F32) for i in range(2)]
        xrtile = b.sbuf("xrtile", [128, T], BF16)
        PS = [b.psum(f"PS{i}", [128, 512], F32) for i in range(7)]
        PSB = b.psum("PSB", [128, 256], BF16)
        onesb = b.sbuf("onesb", [64, 1], BF16)

        b.dma("sync", ident[:], ident_in)
        b.dma("sync", identb[:], identb_in)
        b.dma("sync", boc[:], bo_in)
        b.dma("sync", ns2[:], ns2_in)
        b.op("vector", lambda v: v.memset(ones_col[:], 1.0), writes=[ones_col[:]])
        b.op("vector", lambda v: v.memset(onesb[:], 1.0), writes=[onesb[:]])
        for i in range(NCH):
            b.dma("gpsimd", wosb[i][:], wo_in[128 * i:128 * (i + 1), :])

        sload = b.sbuf("sload", [64, D], F32)
        zload = b.sbuf("zload", [64, 16], F32)
        for pq in range(3):
            b.dma("sync", sload[:],
                  dsp_in[pq:pq + 1, :].rearrange("a (p f) -> p (a f)", p=64))
            b.dma("sync", zload[:],
                  zp_in[pq:pq + 1, :].rearrange("a (p f) -> p (a f)", p=64))
            if pq == 0:
                b.op("vector", lambda v: v.tensor_copy(S32[:], sload[:]),
                     reads=[sload[:]], writes=[S32[:]])
                b.op("vector", lambda v: v.tensor_copy(Z32[:], zload[:]),
                     reads=[zload[:]], writes=[Z32[:]])
            else:
                b.op("vector", lambda v: v.tensor_tensor(out=S32[:], in0=S32[:],
                     in1=sload[:], op=OP.add), reads=[S32[:], sload[:]], writes=[S32[:]])
                b.op("vector", lambda v: v.tensor_tensor(out=Z32[:], in0=Z32[:],
                     in1=zload[:], op=OP.add), reads=[Z32[:], zload[:]], writes=[Z32[:]])

        for tt in range(16):
            qb, kb, vb, gb = qsc[tt % 2], ksc[tt % 2], vsc[tt % 2], gsc[tt % 2]
            b.dma("sync", qb[:].rearrange("p (h t) -> p h t", h=16),
                  q_in[:, 128 * tt:128 * (tt + 1)].rearrange("(h d) t -> d h t", d=64))
            for buf, src in [(kb, k_in), (vb, v_in), (gb, g_in)]:
                b.dma("sync", buf[:].rearrange("p (a h d) -> p a h d", a=2, h=16),
                      src[128 * tt:128 * (tt + 1), :].rearrange(
                          "(a p) (h d) -> p a h d", a=2, h=16))
            for half in range(2):
                for h in range(16):
                    psd = PS[0] if h < 8 else PS[1]
                    col = 64 * (h % 8)
                    ks_ = kb[:, 1024 * half + 64 * h:1024 * half + 64 * (h + 1)]
                    vs_ = vb[:, 1024 * half + 64 * h:1024 * half + 64 * (h + 1)]
                    b.op("tensor", lambda t, psd=psd, col=col, ks_=ks_, vs_=vs_: t.matmul(
                        psd[0:64, col:col + 64], ks_, vs_, start=True, stop=True),
                        reads=[ks_, vs_], writes=[psd[0:64, col:col + 64]])
                    psz = PS[2]
                    b.op("tensor", lambda t, h=h, psz=psz, ks_=ks_: t.matmul(
                        psz[0:64, h:h + 1], ks_, onesb[:], start=True, stop=True),
                        reads=[ks_, onesb[:]], writes=[psz[0:64, h:h + 1]])
                b.op("vector", lambda v: v.tensor_tensor(out=S32[:, 0:512],
                     in0=S32[:, 0:512], in1=PS[0][0:64, :], op=OP.add),
                     reads=[S32[:], PS[0][0:64, :]], writes=[S32[:]])
                b.op("vector", lambda v: v.tensor_tensor(out=S32[:, 512:1024],
                     in0=S32[:, 512:1024], in1=PS[1][0:64, :], op=OP.add),
                     reads=[S32[:], PS[1][0:64, :]], writes=[S32[:]])
                b.op("vector", lambda v: v.tensor_tensor(out=Z32[:],
                     in0=Z32[:], in1=PS[2][0:64, 0:16], op=OP.add),
                     reads=[Z32[:], PS[2][0:64, 0:16]], writes=[Z32[:]])
                b.op("vector", lambda v: v.tensor_copy(S_sb[:], S32[:]),
                     reads=[S32[:]], writes=[S_sb[:]])
                b.op("vector", lambda v: v.tensor_copy(Z_sb[:], Z32[:]),
                     reads=[Z32[:]], writes=[Z_sb[:]])
                for h in range(16):
                    psn = PS[3] if h < 8 else PS[4]
                    col = 64 * (h % 8)
                    qs_ = qb[:, 128 * h + 64 * half:128 * h + 64 * (half + 1)]
                    b.op("tensor", lambda t, psn=psn, col=col, qs_=qs_, h=h: t.matmul(
                        psn[0:64, col:col + 64], qs_, S_sb[:, 64 * h:64 * (h + 1)],
                        start=True, stop=True),
                        reads=[qs_, S_sb[:, 64 * h:64 * (h + 1)]],
                        writes=[psn[0:64, col:col + 64]])
                    b.op("tensor", lambda t, qs_=qs_, h=h: t.matmul(
                        PS[5][0:64, h:h + 1], qs_, Z_sb[:, h:h + 1],
                        start=True, stop=True),
                        reads=[qs_, Z_sb[:, h:h + 1]], writes=[PS[5][0:64, h:h + 1]])
                b.op("vector", lambda v: v.tensor_scalar(out=dent[:], in0=PS[5][0:64, 0:16],
                     scalar1=1e-6, scalar2=None, op0=OP.add),
                     reads=[PS[5][0:64, 0:16]], writes=[dent[:]])
                b.op("vector", lambda v: v.reciprocal(out=dent[:], in_=dent[:]),
                     reads=[dent[:]], writes=[dent[:]])
                b.op("vector", lambda v, half=half, gb=gb: v.tensor_tensor(
                    out=att[:, 0:512], in0=PS[3][0:64, :],
                    in1=gb[:, 1024 * half:1024 * half + 512], op=OP.mult),
                    reads=[PS[3][0:64, :], gb[:]], writes=[att[:]])
                b.op("vector", lambda v, half=half, gb=gb: v.tensor_tensor(
                    out=att[:, 512:1024], in0=PS[4][0:64, :],
                    in1=gb[:, 1024 * half + 512:1024 * half + 1024], op=OP.mult),
                    reads=[PS[4][0:64, :], gb[:]], writes=[att[:]])
                a3 = att[:].rearrange("p (h e) -> p h e", h=16)
                r3 = dent[:].unsqueeze(2).broadcast_to([64, 16, 64])
                b.op("vector", lambda v, a3=a3, r3=r3: v.tensor_tensor(
                    out=a3, in0=a3, in1=r3, op=OP.mult),
                    reads=[att[:], dent[:]], writes=[att[:]])
                if tt == 0 and half == 0:
                    nm = b.sbuf("nm32", [64, D], F32)
                    b.op("vector", lambda v: v.tensor_copy(nm[:, 0:512], PS[3][0:64, :]),
                         reads=[PS[3][0:64, :]], writes=[nm[:]])
                    b.op("vector", lambda v: v.tensor_copy(nm[:, 512:1024], PS[4][0:64, :]),
                         reads=[PS[4][0:64, :]], writes=[nm[:]])
                    b.dma("sync", dbg_num, nm[:])
                    gd = b.sbuf("gd32", [64, 2048], F32)
                    b.op("vector", lambda v, gb=gb: v.tensor_copy(gd[:], gb[:]),
                         reads=[gb[:]], writes=[gd[:]])
                    b.dma("sync", dbg_g, gd[:])
                    b.dma("sync", dbg_att, att[:])
                    b.dma("sync", dbg_S, S32[:])
                    b.dma("sync", dbg_den, dent[:])
                    qd = b.sbuf("qd32", [64, 2048], F32)
                    b.op("vector", lambda v, qb=qb: v.tensor_copy(qd[:], qb[:]),
                         reads=[qb[:]], writes=[qd[:]])
                    b.dma("sync", dbg_q, qd[:])
                b.op("vector", lambda v: v.tensor_copy(attb[:], att[:]),
                     reads=[att[:]], writes=[attb[:]])
                b.dma("sync", at_dram[128 * tt + 64 * half:128 * tt + 64 * (half + 1), :],
                      attb[:])

        for cs in range(4):
            tsl = slice(512 * cs, 512 * (cs + 1))
            for q4 in range(4):
                tadr = 512 * cs + 128 * q4
                b.dma("sync", atile[:], at_dram[tadr:tadr + 128, :])
                for ch in range(NCH):
                    b.op("tensor", lambda t, ch=ch: t.transpose(
                        PSB[:, 0:128], atile[:, 128 * ch:128 * (ch + 1)],
                        identb[:]), reads=[atile[:], identb[:]],
                        writes=[PSB[:, 0:128]])
                    b.op("vector", lambda v, ch=ch, q4=q4: v.tensor_copy(
                        afm[:, 512 * ch + 128 * q4: 512 * ch + 128 * (q4 + 1)],
                        PSB[:, 0:128]),
                        reads=[PSB[:, 0:128]], writes=[afm[:]])
            for ft in range(8):
                ps = PS[6]
                for ch in range(NCH):
                    b.op("tensor", lambda t, ft=ft, ch=ch, ps=ps: t.matmul(
                        ps[:, :], wosb[ch][:, 128 * ft:128 * (ft + 1)],
                        afm[:, 512 * ch:512 * (ch + 1)],
                        start=(ch == 0), stop=(ch == NCH - 1)),
                        reads=[wosb[ch][:], afm[:]], writes=[ps[:, :]])
                xb = st1[ft % 2]
                b.dma("sync", xb[:, 0:512], xn1_in[128 * ft:128 * (ft + 1), tsl])
                b.op("vector", lambda v, ps=ps, ft=ft: v.tensor_scalar(
                    out=x2t[:], in0=ps[:, :], scalar1=boc[:, ft:ft + 1], scalar2=None,
                    op0=OP.add), reads=[ps[:, :], boc[:, ft:ft + 1]], writes=[x2t[:]])
                b.op("vector", lambda v, xb=xb: v.tensor_tensor(out=x2t[:], in0=x2t[:],
                     in1=xb[:, 0:512], op=OP.add), reads=[x2t[:], xb[:, 0:512]],
                     writes=[x2t[:]])
                b.dma("sync", x2_out[128 * ft:128 * (ft + 1), tsl], x2t[:])
                b.op("scalar", lambda s: s.activation(out=sqs[:], in_=x2t[:],
                     func=AF.Square), reads=[x2t[:]], writes=[sqs[:]])
                b.op("tensor", lambda t, ft=ft, cs=cs: t.matmul(
                    PS[5][0:1, :], ones_col[:], sqs[:],
                    start=(ft == 0), stop=(ft == NCH - 1)),
                    reads=[sqs[:], ones_col[:]], writes=[PS[5][0:1, :]])
            b.op("vector", lambda v, tsl=tsl: v.tensor_copy(sx_row[0:1, tsl], PS[5][0:1, :]),
                 reads=[PS[5][0:1, :]], writes=[sx_row[0:1, tsl]])
        b.op("vector", lambda v: v.tensor_scalar(out=sx_row[:], in0=sx_row[:],
             scalar1=1.0 / D, scalar2=None, op0=OP.mult), reads=[sx_row[:]], writes=[sx_row[:]])
        b.op("scalar", lambda s: s.activation(out=sx_row[:], in_=sx_row[:],
             func=AF.Sqrt), reads=[sx_row[:]], writes=[sx_row[:]])
        b.op("vector", lambda v: v.reciprocal(out=sx_row[:], in_=sx_row[:]),
             reads=[sx_row[:]], writes=[sx_row[:]])
        ones_row = b.sbuf("ones_row", [1, 128], F32)
        b.op("vector", lambda v: v.memset(ones_row[:], 1.0), writes=[ones_row[:]])
        h_bcast_row(b, PS[0], rstdbc, sx_row, ones_row, T)
        for i in range(NCH):
            xb = st1[i % 2]
            b.dma("sync", xb[:], x2_out[128 * i:128 * (i + 1), :])
            b.op("vector", lambda v, xb=xb: v.tensor_tensor(out=xb[:], in0=xb[:],
                 in1=rstdbc[:], op=OP.mult), reads=[xb[:], rstdbc[:]], writes=[xb[:]])
            b.op("vector", lambda v, i=i, xb=xb: v.tensor_scalar(out=xrtile[:], in0=xb[:],
                 scalar1=ns2[:, i:i + 1], scalar2=None, op0=OP.mult),
                 reads=[xb[:], ns2[:, i:i + 1]], writes=[xrtile[:]])
            b.dma("sync", xr2_out[128 * i:128 * (i + 1), :], xrtile[:])
        b.finalize()
    return nc



def build_L4b():
    nc = bass.Bass("TRN2", target_bir_lowering=False)
    with ExitStack() as ctx:
        b = TB(nc, ctx)
        xr2_in = b.dram("xr2_bf", [D, T], BF16, kind="ExternalInput").ap()
        xn1_in = b.dram("xn1_fm", [D, T], F32, kind="ExternalInput").ap()
        mins_in = b.dram("mins", [1, NCORES], F32, kind="ExternalInput").ap()
        maxs_in = b.dram("maxs", [1, NCORES], F32, kind="ExternalInput").ap()
        rw_in = b.dram("router_w", [D, 8], F32, kind="ExternalInput").ap()
        rb_in = b.dram("router_b", [1, 8], F32, kind="ExternalInput").ap()
        w1_in = b.dram("w1t_ffn", [D, 4096], F32, kind="ExternalInput").ap()
        b1_in = b.dram("b1_col", [128, 32], F32, kind="ExternalInput").ap()
        iota_in = b.dram("iota8", [1, 8], F32, kind="ExternalInput").ap()
        ident_in = b.dram("ident", [128, 128], F32, kind="ExternalInput").ap()
        ridx1_o = b.dram("ridx1", [16, 128], F32, kind="ExternalOutput").ap()
        ridx2_o = b.dram("ridx2", [16, 128], F32, kind="ExternalOutput").ap()
        rg1_o = b.dram("rg1", [16, 128], F32, kind="ExternalOutput").ap()
        rg2_o = b.dram("rg2", [16, 128], F32, kind="ExternalOutput").ap()
        rr1_o = b.dram("rr1", [1, T], F32, kind="ExternalOutput").ap()
        rr2_o = b.dram("rr2", [1, T], F32, kind="ExternalOutput").ap()
        cnt_o = b.dram("cnt_row", [1, 8], F32, kind="ExternalOutput").ap()
        h2_o = b.dram("h2_bf", [4096, T], BF16, kind="ExternalOutput").ap()
        mmh2_o = b.dram("minmax_h2", [1, 2], F32, kind="ExternalOutput").ap()
        wq1_dram = b.dram("wq1_scr", [D, 4096], BF16).ap()

        xcb = [b.sbuf(f"xcb{i}", [128, D], BF16) for i in range(2)]
        rw_sb = [b.sbuf(f"rwsb{i}", [128, 8], BF16) for i in range(NCH)]
        rbbc = b.sbuf("rbbc", [128, 8], F32)
        bc8 = b.sbuf("bc8", [128, 8], F32)
        lg = b.sbuf("lg", [128, 8], F32)
        m1 = b.sbuf("m1c", [128, 1], F32)
        m2 = b.sbuf("m2c", [128, 1], F32)
        negm = b.sbuf("negm", [128, 1], F32)
        ssum = b.sbuf("ssum", [128, 1], F32)
        eq1 = b.sbuf("eq1", [128, 8], F32)
        tsc1 = b.sbuf("tsc1", [128, 8], F32)
        idx1c = b.sbuf("idx1c", [128, 1], F32)
        idx2c = b.sbuf("idx2c", [128, 1], F32)
        ind1 = b.sbuf("ind1", [128, 8], F32)
        ind2 = b.sbuf("ind2", [128, 8], F32)
        p2 = b.sbuf("p2", [128, 8], F32)
        idx1a = b.sbuf("idx1a", [128, 16], F32)
        idx2a = b.sbuf("idx2a", [128, 16], F32)
        g1a = b.sbuf("g1a", [128, 16], F32)
        g2a = b.sbuf("g2a", [128, 16], F32)
        tr16 = b.sbuf("tr16", [16, 128], F32)
        ind1T = b.sbuf("ind1T", [8, T], F32)
        ind2T = b.sbuf("ind2T", [8, T], F32)
        zrow = b.sbuf("zrow", [8, T], F32)
        excl1 = b.sbuf("excl1", [8, T], F32)
        excl2 = b.sbuf("excl2", [8, T], F32)
        prod = b.sbuf("prod", [8, T], F32)
        r1row = b.sbuf("r1row", [1, T], F32)
        r2row = b.sbuf("r2row", [1, T], F32)
        cnt8 = b.sbuf("cnt8", [8, 1], F32)
        cntr = b.sbuf("cntr", [1, 8], F32)
        ident = b.sbuf("identsb", [128, 128], F32)
        ones_row = b.sbuf("ones_row", [1, 128], F32)
        ones_col = b.sbuf("ones_col", [128, 1], F32)
        sc1 = b.sbuf("sc1", [1, NCORES], F32)
        gmin1 = b.sbuf("gmin1", [1, 1], F32)
        gmax1 = b.sbuf("gmax1", [1, 1], F32)
        scl = b.sbuf("scl", [1, 1], F32)
        zpt = b.sbuf("zpt", [1, 1], F32)
        invs_col = b.sbuf("invs_col", [128, 1], F32)
        zp_col = b.sbuf("zp_col", [128, 1], F32)
        comb_col = b.sbuf("comb_col", [128, 1], F32)
        wmaxc = b.sbuf("wmaxc", [128, 1], F32)
        wrow16 = b.sbuf("wrow16", [1, 128], F32)
        wst = [b.sbuf(f"wst{i}", [128, 4096], F32) for i in range(2)]
        wqt = [b.sbuf(f"wqt4b{i}", [128, 4096], BF16) for i in range(2)]
        xqz = [b.sbuf(f"xqz{i}", [128, T], BF16) for i in range(NCH)]
        stx = [b.sbuf(f"stx{i}", [128, T], F32) for i in range(2)]
        wcol = [b.sbuf(f"wcol4b{i}", [128, D], BF16) for i in range(2)]
        h2t = b.sbuf("h2t", [128, T], F32)
        h2b = b.sbuf("h2b", [128, T], BF16)
        mxc = b.sbuf("mxc", [128, 32], F32)
        mnc = b.sbuf("mnc", [128, 32], F32)
        mmr = b.sbuf("mmr", [1, 128], F32)
        mmo = b.sbuf("mmo", [1, 2], F32)
        b1c = b.sbuf("b1c", [128, 32], F32)
        PS = [b.psum(f"PS{i}", [128, 512], F32) for i in range(8)]

        b.dma("sync", ident[:], ident_in)
        b.dma("sync", b1c[:], b1_in)
        b.op("vector", lambda v: v.memset(ones_row[:], 1.0), writes=[ones_row[:]])
        b.op("vector", lambda v: v.memset(ones_col[:], 1.0), writes=[ones_col[:]])
        b.op("vector", lambda v: v.memset(zrow[:], 0.0), writes=[zrow[:]])
        for i in range(NCH):
            b.dma("gpsimd", rw_sb[i][:], rw_in[128 * i:128 * (i + 1), :])
        iot = b.sbuf("iot", [1, 8], F32)
        b.dma("sync", iot[:], iota_in)
        rbr = b.sbuf("rbr", [1, 8], F32)
        b.dma("sync", rbr[:], rb_in)
        h_bcast_row(b, PS[0], rbbc, rbr, ones_row, 8)
        h_bcast_row(b, PS[0], bc8, iot, ones_row, 8)

        for tt in range(16):
            xb = xcb[tt % 2]
            b.dma("sync", xb[:].rearrange("p (c t) -> p c t", c=NCH),
                  xr2_in[:, 128 * tt:128 * (tt + 1)].rearrange("(c p) t -> p c t", p=128))
            ps = PS[1]
            for i in range(NCH):
                b.op("tensor", lambda t, i=i, ps=ps, xb=xb: t.matmul(
                    ps[:, 0:8], xb[:, 128 * i:128 * (i + 1)], rw_sb[i][:],
                    start=(i == 0), stop=(i == NCH - 1)),
                    reads=[xb[:], rw_sb[i][:]], writes=[ps[:, 0:8]])
            b.op("vector", lambda v, ps=ps: v.tensor_tensor(out=lg[:], in0=ps[:, 0:8],
                 in1=rbbc[:], op=OP.add), reads=[ps[:, 0:8], rbbc[:]], writes=[lg[:]])
            b.op("vector", lambda v: v.reduce_max(out=m1[:], in_=lg[:],
                 axis=mybir.AxisListType.X), reads=[lg[:]], writes=[m1[:]])
            b.op("vector", lambda v: v.tensor_scalar(out=negm[:], in0=m1[:],
                 scalar1=-1.0, scalar2=None, op0=OP.mult), reads=[m1[:]], writes=[negm[:]])
            b.op("scalar", lambda s: s.activation(out=lg[:], in_=lg[:], func=AF.Exp,
                 bias=negm[:]), reads=[lg[:], negm[:]], writes=[lg[:]])
            b.op("vector", lambda v: v.reduce_sum(out=ssum[:], in_=lg[:],
                 axis=mybir.AxisListType.X), reads=[lg[:]], writes=[ssum[:]])
            b.op("vector", lambda v: v.reciprocal(out=ssum[:], in_=ssum[:]),
                 reads=[ssum[:]], writes=[ssum[:]])
            b.op("vector", lambda v: v.tensor_scalar(out=lg[:], in0=lg[:],
                 scalar1=ssum[:], scalar2=None, op0=OP.mult),
                 reads=[lg[:], ssum[:]], writes=[lg[:]])
            b.op("vector", lambda v: v.reduce_max(out=m1[:], in_=lg[:],
                 axis=mybir.AxisListType.X), reads=[lg[:]], writes=[m1[:]])
            b.op("vector", lambda v: v.tensor_scalar(out=eq1[:], in0=lg[:],
                 scalar1=m1[:], scalar2=None, op0=OP.is_ge),
                 reads=[lg[:], m1[:]], writes=[eq1[:]])
            b.op("vector", lambda v: v.scalar_tensor_tensor(out=tsc1[:], in0=eq1[:],
                 scalar=-9.0, in1=bc8[:], op0=OP.mult, op1=OP.mult),
                 reads=[eq1[:], bc8[:]], writes=[tsc1[:]])
            b.op("vector", lambda v: v.tensor_tensor(out=tsc1[:], in0=eq1[:],
                 in1=bc8[:], op=OP.mult), reads=[eq1[:], bc8[:]], writes=[tsc1[:]])
            b.op("vector", lambda v: v.tensor_scalar(out=p2[:], in0=eq1[:],
                 scalar1=-9.0, scalar2=9.0, op0=OP.mult, op1=OP.add),
                 reads=[eq1[:]], writes=[p2[:]])
            b.op("vector", lambda v: v.tensor_tensor(out=tsc1[:], in0=tsc1[:],
                 in1=p2[:], op=OP.add), reads=[tsc1[:], p2[:]], writes=[tsc1[:]])
            b.op("vector", lambda v: v.tensor_reduce(out=idx1c[:], in_=tsc1[:],
                 axis=mybir.AxisListType.X, op=OP.min), reads=[tsc1[:]], writes=[idx1c[:]])
            b.op("vector", lambda v: v.tensor_scalar(out=ind1[:], in0=bc8[:],
                 scalar1=idx1c[:], scalar2=None, op0=OP.is_equal),
                 reads=[bc8[:], idx1c[:]], writes=[ind1[:]])
            b.op("vector", lambda v: v.scalar_tensor_tensor(out=p2[:], in0=ind1[:],
                 scalar=-1e30, in1=lg[:], op0=OP.mult, op1=OP.add),
                 reads=[ind1[:], lg[:]], writes=[p2[:]])
            b.op("vector", lambda v: v.reduce_max(out=m2[:], in_=p2[:],
                 axis=mybir.AxisListType.X), reads=[p2[:]], writes=[m2[:]])
            b.op("vector", lambda v: v.tensor_scalar(out=eq1[:], in0=p2[:],
                 scalar1=m2[:], scalar2=None, op0=OP.is_ge),
                 reads=[p2[:], m2[:]], writes=[eq1[:]])
            b.op("vector", lambda v: v.tensor_tensor(out=tsc1[:], in0=eq1[:],
                 in1=bc8[:], op=OP.mult), reads=[eq1[:], bc8[:]], writes=[tsc1[:]])
            b.op("vector", lambda v: v.tensor_scalar(out=p2[:], in0=eq1[:],
                 scalar1=-9.0, scalar2=9.0, op0=OP.mult, op1=OP.add),
                 reads=[eq1[:]], writes=[p2[:]])
            b.op("vector", lambda v: v.tensor_tensor(out=tsc1[:], in0=tsc1[:],
                 in1=p2[:], op=OP.add), reads=[tsc1[:], p2[:]], writes=[tsc1[:]])
            b.op("vector", lambda v: v.tensor_reduce(out=idx2c[:], in_=tsc1[:],
                 axis=mybir.AxisListType.X, op=OP.min), reads=[tsc1[:]], writes=[idx2c[:]])
            b.op("vector", lambda v: v.tensor_scalar(out=ind2[:], in0=bc8[:],
                 scalar1=idx2c[:], scalar2=None, op0=OP.is_equal),
                 reads=[bc8[:], idx2c[:]], writes=[ind2[:]])
            b.op("vector", lambda v: v.tensor_tensor(out=ssum[:], in0=m1[:],
                 in1=m2[:], op=OP.add), reads=[m1[:], m2[:]], writes=[ssum[:]])
            b.op("vector", lambda v: v.reciprocal(out=ssum[:], in_=ssum[:]),
                 reads=[ssum[:]], writes=[ssum[:]])
            b.op("vector", lambda v, tt=tt: v.tensor_tensor(out=g1a[:, tt:tt + 1],
                 in0=m1[:], in1=ssum[:], op=OP.mult),
                 reads=[m1[:], ssum[:]], writes=[g1a[:, tt:tt + 1]])
            b.op("vector", lambda v, tt=tt: v.tensor_tensor(out=g2a[:, tt:tt + 1],
                 in0=m2[:], in1=ssum[:], op=OP.mult),
                 reads=[m2[:], ssum[:]], writes=[g2a[:, tt:tt + 1]])
            b.op("vector", lambda v, tt=tt: v.tensor_copy(idx1a[:, tt:tt + 1], idx1c[:]),
                 reads=[idx1c[:]], writes=[idx1a[:, tt:tt + 1]])
            b.op("vector", lambda v, tt=tt: v.tensor_copy(idx2a[:, tt:tt + 1], idx2c[:]),
                 reads=[idx2c[:]], writes=[idx2a[:, tt:tt + 1]])
            for ind, dst in [(ind1, ind1T), (ind2, ind2T)]:
                b.op("tensor", lambda t, ind=ind: t.transpose(
                    PS[2][0:8, 0:128], ind[:], ident[:]),
                    reads=[ind[:], ident[:]], writes=[PS[2][0:8, 0:128]])
                b.op("vector", lambda v, dst=dst, tt=tt: v.tensor_copy(
                    dst[:, 128 * tt:128 * (tt + 1)], PS[2][0:8, 0:128]),
                    reads=[PS[2][0:8, 0:128]], writes=[dst[:, 128 * tt:128 * (tt + 1)]])
        b.op("vector", lambda v: v.tensor_tensor_scan(out=excl1[:], data0=ind1T[:],
             data1=zrow[:], initial=0.0, op0=OP.add, op1=OP.add),
             reads=[ind1T[:], zrow[:]], writes=[excl1[:]])
        b.op("vector", lambda v: v.tensor_tensor_scan(out=excl2[:], data0=ind2T[:],
             data1=zrow[:], initial=0.0, op0=OP.add, op1=OP.add),
             reads=[ind2T[:], zrow[:]], writes=[excl2[:]])
        b.op("vector", lambda v: v.tensor_tensor(out=cnt8[:], in0=excl1[:, T - 1:T],
             in1=excl2[:, T - 1:T], op=OP.add),
             reads=[excl1[:], excl2[:]], writes=[cnt8[:]])
        b.op("tensor", lambda t: t.transpose(PS[2][0:1, 0:8], cnt8[:, 0:1],
             ident[:, 0:8] if False else ident[0:8, 0:8]),
             reads=[cnt8[:], ident[:]], writes=[PS[2][0:1, 0:8]])
        b.op("vector", lambda v: v.tensor_copy(cntr[:], PS[2][0:1, 0:8]),
             reads=[PS[2][0:1, 0:8]], writes=[cntr[:]])
        b.dma("sync", cnt_o, cntr[:])
        b.op("vector", lambda v: v.tensor_tensor(out=excl1[:], in0=excl1[:],
             in1=ind1T[:], op=OP.subtract), reads=[excl1[:], ind1T[:]], writes=[excl1[:]])
        b.op("vector", lambda v: v.tensor_tensor(out=excl2[:], in0=excl2[:],
             in1=ind2T[:], op=OP.subtract), reads=[excl2[:], ind2T[:]], writes=[excl2[:]])
        b.op("vector", lambda v: v.tensor_tensor(out=prod[:], in0=excl1[:],
             in1=excl2[:], op=OP.add), reads=[excl1[:], excl2[:]], writes=[prod[:]])
        for indT, rrow, rout in [(ind1T, r1row, rr1_o), (ind2T, r2row, rr2_o)]:
            sel = b.sbuf(f"sel_{rrow.name if hasattr(rrow,'name') else id(rrow)}", [8, T], F32)
            b.op("vector", lambda v, indT=indT, sel=sel: v.tensor_tensor(out=sel[:],
                 in0=indT[:], in1=prod[:], op=OP.mult),
                 reads=[indT[:], prod[:]], writes=[sel[:]])
            for cs in range(4):
                b.op("tensor", lambda t, sel=sel, cs=cs: t.matmul(
                    PS[3][0:1, :], ones_col[0:8, 0:1], sel[:, 512 * cs:512 * (cs + 1)],
                    start=True, stop=True),
                    reads=[sel[:], ones_col[:]], writes=[PS[3][0:1, :]])
                b.op("vector", lambda v, rrow=rrow, cs=cs: v.tensor_copy(
                    rrow[0:1, 512 * cs:512 * (cs + 1)], PS[3][0:1, :]),
                    reads=[PS[3][0:1, :]], writes=[rrow[0:1, 512 * cs:512 * (cs + 1)]])
            b.dma("sync", rout, rrow[:])
        for arr, out in [(idx1a, ridx1_o), (idx2a, ridx2_o), (g1a, rg1_o), (g2a, rg2_o)]:
            b.op("tensor", lambda t, arr=arr: t.transpose(PS[2][0:16, 0:128], arr[:],
                 ident[:]), reads=[arr[:], ident[:]], writes=[PS[2][0:16, 0:128]])
            b.op("vector", lambda v: v.tensor_copy(tr16[:], PS[2][0:16, 0:128]),
                 reads=[PS[2][0:16, 0:128]], writes=[tr16[:]])
            b.dma("sync", out, tr16[:])

        b.dma("sync", sc1[:], mins_in)
        b.op("vector", lambda v: v.tensor_reduce(out=gmin1[:], in_=sc1[:],
             axis=mybir.AxisListType.X, op=OP.min), reads=[sc1[:]], writes=[gmin1[:]])
        b.dma("sync", sc1[:], maxs_in)
        b.op("vector", lambda v: v.reduce_max(out=gmax1[:], in_=sc1[:],
             axis=mybir.AxisListType.X), reads=[sc1[:]], writes=[gmax1[:]])
        b.op("vector", lambda v: v.tensor_tensor(out=scl[:], in0=gmax1[:],
             in1=gmin1[:], op=OP.subtract), reads=[gmax1[:], gmin1[:]], writes=[scl[:]])
        b.op("vector", lambda v: v.tensor_scalar(out=scl[:], in0=scl[:],
             scalar1=1.0 / 255.0, scalar2=1e-8, op0=OP.mult, op1=OP.max),
             reads=[scl[:]], writes=[scl[:]])
        b.op("vector", lambda v: v.reciprocal(out=zpt[:], in_=scl[:]),
             reads=[scl[:]], writes=[zpt[:]])
        b.op("vector", lambda v: v.tensor_tensor(out=gmin1[:], in0=gmin1[:],
             in1=zpt[:], op=OP.mult), reads=[gmin1[:], zpt[:]], writes=[gmin1[:]])
        b.op("vector", lambda v: v.tensor_scalar(out=gmin1[:], in0=gmin1[:],
             scalar1=-1.0, scalar2=RC, op0=OP.mult, op1=OP.add),
             reads=[gmin1[:]], writes=[gmin1[:]])
        b.op("vector", lambda v: v.tensor_scalar(out=gmin1[:], in0=gmin1[:],
             scalar1=-RC, scalar2=-128.0, op0=OP.add, op1=OP.add),
             reads=[gmin1[:]], writes=[gmin1[:]])
        b.op("vector", lambda v: v.tensor_scalar(out=gmin1[:], in0=gmin1[:],
             scalar1=127.0, scalar2=-128.0, op0=OP.min, op1=OP.max),
             reads=[gmin1[:]], writes=[gmin1[:]])
        b.op("tensor", lambda t: t.matmul(PS[0][:, 0:1], ones_row[:], zpt[:],
             start=True, stop=True), reads=[ones_row[:], zpt[:]], writes=[PS[0][:, 0:1]])
        b.op("vector", lambda v: v.tensor_copy(invs_col[:], PS[0][:, 0:1]),
             reads=[PS[0][:, 0:1]], writes=[invs_col[:]])
        b.op("tensor", lambda t: t.matmul(PS[0][:, 0:1], ones_row[:], gmin1[:],
             start=True, stop=True), reads=[ones_row[:], gmin1[:]], writes=[PS[0][:, 0:1]])
        b.op("vector", lambda v: v.tensor_copy(zp_col[:], PS[0][:, 0:1]),
             reads=[PS[0][:, 0:1]], writes=[zp_col[:]])
        for i in range(NCH):
            xb = stx[i % 2]
            b.dma("sync", xb[:], xn1_in[128 * i:128 * (i + 1), :])
            b.op("vector", lambda v, xb=xb: v.tensor_scalar(out=xb[:], in0=xb[:],
                 scalar1=invs_col[:], scalar2=RC, op0=OP.mult, op1=OP.add),
                 reads=[xb[:], invs_col[:]], writes=[xb[:]])
            b.op("vector", lambda v, xb=xb: v.tensor_scalar(out=xb[:], in0=xb[:],
                 scalar1=-RC, scalar2=zp_col[:], op0=OP.add, op1=OP.add),
                 reads=[xb[:], zp_col[:]], writes=[xb[:]])
            b.op("vector", lambda v, xb=xb: v.tensor_scalar(out=xb[:], in0=xb[:],
                 scalar1=127.0, scalar2=-128.0, op0=OP.min, op1=OP.max),
                 reads=[xb[:]], writes=[xb[:]])
            b.op("vector", lambda v, xb=xb, i=i: v.tensor_scalar(out=xqz[i][:], in0=xb[:],
                 scalar1=zp_col[:], scalar2=None, op0=OP.subtract),
                 reads=[xb[:], zp_col[:]], writes=[xqz[i][:]])
        for i in range(NCH):
            ws = wst[i % 2]
            b.dma("sync", ws[:], w1_in[128 * i:128 * (i + 1), :])
            b.op("scalar", lambda s, ws=ws: s.activation(out=ws[:], in_=ws[:],
                 func=AF.Abs), reads=[ws[:]], writes=[ws[:]])
            if i == 0:
                b.op("vector", lambda v, ws=ws: v.reduce_max(out=wmaxc[:], in_=ws[:],
                     axis=mybir.AxisListType.X), reads=[ws[:]], writes=[wmaxc[:]])
            else:
                b.op("vector", lambda v, ws=ws: v.reduce_max(out=m1[:], in_=ws[:],
                     axis=mybir.AxisListType.X), reads=[ws[:]], writes=[m1[:]])
                b.op("vector", lambda v: v.tensor_tensor(out=wmaxc[:], in0=wmaxc[:],
                     in1=m1[:], op=OP.max), reads=[wmaxc[:], m1[:]], writes=[wmaxc[:]])
        b.op("tensor", lambda t: t.transpose(PS[0][0:1, 0:128], wmaxc[:], ident[:]),
             reads=[wmaxc[:], ident[:]], writes=[PS[0][0:1, 0:128]])
        b.op("vector", lambda v: v.tensor_copy(wrow16[:], PS[0][0:1, 0:128]),
             reads=[PS[0][0:1, 0:128]], writes=[wrow16[:]])
        b.op("vector", lambda v: v.reduce_max(out=wrow16[0:1, 0:1], in_=wrow16[:],
             axis=mybir.AxisListType.X), reads=[wrow16[:]], writes=[wrow16[:]])
        b.op("vector", lambda v: v.tensor_scalar(out=wrow16[0:1, 0:1],
             in0=wrow16[0:1, 0:1], scalar1=1.0 / 127.0, scalar2=1e-8,
             op0=OP.mult, op1=OP.max), reads=[wrow16[:]], writes=[wrow16[:]])
        b.op("vector", lambda v: v.tensor_tensor(out=scl[:], in0=scl[:],
             in1=wrow16[0:1, 0:1], op=OP.mult), reads=[scl[:], wrow16[:]], writes=[scl[:]])
        b.op("tensor", lambda t: t.matmul(PS[0][:, 0:1], ones_row[:], scl[:],
             start=True, stop=True), reads=[ones_row[:], scl[:]], writes=[PS[0][:, 0:1]])
        b.op("vector", lambda v: v.tensor_copy(comb_col[:], PS[0][:, 0:1]),
             reads=[PS[0][:, 0:1]], writes=[comb_col[:]])
        b.op("vector", lambda v: v.reciprocal(out=m2[:], in_=wrow16[0:1, 0:1].broadcast_to([1, 1])),
             reads=[wrow16[:]], writes=[m2[:]]) if False else None
        invsw = b.sbuf("invsw", [1, 1], F32)
        b.op("vector", lambda v: v.reciprocal(out=invsw[:], in_=wrow16[0:1, 0:1]),
             reads=[wrow16[:]], writes=[invsw[:]])
        invswc = b.sbuf("invswc", [128, 1], F32)
        b.op("tensor", lambda t: t.matmul(PS[0][:, 0:1], ones_row[:], invsw[:],
             start=True, stop=True), reads=[ones_row[:], invsw[:]], writes=[PS[0][:, 0:1]])
        b.op("vector", lambda v: v.tensor_copy(invswc[:], PS[0][:, 0:1]),
             reads=[PS[0][:, 0:1]], writes=[invswc[:]])
        for i in range(NCH):
            ws = wst[i % 2]
            wq_ = wqt[i % 2]
            b.dma("sync", ws[:], w1_in[128 * i:128 * (i + 1), :])
            b.op("vector", lambda v, ws=ws: v.tensor_scalar(out=ws[:], in0=ws[:],
                 scalar1=invswc[:], scalar2=RC, op0=OP.mult, op1=OP.add),
                 reads=[ws[:], invswc[:]], writes=[ws[:]])
            b.op("vector", lambda v, ws=ws: v.tensor_scalar(out=ws[:], in0=ws[:],
                 scalar1=-RC, scalar2=127.0, op0=OP.add, op1=OP.min),
                 reads=[ws[:]], writes=[ws[:]])
            b.op("vector", lambda v, ws=ws, wq_=wq_: v.tensor_scalar(out=wq_[:], in0=ws[:],
                 scalar1=-127.0, scalar2=None, op0=OP.max),
                 reads=[ws[:]], writes=[wq_[:]])
            b.dma("sync", wq1_dram[128 * i:128 * (i + 1), :], wq_[:])
        for ot in range(32):
            wc = wcol[ot % 2]
            src2 = wq1_dram[:, 128 * ot:128 * (ot + 1)].rearrange("(c p) j -> p c j", p=128)
            b.dma("sync", wc[:].rearrange("p (c j) -> p c j", c=NCH), src2)
            for cs in range(4):
                ps = PS[4 + cs % 2]
                sl = slice(512 * cs, 512 * (cs + 1))
                for i in range(NCH):
                    b.op("tensor", lambda t, i=i, sl=sl, ps=ps, wc=wc: t.matmul(
                        ps[:, :], wc[:, 128 * i:128 * (i + 1)], xqz[i][:, sl],
                        start=(i == 0), stop=(i == NCH - 1)),
                        reads=[wc[:, 128 * i:128 * (i + 1)], xqz[i][:, sl]],
                        writes=[ps[:, :]])
                b.op("scalar", lambda s, sl=sl, ps=ps, ot=ot: s.activation(
                    out=h2t[:, sl], in_=ps[:, :], func=AF.Gelu,
                    scale=comb_col[:], bias=b1c[:, ot:ot + 1]),
                    reads=[ps[:, :], comb_col[:], b1c[:, ot:ot + 1]], writes=[h2t[:, sl]])
            b.op("vector", lambda v: v.tensor_copy(h2b[:], h2t[:]),
                 reads=[h2t[:]], writes=[h2b[:]])
            b.op("vector", lambda v, ot=ot: v.reduce_max(out=mxc[:, ot:ot + 1],
                 in_=h2b[:], axis=mybir.AxisListType.X), reads=[h2b[:]],
                 writes=[mxc[:, ot:ot + 1]])
            b.op("vector", lambda v, ot=ot: v.tensor_reduce(out=mnc[:, ot:ot + 1],
                 in_=h2b[:], axis=mybir.AxisListType.X, op=OP.min), reads=[h2b[:]],
                 writes=[mnc[:, ot:ot + 1]])
            b.dma("sync", h2_o[128 * ot:128 * (ot + 1), :], h2b[:])
        b.op("vector", lambda v: v.reduce_max(out=mxc[:, 0:1], in_=mxc[:],
             axis=mybir.AxisListType.X), reads=[mxc[:]], writes=[mxc[:, 0:1]])
        b.op("vector", lambda v: v.tensor_reduce(out=mnc[:, 0:1], in_=mnc[:],
             axis=mybir.AxisListType.X, op=OP.min), reads=[mnc[:]], writes=[mnc[:, 0:1]])
        b.op("tensor", lambda t: t.transpose(PS[0][0:1, 0:128], mxc[:, 0:1], ident[:]),
             reads=[mxc[:], ident[:]], writes=[PS[0][0:1, 0:128]])
        b.op("vector", lambda v: v.tensor_copy(mmr[:], PS[0][0:1, 0:128]),
             reads=[PS[0][0:1, 0:128]], writes=[mmr[:]])
        b.op("vector", lambda v: v.reduce_max(out=mmo[0:1, 1:2], in_=mmr[:],
             axis=mybir.AxisListType.X), reads=[mmr[:]], writes=[mmo[:]])
        b.op("tensor", lambda t: t.transpose(PS[0][0:1, 0:128], mnc[:, 0:1], ident[:]),
             reads=[mnc[:], ident[:]], writes=[PS[0][0:1, 0:128]])
        b.op("vector", lambda v: v.tensor_copy(mmr[:], PS[0][0:1, 0:128]),
             reads=[PS[0][0:1, 0:128]], writes=[mmr[:]])
        b.op("vector", lambda v: v.tensor_reduce(out=mmo[0:1, 0:1], in_=mmr[:],
             axis=mybir.AxisListType.X, op=OP.min), reads=[mmr[:]], writes=[mmo[:]])
        b.dma("sync", mmh2_o, mmo[:])

        cmp_part = b.dram("cmp_part", [16384, D], BF16, kind="ExternalOutput").ap()
        rr_scr = b.dram("rr_scr", [2, T], F32).ap()
        b.dma("sync", rr_scr[0:1, :], r1row[:])
        b.dma("sync", rr_scr[1:2, :], r2row[:])
        rr1cf = b.sbuf("rr1cf", [128, 16], F32)
        rr2cf = b.sbuf("rr2cf", [128, 16], F32)
        b.dma("sync", rr1cf[:], rr_scr[0:1, :].rearrange("a (j p) -> p (a j)", p=128),
              allow_slow_non_contiguous=True)
        b.dma("sync", rr2cf[:], rr_scr[1:2, :].rearrange("a (j p) -> p (a j)", p=128),
              allow_slow_non_contiguous=True)
        dd1 = b.sbuf("dd1", [128, 16], F32)
        dd2 = b.sbuf("dd2", [128, 16], F32)
        di1 = b.sbuf("di1", [128, 16], U32)
        di2 = b.sbuf("di2", [128, 16], U32)
        b.op("vector", lambda v: v.scalar_tensor_tensor(out=dd1[:], in0=idx1a[:],
             scalar=2048.0, in1=rr1cf[:], op0=OP.mult, op1=OP.add),
             reads=[idx1a[:], rr1cf[:]], writes=[dd1[:]])
        b.op("vector", lambda v: v.scalar_tensor_tensor(out=dd2[:], in0=idx2a[:],
             scalar=2048.0, in1=rr2cf[:], op0=OP.mult, op1=OP.add),
             reads=[idx2a[:], rr2cf[:]], writes=[dd2[:]])
        b.op("vector", lambda v: v.tensor_copy(di1[:], dd1[:]),
             reads=[dd1[:]], writes=[di1[:]])
        b.op("vector", lambda v: v.tensor_copy(di2[:], dd2[:]),
             reads=[dd2[:]], writes=[di2[:]])
        identb_in4 = b.dram("identb", [128, 128], BF16, kind="ExternalInput").ap()
        identb4 = b.sbuf("identb4", [128, 128], BF16)
        b.dma("sync", identb4[:], identb_in4)
        xtm = b.sbuf("xtm", [128, D], BF16)
        for tt in range(16):
            xb = xcb[tt % 2]
            b.dma("sync", xb[:].rearrange("p (c t) -> p c t", c=NCH),
                  xr2_in[:, 128 * tt:128 * (tt + 1)].rearrange("(c p) t -> p c t", p=128))
            for ch in range(NCH):
                pbank = PS[ch % 2]
                b.op("tensor", lambda t, ch=ch, xb=xb, pbank=pbank: t.matmul(
                    pbank[:, 0:128], xb[:, 128 * ch:128 * (ch + 1)], identb4[:],
                    start=True, stop=True),
                    reads=[xb[:], identb4[:]], writes=[pbank[:, 0:128]])
                b.op("vector", lambda v, ch=ch, pbank=pbank: v.tensor_copy(
                    xtm[:, 128 * ch:128 * (ch + 1)], pbank[:, 0:128]),
                    reads=[pbank[:, 0:128]], writes=[xtm[:, 128 * ch:128 * (ch + 1)]])
            b.idma(cmp_part, xtm[:], di1[:, tt:tt + 1], "out", 16383,
                   reads=[xtm[:]], writes=[cmp_part])
            b.idma(cmp_part, xtm[:], di2[:, tt:tt + 1], "out", 16383,
                   reads=[xtm[:]], writes=[cmp_part])
        b.finalize()
    return nc



NCAP = 5120

def build_L5():
    nc = bass.Bass("TRN2", target_bir_lowering=False)
    with ExitStack() as ctx:
        b = TB(nc, ctx)
        cmpmy_in = b.dram("cmp_my", [8 * 2048, D], BF16, kind="ExternalInput").ap()
        cnt_in = b.dram("cnts", [8, 8], F32, kind="ExternalInput").ap()
        w1_in = b.dram("w1e", [D, 4096], F32, kind="ExternalInput").ap()
        w2_in = b.dram("w2e", [4096, D], F32, kind="ExternalInput").ap()
        b1_in = b.dram("b1e_col", [128, 32], F32, kind="ExternalInput").ap()
        b2_in = b.dram("b2e_col", [128, 8], F32, kind="ExternalInput").ap()
        my_in = b.dram("myexp_col", [128, 1], F32, kind="ExternalInput").ap()
        ltri_in = b.dram("ltri8", [8, 8], F32, kind="ExternalInput").ap()
        ident_in = b.dram("ident", [128, 128], F32, kind="ExternalInput").ap()
        identb_in = b.dram("identb", [128, 128], BF16, kind="ExternalInput").ap()
        cmp_out = b.dram("cmp_out", [NCAP, D], BF16, kind="ExternalOutput").ap()

        myc = b.sbuf("myc", [128, 1], F32)
        cnt8 = b.sbuf("cnt8sb", [8, 8], F32)
        ltri = b.sbuf("ltri", [8, 8], F32)
        ident = b.sbuf("identsb", [128, 128], F32)
        identb = b.sbuf("identbb", [128, 128], BF16)
        ones_row = b.sbuf("ones_row", [1, 128], F32)
        prefc = b.sbuf("prefc", [8, 1], F32)
        NGT = NCAP // 128
        riota = b.sbuf("riota", [128, NGT], F32)
        stcols = b.sbuf("stcols", [128, 8], F32)
        cnc = b.sbuf("cnc", [128, 8], F32)
        totc = b.sbuf("totc", [128, 1], F32)
        geq = b.sbuf("geq", [128, NGT], F32)
        aci = b.sbuf("aci", [128, NGT], F32)
        ass = b.sbuf("ass", [128, NGT], F32)
        lrb = b.sbuf("lrb", [128, NGT], F32)
        srcb = b.sbuf("srcb", [128, NGT], F32)
        vld = b.sbuf("vld", [128, NGT], F32)
        t2b = b.sbuf("t2b", [128, NGT], F32)
        didxg = b.sbuf("didxg", [128, NGT], U32)
        xtile = [b.sbuf(f"xtile{i}", [128, D], BF16) for i in range(2)]
        wcol = [b.sbuf(f"wcol5{i}", [128, 1024], BF16) for i in range(2)]
        wcol2 = [b.sbuf(f"wcol5b{i}", [128, 4096], BF16) for i in range(2)]
        b1c = b.sbuf("b1c5", [128, 32], F32)
        b2c = b.sbuf("b2c5", [128, 8], F32)
        o2bs2 = [b.sbuf(f"o2bs2_{i}", [128, 4096], BF16) for i in range(2)]
        otile = b.sbuf("otile", [128, D], BF16)
        PS = [b.psum(f"PS{i}", [128, 512], F32) for i in range(7)]
        PSB = b.psum("PSB", [128, 256], BF16)

        b.dma("sync", ident[:], ident_in)
        b.dma("sync", identb[:], identb_in)
        b.dma("sync", myc[:], my_in)
        b.dma("sync", cnt8[:], cnt_in)
        b.dma("sync", ltri[:], ltri_in)
        b.dma("sync", b1c[:], b1_in)
        b.dma("sync", b2c[:], b2_in)
        b.op("vector", lambda v: v.memset(ones_row[:], 1.0), writes=[ones_row[:]])
        mysel = b.sbuf("mysel", [8, 8], F32)
        iot8 = b.sbuf("iot8", [1, 8], F32)
        b.op("gpsimd", lambda g: g.iota(iot8[:], pattern=[[1, 8]], base=0,
             channel_multiplier=0, allow_small_or_imprecise_dtypes=True),
             writes=[iot8[:]])
        iot8bc = b.sbuf("iot8bc", [8, 8], F32)
        b.op("tensor", lambda t: t.matmul(PS[0][0:8, 0:8], ones_row[0:1, 0:8], iot8[:],
             start=True, stop=True), reads=[ones_row[:], iot8[:]], writes=[PS[0][0:8, 0:8]])
        b.op("vector", lambda v: v.tensor_copy(iot8bc[:], PS[0][0:8, 0:8]),
             reads=[PS[0][0:8, 0:8]], writes=[iot8bc[:]])
        b.op("vector", lambda v: v.tensor_scalar(out=iot8bc[:], in0=iot8bc[:],
             scalar1=myc[0:8, 0:1], scalar2=None, op0=OP.is_equal),
             reads=[iot8bc[:], myc[:]], writes=[iot8bc[:]])
        b.op("vector", lambda v: v.tensor_tensor(out=mysel[:], in0=cnt8[:],
             in1=iot8bc[:], op=OP.mult), reads=[cnt8[:], iot8bc[:]], writes=[mysel[:]])
        mycnt = b.sbuf("mycnt", [8, 1], F32)
        b.op("vector", lambda v: v.reduce_sum(out=mycnt[:], in_=mysel[:],
             axis=mybir.AxisListType.X), reads=[mysel[:]], writes=[mycnt[:]])
        b.op("tensor", lambda t: t.matmul(PS[0][0:8, 0:1], ltri[:], mycnt[:],
             start=True, stop=True), reads=[ltri[:], mycnt[:]], writes=[PS[0][0:8, 0:1]])
        b.op("vector", lambda v: v.tensor_copy(prefc[:], PS[0][0:8, 0:1]),
             reads=[PS[0][0:8, 0:1]], writes=[prefc[:]])
        prefrow = b.sbuf("prefrow", [1, 8], F32)
        cntrow = b.sbuf("cntrow", [1, 8], F32)
        b.op("tensor", lambda t: t.transpose(PS[0][0:1, 0:8], prefc[:], ident[0:8, 0:8]),
             reads=[prefc[:], ident[:]], writes=[PS[0][0:1, 0:8]])
        b.op("vector", lambda v: v.tensor_copy(prefrow[:], PS[0][0:1, 0:8]),
             reads=[PS[0][0:1, 0:8]], writes=[prefrow[:]])
        b.op("tensor", lambda t: t.transpose(PS[0][0:1, 0:8], mycnt[:], ident[0:8, 0:8]),
             reads=[mycnt[:], ident[:]], writes=[PS[0][0:1, 0:8]])
        b.op("vector", lambda v: v.tensor_copy(cntrow[:], PS[0][0:1, 0:8]),
             reads=[PS[0][0:1, 0:8]], writes=[cntrow[:]])
        for c in range(8):
            h_bcast_row(b, PS[1], stcols[:, c:c + 1], prefrow[0:1, c:c + 1], ones_row, 1)
            h_bcast_row(b, PS[2], cnc[:, c:c + 1], cntrow[0:1, c:c + 1], ones_row, 1)
        tot1 = b.sbuf("tot1", [1, 1], F32)
        b.op("vector", lambda v: v.reduce_sum(out=tot1[:], in_=cntrow[:],
             axis=mybir.AxisListType.X), reads=[cntrow[:]], writes=[tot1[:]])
        h_bcast_row(b, PS[1], totc[:], tot1[:], ones_row, 1)
        b.op("gpsimd", lambda g: g.iota(riota[:], pattern=[[128, NGT]], base=0,
             channel_multiplier=1, allow_small_or_imprecise_dtypes=True),
             writes=[riota[:]])
        b.op("vector", lambda v: v.memset(aci[:], 0.0), writes=[aci[:]])
        b.op("vector", lambda v: v.memset(ass[:], 0.0), writes=[ass[:]])
        for c in range(1, 8):
            b.op("vector", lambda v, c=c: v.tensor_scalar(out=geq[:], in0=riota[:],
                 scalar1=stcols[:, c:c + 1], scalar2=None, op0=OP.is_ge),
                 reads=[riota[:], stcols[:, c:c + 1]], writes=[geq[:]])
            b.op("vector", lambda v: v.tensor_tensor(out=aci[:], in0=aci[:],
                 in1=geq[:], op=OP.add), reads=[aci[:], geq[:]], writes=[aci[:]])
            b.op("vector", lambda v, c=c: v.scalar_tensor_tensor(out=ass[:],
                 in0=geq[:], scalar=cnc[:, c - 1:c], in1=ass[:],
                 op0=OP.mult, op1=OP.add),
                 reads=[geq[:], cnc[:, c - 1:c], ass[:]], writes=[ass[:]])
        b.op("vector", lambda v: v.tensor_tensor(out=lrb[:], in0=riota[:],
             in1=ass[:], op=OP.subtract), reads=[riota[:], ass[:]], writes=[lrb[:]])
        b.op("vector", lambda v: v.scalar_tensor_tensor(out=srcb[:], in0=aci[:],
             scalar=2048.0, in1=lrb[:], op0=OP.mult, op1=OP.add),
             reads=[aci[:], lrb[:]], writes=[srcb[:]])
        b.op("vector", lambda v: v.tensor_scalar(out=vld[:], in0=riota[:],
             scalar1=totc[:, 0:1], scalar2=None, op0=OP.is_lt),
             reads=[riota[:], totc[:]], writes=[vld[:]])
        b.op("vector", lambda v: v.tensor_tensor(out=srcb[:], in0=srcb[:],
             in1=vld[:], op=OP.mult), reads=[srcb[:], vld[:]], writes=[srcb[:]])
        b.op("vector", lambda v: v.tensor_scalar(out=t2b[:], in0=vld[:],
             scalar1=-1.0e9, scalar2=1.0e9, op0=OP.mult, op1=OP.add),
             reads=[vld[:]], writes=[t2b[:]])
        b.op("vector", lambda v: v.tensor_tensor(out=srcb[:], in0=srcb[:],
             in1=t2b[:], op=OP.add), reads=[srcb[:], t2b[:]], writes=[srcb[:]])
        b.op("vector", lambda v: v.tensor_copy(didxg[:], srcb[:]),
             reads=[srcb[:]], writes=[didxg[:]])

        w1bf_dram = b.dram("w1bf_scr", [D, 4096], BF16).ap()
        w2bf_dram = b.dram("w2bf_scr", [4096, D], BF16).ap()
        cst = [b.sbuf(f"cst{i}", [128, 4096], F32) for i in range(2)]
        cbf = [b.sbuf(f"cbf{i}", [128, 4096], BF16) for i in range(2)]
        for i in range(8):
            cs_ = cst[i % 2]
            cb_ = cbf[i % 2]
            b.dma("sync", cs_[:], w1_in[128 * i:128 * (i + 1), :])
            b.op("vector", lambda v, cs_=cs_, cb_=cb_: v.tensor_copy(cb_[:], cs_[:]),
                 reads=[cs_[:]], writes=[cb_[:]])
            b.dma("sync", w1bf_dram[128 * i:128 * (i + 1), :], cb_[:])
        for i in range(32):
            cs_ = cst[i % 2]
            cb_ = cbf[i % 2]
            b.dma("sync", cs_[:, 0:D], w2_in[128 * i:128 * (i + 1), :])
            b.op("vector", lambda v, cs_=cs_, cb_=cb_: v.tensor_copy(
                 cb_[:, 0:D], cs_[:, 0:D]), reads=[cs_[:, 0:D]], writes=[cb_[:, 0:D]])
            b.dma("sync", w2bf_dram[128 * i:128 * (i + 1), :], cb_[:, 0:D])

        xgp = [b.sbuf(f"xgp{i}", [128, 4096], BF16) for i in range(2)]
        hstp = [b.sbuf(f"hstp{i}", [128, 32 * 512], BF16) for i in range(2)]
        for pp in range(5):
            for g2 in range(2):
                grp = 2 * pp + g2
                r0g = 512 * grp
                for q4 in range(4):
                    j = 4 * grp + q4
                    xb = xtile[q4 % 2]
                    b.idma(xb[:], cmpmy_in, didxg[:, j:j + 1], "in", 8 * 2048 - 1,
                           reads=[cmpmy_in], writes=[xb[:]])
                    for ch in range(NCH):
                        b.op("tensor", lambda t, ch=ch, xb=xb: t.transpose(
                            PSB[:, 0:128], xb[:, 128 * ch:128 * (ch + 1)], identb[:]),
                            reads=[xb[:], identb[:]], writes=[PSB[:, 0:128]])
                        b.op("vector", lambda v, ch=ch, q4=q4, g2=g2: v.tensor_copy(
                            xgp[g2][:, 512 * ch + 128 * q4:512 * ch + 128 * (q4 + 1)],
                            PSB[:, 0:128]), reads=[PSB[:, 0:128]], writes=[xgp[g2][:]])
            for ht in range(32):
                wc = wcol[ht % 2]
                b.dma("gpsimd", wc[:].rearrange("p (c j) -> p c j", c=NCH),
                      w1bf_dram[:, 128 * ht:128 * (ht + 1)].rearrange(
                          "(c p) j -> p c j", p=128))
                for g2 in range(2):
                    ps = PS[2 * (ht % 2) + g2]
                    for ch in range(NCH):
                        b.op("tensor", lambda t, ch=ch, ps=ps, wc=wc, g2=g2: t.matmul(
                            ps[:, :], wc[:, 128 * ch:128 * (ch + 1)],
                            xgp[g2][:, 512 * ch:512 * (ch + 1)],
                            start=(ch == 0), stop=(ch == NCH - 1)),
                            reads=[wc[:, 128 * ch:128 * (ch + 1)], xgp[g2][:]],
                            writes=[ps[:, :]])
                    b.op("scalar", lambda s, ps=ps, ht=ht, g2=g2: s.activation(
                        out=hstp[g2][:, 512 * ht:512 * (ht + 1)], in_=ps[:, :],
                        func=AF.Gelu, bias=b1c[:, ht:ht + 1]),
                        reads=[ps[:, :], b1c[:, ht:ht + 1]],
                        writes=[hstp[g2][:, 512 * ht:512 * (ht + 1)]])
            for ft in range(8):
                wc2 = wcol2[ft % 2]
                b.dma("gpsimd", wc2[:].rearrange("p (c j) -> p c j", c=32),
                      w2bf_dram[:, 128 * ft:128 * (ft + 1)].rearrange(
                          "(c p) j -> p c j", p=128))
                for g2 in range(2):
                    ps = PS[2 * (ft % 2) + g2]
                    for ch in range(32):
                        b.op("tensor", lambda t, ch=ch, ps=ps, wc2=wc2, g2=g2: t.matmul(
                            ps[:, :], wc2[:, 128 * ch:128 * (ch + 1)],
                            hstp[g2][:, 512 * ch:512 * (ch + 1)],
                            start=(ch == 0), stop=(ch == 31)),
                            reads=[wc2[:, 128 * ch:128 * (ch + 1)], hstp[g2][:]],
                            writes=[ps[:, :]])
                    b.op("vector", lambda v, ps=ps, ft=ft, g2=g2: v.tensor_scalar(
                        out=o2bs2[g2][:, 512 * ft:512 * (ft + 1)], in0=ps[:, :],
                        scalar1=b2c[:, ft:ft + 1], scalar2=None, op0=OP.add),
                        reads=[ps[:, :], b2c[:, ft:ft + 1]],
                        writes=[o2bs2[g2][:, 512 * ft:512 * (ft + 1)]])
            for g2 in range(2):
                r0g = 512 * (2 * pp + g2)
                for q4 in range(4):
                    for ft in range(8):
                        b.op("tensor", lambda t, q4=q4, ft=ft, g2=g2: t.transpose(
                            PSB[:, 0:128],
                            o2bs2[g2][:, 512 * ft + 128 * q4:512 * ft + 128 * (q4 + 1)],
                            identb[:]), reads=[o2bs2[g2][:], identb[:]],
                            writes=[PSB[:, 0:128]])
                        b.op("vector", lambda v, ft=ft: v.tensor_copy(
                            otile[:, 128 * ft:128 * (ft + 1)], PSB[:, 0:128]),
                            reads=[PSB[:, 0:128]], writes=[otile[:, 128 * ft:128 * (ft + 1)]])
                    b.dma("sync", cmp_out[r0g + 128 * q4:r0g + 128 * (q4 + 1), :],
                          otile[:])
        b.finalize()
    return nc


def build_L6():
    nc = bass.Bass("TRN2", target_bir_lowering=False)
    with ExitStack() as ctx:
        b = TB(nc, ctx)
        x1_in = b.dram("x1_fm", [D, T], F32, kind="ExternalInput").ap()
        x2_in = b.dram("x2_fm", [D, T], F32, kind="ExternalInput").ap()
        h2_in = b.dram("h2_bf", [4096, T], BF16, kind="ExternalInput").ap()
        mins_in = b.dram("mins2", [1, NCORES], F32, kind="ExternalInput").ap()
        maxs_in = b.dram("maxs2", [1, NCORES], F32, kind="ExternalInput").ap()
        w2_in = b.dram("w2_ffn", [4096, D], F32, kind="ExternalInput").ap()
        b2f_in = b.dram("b2f_col", [128, 8], F32, kind="ExternalInput").ap()
        cmp_in = b.dram("cmp_all", [8 * NCAP, D], BF16, kind="ExternalInput").ap()
        cnt_in = b.dram("cnts", [8, 8], F32, kind="ExternalInput").ap()
        pm_in = b.dram("premask_col", [8, 1], F32, kind="ExternalInput").ap()
        idx1_in = b.dram("idx1_tm", [128, 16], F32, kind="ExternalInput").ap()
        idx2_in = b.dram("idx2_tm", [128, 16], F32, kind="ExternalInput").ap()
        rr1_in = b.dram("rr1_tm", [128, 16], F32, kind="ExternalInput").ap()
        rr2_in = b.dram("rr2_tm", [128, 16], F32, kind="ExternalInput").ap()
        g1_in = b.dram("g1_tm", [128, 16], F32, kind="ExternalInput").ap()
        g2_in = b.dram("g2_tm", [128, 16], F32, kind="ExternalInput").ap()
        ident_in = b.dram("ident", [128, 128], F32, kind="ExternalInput").ap()
        out_o = b.dram("out_fm", [D, T], F32, kind="ExternalOutput").ap()
        wq2_dram = b.dram("wq2_scr", [4096, D], BF16).ap()
        moe_dram = b.dram("moe_scr", [T, D], F32).ap()

        ident = b.sbuf("identsb", [128, 128], F32)
        ones_row = b.sbuf("ones_row", [1, 128], F32)
        sc1 = b.sbuf("sc1", [1, NCORES], F32)
        gmin1 = b.sbuf("gmin1", [1, 1], F32)
        gmax1 = b.sbuf("gmax1", [1, 1], F32)
        scl = b.sbuf("scl", [1, 1], F32)
        zpt = b.sbuf("zpt", [1, 1], F32)
        invs_col = b.sbuf("invs_col", [128, 1], F32)
        zp_col = b.sbuf("zp_col", [128, 1], F32)
        comb_col = b.sbuf("comb_col", [128, 1], F32)
        invsw = b.sbuf("invsw", [1, 1], F32)
        invswc = b.sbuf("invswc", [128, 1], F32)
        wmaxc = b.sbuf("wmaxc", [128, 1], F32)
        wrow16 = b.sbuf("wrow16", [1, 128], F32)
        m1 = b.sbuf("m1c", [128, 1], F32)
        cnt8 = b.sbuf("cnt8sb", [8, 8], F32)
        pmc = b.sbuf("pmc", [8, 1], F32)
        strow = b.sbuf("strow", [1, 8], F32)
        stecol = b.sbuf("stecol", [128, 8], F32)
        idxa = [b.sbuf(f"idxa{k}", [128, 16], F32) for k in range(2)]
        rra = [b.sbuf(f"rra{k}", [128, 16], F32) for k in range(2)]
        ga = [b.sbuf(f"ga{k}", [128, 16], F32) for k in range(2)]
        ssel = b.sbuf("ssel", [128, 16], F32)
        eqt = b.sbuf("eqt", [128, 16], F32)
        grk = b.sbuf("grk", [128, 16], F32)
        kpt = b.sbuf("kpt", [128, 16], F32)
        tmq = b.sbuf("tmq", [128, 16], F32)
        didx = [b.sbuf(f"didx{k}", [128, 16], U32) for k in range(2)]
        geff = [b.sbuf(f"geff{k}", [128, 16], F32) for k in range(2)]
        e1t = b.sbuf("e1t", [128, D], BF16)
        e2t = b.sbuf("e2t", [128, D], BF16)
        m1t = b.sbuf("m1t", [128, D], F32)
        hb = [b.sbuf("hb0", [128, T], BF16)] * 2
        ft32 = b.sbuf("ft32", [128, T], F32)
        xq2z = [b.sbuf(f"xq2z{i}", [128, T], BF16) for i in range(32)]
        wst = [b.sbuf("wst0", [128, D], F32)] * 2
        wqt = [b.sbuf(f"wqt{i}", [128, D], BF16) for i in range(2)]
        wqcol = [b.sbuf("wqcol0", [128, 4096], BF16)] * 2
        b2c = b.sbuf("b2c6", [128, 8], F32)
        g2ch = b.sbuf("g2ch", [128, T], F32)
        moech = b.sbuf("moech", [128, T], F32)
        x1ch = b.sbuf("x1ch", [128, T], F32)
        x2ch = b.sbuf("x2ch", [128, T], F32)
        outch = ft32
        mtile = [b.sbuf(f"mtile{i}", [128, 128], F32) for i in range(2)]
        PS = [b.psum(f"PS{i}", [128, 512], F32) for i in range(8)]
        RC = 1.5 * 2 ** 23

        b.dma("sync", ident[:], ident_in)
        b.dma("sync", cnt8[:], cnt_in)
        b.dma("sync", pmc[:], pm_in)
        b.dma("sync", b2c[:], b2f_in)
        for k, (ii, rr, gg) in enumerate([(idx1_in, rr1_in, g1_in),
                                          (idx2_in, rr2_in, g2_in)]):
            b.dma("sync", idxa[k][:], ii)
            b.dma("sync", rra[k][:], rr)
            b.dma("sync", ga[k][:], gg)
        b.op("vector", lambda v: v.memset(ones_row[:], 1.0), writes=[ones_row[:]])

        b.dma("sync", sc1[:], mins_in)
        b.op("vector", lambda v: v.tensor_reduce(out=gmin1[:], in_=sc1[:],
             axis=mybir.AxisListType.X, op=OP.min), reads=[sc1[:]], writes=[gmin1[:]])
        b.dma("sync", sc1[:], maxs_in)
        b.op("vector", lambda v: v.reduce_max(out=gmax1[:], in_=sc1[:],
             axis=mybir.AxisListType.X), reads=[sc1[:]], writes=[gmax1[:]])
        b.op("vector", lambda v: v.tensor_tensor(out=scl[:], in0=gmax1[:],
             in1=gmin1[:], op=OP.subtract), reads=[gmax1[:], gmin1[:]], writes=[scl[:]])
        b.op("vector", lambda v: v.tensor_scalar(out=scl[:], in0=scl[:],
             scalar1=1.0 / 255.0, scalar2=1e-8, op0=OP.mult, op1=OP.max),
             reads=[scl[:]], writes=[scl[:]])
        b.op("vector", lambda v: v.reciprocal(out=zpt[:], in_=scl[:]),
             reads=[scl[:]], writes=[zpt[:]])
        b.op("vector", lambda v: v.tensor_tensor(out=gmin1[:], in0=gmin1[:],
             in1=zpt[:], op=OP.mult), reads=[gmin1[:], zpt[:]], writes=[gmin1[:]])
        b.op("vector", lambda v: v.tensor_scalar(out=gmin1[:], in0=gmin1[:],
             scalar1=-1.0, scalar2=RC, op0=OP.mult, op1=OP.add),
             reads=[gmin1[:]], writes=[gmin1[:]])
        b.op("vector", lambda v: v.tensor_scalar(out=gmin1[:], in0=gmin1[:],
             scalar1=-RC, scalar2=-128.0, op0=OP.add, op1=OP.add),
             reads=[gmin1[:]], writes=[gmin1[:]])
        b.op("vector", lambda v: v.tensor_scalar(out=gmin1[:], in0=gmin1[:],
             scalar1=127.0, scalar2=-128.0, op0=OP.min, op1=OP.max),
             reads=[gmin1[:]], writes=[gmin1[:]])
        b.op("tensor", lambda t: t.matmul(PS[0][:, 0:1], ones_row[:], zpt[:],
             start=True, stop=True), reads=[ones_row[:], zpt[:]], writes=[PS[0][:, 0:1]])
        b.op("vector", lambda v: v.tensor_copy(invs_col[:], PS[0][:, 0:1]),
             reads=[PS[0][:, 0:1]], writes=[invs_col[:]])
        b.op("tensor", lambda t: t.matmul(PS[0][:, 0:1], ones_row[:], gmin1[:],
             start=True, stop=True), reads=[ones_row[:], gmin1[:]], writes=[PS[0][:, 0:1]])
        b.op("vector", lambda v: v.tensor_copy(zp_col[:], PS[0][:, 0:1]),
             reads=[PS[0][:, 0:1]], writes=[zp_col[:]])

        b.op("tensor", lambda t: t.matmul(PS[0][0:1, 0:8], pmc[:], cnt8[:],
             start=True, stop=True), reads=[pmc[:], cnt8[:]], writes=[PS[0][0:1, 0:8]])
        b.op("vector", lambda v: v.tensor_copy(strow[:], PS[0][0:1, 0:8]),
             reads=[PS[0][0:1, 0:8]], writes=[strow[:]])
        for e in range(8):
            h_bcast_row(b, PS[1], stecol[:, e:e + 1], strow[0:1, e:e + 1], ones_row, 1)
        for k in range(2):
            b.op("vector", lambda v, k=k: v.memset(ssel[:], 0.0), writes=[ssel[:]])
            for e in range(8):
                b.op("vector", lambda v, k=k, e=e: v.tensor_scalar(out=eqt[:],
                     in0=idxa[k][:], scalar1=float(e), scalar2=None, op0=OP.is_equal),
                     reads=[idxa[k][:]], writes=[eqt[:]])
                b.op("vector", lambda v, e=e: v.scalar_tensor_tensor(out=ssel[:],
                     in0=eqt[:], scalar=stecol[:, e:e + 1], in1=ssel[:],
                     op0=OP.mult, op1=OP.add),
                     reads=[eqt[:], stecol[:, e:e + 1], ssel[:]], writes=[ssel[:]])
            b.op("vector", lambda v, k=k: v.tensor_tensor(out=grk[:], in0=rra[k][:],
                 in1=ssel[:], op=OP.add), reads=[rra[k][:], ssel[:]], writes=[grk[:]])
            b.op("vector", lambda v: v.tensor_scalar(out=kpt[:], in0=grk[:],
                 scalar1=float(NCAP), scalar2=None, op0=OP.is_lt),
                 reads=[grk[:]], writes=[kpt[:]])
            b.op("vector", lambda v, k=k: v.scalar_tensor_tensor(out=tmq[:],
                 in0=idxa[k][:], scalar=float(NCAP), in1=grk[:],
                 op0=OP.mult, op1=OP.add),
                 reads=[idxa[k][:], grk[:]], writes=[tmq[:]])
            b.op("vector", lambda v: v.tensor_tensor(out=tmq[:], in0=tmq[:],
                 in1=kpt[:], op=OP.mult), reads=[tmq[:], kpt[:]], writes=[tmq[:]])
            b.op("vector", lambda v: v.tensor_scalar(out=eqt[:], in0=kpt[:],
                 scalar1=-1.0e9, scalar2=1.0e9, op0=OP.mult, op1=OP.add),
                 reads=[kpt[:]], writes=[eqt[:]])
            b.op("vector", lambda v: v.tensor_tensor(out=tmq[:], in0=tmq[:],
                 in1=eqt[:], op=OP.add), reads=[tmq[:], eqt[:]], writes=[tmq[:]])
            b.op("vector", lambda v, k=k: v.tensor_copy(didx[k][:], tmq[:]),
                 reads=[tmq[:]], writes=[didx[k][:]])
            b.op("vector", lambda v, k=k: v.tensor_tensor(out=geff[k][:],
                 in0=ga[k][:], in1=kpt[:], op=OP.mult),
                 reads=[ga[k][:], kpt[:]], writes=[geff[k][:]])

        b.op("vector", lambda v: v.memset(e1t[:], 0.0), writes=[e1t[:]])
        b.op("vector", lambda v: v.memset(e2t[:], 0.0), writes=[e2t[:]])
        for j in range(16):
            b.idma(e1t[:], cmp_in, didx[0][:, j:j + 1], "in", 8 * NCAP - 1,
                   reads=[cmp_in], writes=[e1t[:]])
            b.idma(e2t[:], cmp_in, didx[1][:, j:j + 1], "in", 8 * NCAP - 1,
                   reads=[cmp_in], writes=[e2t[:]])
            b.op("vector", lambda v, j=j: v.tensor_scalar(out=m1t[:], in0=e1t[:],
                 scalar1=geff[0][:, j:j + 1], scalar2=None, op0=OP.mult),
                 reads=[e1t[:], geff[0][:, j:j + 1]], writes=[m1t[:]])
            b.op("vector", lambda v, j=j: v.scalar_tensor_tensor(out=m1t[:],
                 in0=e2t[:], scalar=geff[1][:, j:j + 1], in1=m1t[:],
                 op0=OP.mult, op1=OP.add),
                 reads=[e2t[:], geff[1][:, j:j + 1], m1t[:]], writes=[m1t[:]])
            b.dma("sync", moe_dram[128 * j:128 * (j + 1), :], m1t[:])

        for i in range(32):
            xb = hb[i % 2]
            b.dma("sync", xb[:], h2_in[128 * i:128 * (i + 1), :])
            b.op("vector", lambda v, xb=xb: v.tensor_scalar(out=ft32[:], in0=xb[:],
                 scalar1=invs_col[:], scalar2=RC, op0=OP.mult, op1=OP.add),
                 reads=[xb[:], invs_col[:]], writes=[ft32[:]])
            b.op("vector", lambda v: v.tensor_scalar(out=ft32[:], in0=ft32[:],
                 scalar1=-RC, scalar2=zp_col[:], op0=OP.add, op1=OP.add),
                 reads=[ft32[:], zp_col[:]], writes=[ft32[:]])
            b.op("vector", lambda v: v.tensor_scalar(out=ft32[:], in0=ft32[:],
                 scalar1=127.0, scalar2=-128.0, op0=OP.min, op1=OP.max),
                 reads=[ft32[:]], writes=[ft32[:]])
            b.op("vector", lambda v, i=i: v.tensor_scalar(out=xq2z[i][:], in0=ft32[:],
                 scalar1=zp_col[:], scalar2=None, op0=OP.subtract),
                 reads=[ft32[:], zp_col[:]], writes=[xq2z[i][:]])

        for i in range(32):
            ws = wst[i % 2]
            b.dma("sync", ws[:], w2_in[128 * i:128 * (i + 1), :])
            b.op("scalar", lambda s, ws=ws: s.activation(out=ws[:], in_=ws[:],
                 func=AF.Abs), reads=[ws[:]], writes=[ws[:]])
            if i == 0:
                b.op("vector", lambda v, ws=ws: v.reduce_max(out=wmaxc[:], in_=ws[:],
                     axis=mybir.AxisListType.X), reads=[ws[:]], writes=[wmaxc[:]])
            else:
                b.op("vector", lambda v, ws=ws: v.reduce_max(out=m1[:], in_=ws[:],
                     axis=mybir.AxisListType.X), reads=[ws[:]], writes=[m1[:]])
                b.op("vector", lambda v: v.tensor_tensor(out=wmaxc[:], in0=wmaxc[:],
                     in1=m1[:], op=OP.max), reads=[wmaxc[:], m1[:]], writes=[wmaxc[:]])
        b.op("tensor", lambda t: t.transpose(PS[0][0:1, 0:128], wmaxc[:], ident[:]),
             reads=[wmaxc[:], ident[:]], writes=[PS[0][0:1, 0:128]])
        b.op("vector", lambda v: v.tensor_copy(wrow16[:], PS[0][0:1, 0:128]),
             reads=[PS[0][0:1, 0:128]], writes=[wrow16[:]])
        b.op("vector", lambda v: v.reduce_max(out=wrow16[0:1, 0:1], in_=wrow16[:],
             axis=mybir.AxisListType.X), reads=[wrow16[:]], writes=[wrow16[:]])
        b.op("vector", lambda v: v.tensor_scalar(out=wrow16[0:1, 0:1],
             in0=wrow16[0:1, 0:1], scalar1=1.0 / 127.0, scalar2=1e-8,
             op0=OP.mult, op1=OP.max), reads=[wrow16[:]], writes=[wrow16[:]])
        b.op("vector", lambda v: v.tensor_tensor(out=scl[:], in0=scl[:],
             in1=wrow16[0:1, 0:1], op=OP.mult), reads=[scl[:], wrow16[:]], writes=[scl[:]])
        b.op("tensor", lambda t: t.matmul(PS[0][:, 0:1], ones_row[:], scl[:],
             start=True, stop=True), reads=[ones_row[:], scl[:]], writes=[PS[0][:, 0:1]])
        b.op("vector", lambda v: v.tensor_copy(comb_col[:], PS[0][:, 0:1]),
             reads=[PS[0][:, 0:1]], writes=[comb_col[:]])
        b.op("vector", lambda v: v.reciprocal(out=invsw[:], in_=wrow16[0:1, 0:1]),
             reads=[wrow16[:]], writes=[invsw[:]])
        b.op("tensor", lambda t: t.matmul(PS[0][:, 0:1], ones_row[:], invsw[:],
             start=True, stop=True), reads=[ones_row[:], invsw[:]], writes=[PS[0][:, 0:1]])
        b.op("vector", lambda v: v.tensor_copy(invswc[:], PS[0][:, 0:1]),
             reads=[PS[0][:, 0:1]], writes=[invswc[:]])
        for i in range(32):
            ws = wst[i % 2]
            wq_ = wqt[i % 2]
            b.dma("sync", ws[:], w2_in[128 * i:128 * (i + 1), :])
            b.op("vector", lambda v, ws=ws: v.tensor_scalar(out=ws[:], in0=ws[:],
                 scalar1=invswc[:], scalar2=RC, op0=OP.mult, op1=OP.add),
                 reads=[ws[:], invswc[:]], writes=[ws[:]])
            b.op("vector", lambda v, ws=ws: v.tensor_scalar(out=ws[:], in0=ws[:],
                 scalar1=-RC, scalar2=127.0, op0=OP.add, op1=OP.min),
                 reads=[ws[:]], writes=[ws[:]])
            b.op("vector", lambda v, ws=ws, wq_=wq_: v.tensor_scalar(out=wq_[:], in0=ws[:],
                 scalar1=-127.0, scalar2=None, op0=OP.max),
                 reads=[ws[:]], writes=[wq_[:]])
            b.dma("sync", wq2_dram[128 * i:128 * (i + 1), :], wq_[:])

        for of in range(8):
            wc = wqcol[of % 2]
            b.dma("sync", wc[:].rearrange("p (c j) -> p c j", c=32),
                  wq2_dram[:, 128 * of:128 * (of + 1)].rearrange("(c p) j -> p c j", p=128))
            for cs in range(4):
                ps = PS[4 + cs % 2]
                sl = slice(512 * cs, 512 * (cs + 1))
                for i in range(32):
                    b.op("tensor", lambda t, i=i, sl=sl, ps=ps, wc=wc: t.matmul(
                        ps[:, :], wc[:, 128 * i:128 * (i + 1)], xq2z[i][:, sl],
                        start=(i == 0), stop=(i == 31)),
                        reads=[wc[:, 128 * i:128 * (i + 1)], xq2z[i][:, sl]],
                        writes=[ps[:, :]])
                b.op("scalar", lambda s, sl=sl, ps=ps, of=of: s.activation(
                    out=g2ch[:, sl], in_=ps[:, :], func=AF.Silu,
                    scale=comb_col[:], bias=b2c[:, of:of + 1]),
                    reads=[ps[:, :], comb_col[:], b2c[:, of:of + 1]],
                    writes=[g2ch[:, sl]])
            for j in range(16):
                mt = mtile[j % 2]
                b.dma("sync", mt[:],
                      moe_dram[128 * j:128 * (j + 1), 128 * of:128 * (of + 1)])
                b.op("tensor", lambda t, mt=mt: t.transpose(PS[2][:, 0:128], mt[:],
                     ident[:]), reads=[mt[:], ident[:]], writes=[PS[2][:, 0:128]])
                b.op("vector", lambda v, j=j: v.tensor_copy(
                    moech[:, 128 * j:128 * (j + 1)], PS[2][:, 0:128]),
                    reads=[PS[2][:, 0:128]], writes=[moech[:, 128 * j:128 * (j + 1)]])
            b.dma("sync", x1ch[:], x1_in[128 * of:128 * (of + 1), :])
            b.dma("sync", x2ch[:], x2_in[128 * of:128 * (of + 1), :])
            b.op("vector", lambda v: v.tensor_tensor(out=x2ch[:], in0=x2ch[:],
                 in1=moech[:], op=OP.add), reads=[x2ch[:], moech[:]], writes=[x2ch[:]])
            b.op("vector", lambda v: v.tensor_tensor(out=outch[:], in0=x2ch[:],
                 in1=g2ch[:], op=OP.mult), reads=[x2ch[:], g2ch[:]], writes=[outch[:]])
            b.op("vector", lambda v: v.tensor_tensor(out=outch[:], in0=outch[:],
                 in1=x1ch[:], op=OP.add), reads=[outch[:], x1ch[:]], writes=[outch[:]])
            b.dma("sync", out_o[128 * of:128 * (of + 1), :], outch[:])
        b.finalize()
    return nc



def _splits(full):
    out = []
    for c in range(NCORES):
        bb, qq = c // 4, c % 4
        out.append(full[bb, 2048 * qq:2048 * (qq + 1)])
    return out

_IDENT = np.eye(128, dtype=np.float32)
_TRIMASK = np.where(np.arange(128)[None, :] >= np.arange(128)[:, None],
                    0.0, -1.0e4).astype(np.float32)

def run_pipeline(inputs, upto=99, keep=None):
    st = {}
    x = np.asarray(inputs["L0_attn_inputs"])
    WQ = np.asarray(inputs["L0_attn_WQ"]).transpose(1, 0, 2).reshape(D, D)
    WK = np.asarray(inputs["L0_attn_WK"]).transpose(1, 0, 2).reshape(D, D)
    WVd = np.asarray(inputs["L0_attn_WV_down"]).transpose(1, 0, 2).reshape(D, D)
    wqkd = np.ascontiguousarray(np.concatenate([WQ, WK, WVd], axis=1))
    in_maps = []
    for c in range(NCORES):
        bb, qq = c // 4, c % 4
        xc = np.ascontiguousarray(x[bb, 2048 * qq:2048 * (qq + 1), :].T)
        in_maps.append(dict(x_fm=xc, wqkd_fm=wqkd, ident=_IDENT))
    nc = build_L1()
    res = run_bass_kernel_spmd(nc, in_maps, core_ids=list(range(NCORES)))
    L1 = res.results
    st["L1"] = L1
    if upto <= 1:
        return st

    maxxn_all = np.concatenate([L1[c]["maxabs_xn"] for c in range(NCORES)], axis=1)
    wg_t = np.ascontiguousarray(np.asarray(inputs["L0_ffn_w_gate"]).T)
    wu_t = np.ascontiguousarray(np.asarray(inputs["L0_ffn_w_up"]).T)
    clip_x = np.asarray(inputs["L0_ffn_clip_x"]).reshape(1, 1).astype(np.float32)
    in_maps = []
    for c in range(NCORES):
        bu, hu = c // 4, 4 * (c % 4)
        r0 = 64 * hu
        qh = np.concatenate(
            [L1[4 * bu + qq]["qkd_bf"][r0:r0 + 256, :] for qq in range(4)], axis=1)
        kh = np.concatenate(
            [L1[4 * bu + qq]["qkd_bf"][D + r0:D + r0 + 256, :] for qq in range(4)], axis=1)
        in_maps.append(dict(qh_bf=np.ascontiguousarray(qh),
                            kh_bf=np.ascontiguousarray(kh),
                            xn_fm=L1[c]["xn_fm"], maxxn_all=maxxn_all,
                            clip_x=clip_x, wg_t=wg_t, wu_t=wu_t, ident=_IDENT,
                            trimask=_TRIMASK))
    nc = build_L2()
    res = run_bass_kernel_spmd(nc, in_maps, core_ids=list(range(NCORES)))
    L2 = res.results
    st["L2"] = L2
    if upto <= 2:
        return st

    diag_full = []
    for bb in range(2):
        rows = np.zeros((16, 8192), np.float32)
        for k in range(4):
            arr = L2[4 * bb + k]["attn_diag"].reshape(4, 128, 64)
            for u in range(4):
                rows[4 * k + u] = arr[u].T.reshape(8192)
        diag_full.append(rows)
    maxh_all = np.concatenate([L2[c]["maxh"] for c in range(NCORES)], axis=1)
    wd_t = np.ascontiguousarray(np.asarray(inputs["L0_ffn_w_down"]).T)
    wvu_fm = np.ascontiguousarray(
        np.asarray(inputs["L0_attn_WV_up"]).reshape(D, D))
    clip_down = np.asarray(inputs["L0_ffn_clip_down"]).reshape(1, 1).astype(np.float32)
    m0 = np.zeros((1, 128), np.float32); m0[0, 0:64] = 1.0
    m1 = np.zeros((1, 128), np.float32); m1[0, 64:128] = 1.0
    ns_col = np.ascontiguousarray(
        np.asarray(inputs["L1_attn_norm1_scale"]).reshape(8, 128).T).astype(np.float32)
    in_maps = []
    for c in range(NCORES):
        bb, qq = c // 4, c % 4
        in_maps.append(dict(
            x_fm=np.ascontiguousarray(x[bb, 2048 * qq:2048 * (qq + 1), :].T),
            xn_fm=L1[c]["xn_fm"], h_fm=L2[c]["h_fm"], maxh_all=maxh_all,
            clip_down=clip_down,
            diag16=np.ascontiguousarray(diag_full[bb][:, 2048 * qq:2048 * (qq + 1)]),
            dvdown_bf=np.ascontiguousarray(L1[c]["qkd_bf"][2 * D:3 * D, :]),
            wd_t=wd_t, wvu_fm=wvu_fm, mask0=m0, mask1=m1, ident=_IDENT,
            norm1_scale_col=ns_col))
    nc = build_L3a()
    res = run_bass_kernel_spmd(nc, in_maps, core_ids=list(range(NCORES)))
    L3a = res.results
    st["L3a"] = L3a
    if upto <= 3:
        return st

    A = lambda k: np.asarray(inputs[f"L1_attn_{k}"]).astype(np.float32)
    wqkvg = np.ascontiguousarray(np.concatenate(
        [A("Wq"), A("Wk"), A("Wv"), A("Wg")], axis=1))
    bq_col = np.ascontiguousarray(A("bq").reshape(8, 128).T)
    bkvg_rows = np.concatenate([A("bk"), A("bv"), A("bg")]).reshape(1, 3 * D)
    jj = np.arange(32, dtype=np.float64)
    inv = 1.0 / 10000.0 ** (jj / 32.0)
    pswap_m = np.zeros((128, 128), np.float32)
    for p in range(128):
        pswap_m[(p // 64) * 64 + ((p % 64) + 32) % 64, p] = 1.0
    in_maps = []
    for c in range(NCORES):
        bb, qq = c // 4, c % 4
        s = np.arange(2048 * qq, 2048 * (qq + 1), dtype=np.float64)
        ang = (s[:, None] * inv[None, :]).astype(np.float32)
        cosk = np.cos(ang).astype(np.float32)
        sink = np.sin(ang).astype(np.float32)
        cb = np.zeros((128, T), np.float32)
        sb = np.zeros((128, T), np.float32)
        for p in range(128):
            j = p % 32
            cb[p] = cosk[:, j]
            sb[p] = (-sink[:, j]) if (p % 64) < 32 else sink[:, j]
        in_maps.append(dict(
            xr1_bf=L3a[c]["xr1_bf"], wqkvg=wqkvg, bq_col=bq_col,
            bkvg_rows=bkvg_rows, CBIG=cb, SBIG=sb, cosK=cosk, sinK=sink,
            PSWAP=pswap_m))
    nc = build_L3b()
    res = run_bass_kernel_spmd(nc, in_maps, core_ids=list(range(NCORES)))
    L3b = res.results
    st["L3b"] = L3b
    if upto <= 4:
        return st

    wo = np.ascontiguousarray(A("Wo"))
    bo_col = np.ascontiguousarray(A("bo").reshape(8, 128).T)
    ns2_col = np.ascontiguousarray(A("norm2_scale").reshape(8, 128).T)
    identb = np.eye(128, dtype=ml_dtypes.bfloat16)
    in_maps = []
    for c in range(NCORES):
        bb, qq = c // 4, c % 4
        dsp = np.zeros((3, 64 * D), np.float32)
        zp = np.zeros((3, 64 * 16), np.float32)
        for j, q2 in enumerate(range(qq)):
            dsp[j] = L3b[4 * bb + q2]["ds_tot"].reshape(-1)
            zp[j] = L3b[4 * bb + q2]["z_tot"].reshape(-1)
        in_maps.append(dict(
            qro_bf=L3b[c]["qro_bf"], k_tm=L3b[c]["k_tm"], v_tm=L3b[c]["v_tm"],
            g_tm=L3b[c]["g_tm"], ds_prev=dsp, z_prev=zp,
            xn1_fm=L3a[c]["xn1_fm"], wo=wo, bo_col=bo_col,
            norm2_scale_col=ns2_col, ident=_IDENT, identb=identb))
    nc = build_L4a()
    res = run_bass_kernel_spmd(nc, in_maps, core_ids=list(range(NCORES)))
    L4a = res.results
    st["L4a"] = L4a
    if upto <= 5:
        return st

    mins = np.concatenate([L3a[c]["minmax_xn1"][:, 0:1] for c in range(NCORES)], axis=1)
    maxs = np.concatenate([L3a[c]["minmax_xn1"][:, 1:2] for c in range(NCORES)], axis=1)
    w1t = np.ascontiguousarray(np.asarray(inputs["L1_ffn_W1"]).T)
    b1_col = np.ascontiguousarray(np.asarray(inputs["L1_ffn_b1"]).reshape(32, 128).T)
    iota8 = np.arange(8, dtype=np.float32).reshape(1, 8)
    rb = A("router_b").reshape(1, 8)
    rw = np.ascontiguousarray(A("router_W"))
    in_maps = []
    for c in range(NCORES):
        in_maps.append(dict(
            xr2_bf=L4a[c]["xr2_bf"], xn1_fm=L3a[c]["xn1_fm"], mins=mins, maxs=maxs,
            router_w=rw, router_b=rb, w1t_ffn=w1t, b1_col=b1_col, iota8=iota8,
            ident=_IDENT, identb=identb))
    nc = build_L4b()
    res = run_bass_kernel_spmd(nc, in_maps, core_ids=list(range(NCORES)))
    L4b = res.results
    st["L4b"] = L4b
    if upto <= 6:
        return st

    cnts = np.concatenate([L4b[c]["cnt_row"] for c in range(NCORES)], axis=0)
    ltri8 = np.triu(np.ones((8, 8), np.float32), k=1)
    W1e = np.asarray(inputs["L1_attn_W1"]).astype(np.float32)
    W2e = np.asarray(inputs["L1_attn_W2"]).astype(np.float32)
    b1e = np.asarray(inputs["L1_attn_b1"]).astype(np.float32)
    b2e = np.asarray(inputs["L1_attn_b2"]).astype(np.float32)
    in_maps = []
    for c in range(NCORES):
        cmp_my = np.concatenate(
            [L4b[s]["cmp_part"][2048 * c:2048 * (c + 1)] for s in range(NCORES)], axis=0)
        in_maps.append(dict(
            cmp_my=np.ascontiguousarray(cmp_my), cnts=cnts,
            w1e=np.ascontiguousarray(W1e[c]), w2e=np.ascontiguousarray(W2e[c]),
            b1e_col=np.ascontiguousarray(b1e[c].reshape(32, 128).T),
            b2e_col=np.ascontiguousarray(b2e[c].reshape(8, 128).T),
            myexp_col=np.full((128, 1), float(c), np.float32),
            ltri8=ltri8, ident=_IDENT, identb=identb))
    nc = build_L5()
    res = run_bass_kernel_spmd(nc, in_maps, core_ids=list(range(NCORES)))
    L5 = res.results
    st["L5"] = L5
    if upto <= 7:
        return st

    mins2 = np.concatenate([L4b[c]["minmax_h2"][:, 0:1] for c in range(NCORES)], axis=1)
    maxs2 = np.concatenate([L4b[c]["minmax_h2"][:, 1:2] for c in range(NCORES)], axis=1)
    cmp_all = np.ascontiguousarray(
        np.concatenate([L5[e]["cmp_out"] for e in range(NCORES)], axis=0))
    w2f = np.ascontiguousarray(np.asarray(inputs["L1_ffn_W2"]).astype(np.float32).T)
    b2f_col = np.ascontiguousarray(
        np.asarray(inputs["L1_ffn_b2"]).astype(np.float32).reshape(8, 128).T)
    in_maps = []
    for c in range(NCORES):
        pm = np.zeros((8, 1), np.float32)
        pm[:c] = 1.0
        in_maps.append(dict(
            x1_fm=L3a[c]["x1_fm"], x2_fm=L4a[c]["x2_fm"], h2_bf=L4b[c]["h2_bf"],
            mins2=mins2, maxs2=maxs2, w2_ffn=w2f, b2f_col=b2f_col,
            cmp_all=cmp_all, cnts=cnts, premask_col=pm,
            idx1_tm=np.ascontiguousarray(L4b[c]["ridx1"].T),
            idx2_tm=np.ascontiguousarray(L4b[c]["ridx2"].T),
            rr1_tm=np.ascontiguousarray(L4b[c]["rr1"].reshape(16, 128).T),
            rr2_tm=np.ascontiguousarray(L4b[c]["rr2"].reshape(16, 128).T),
            g1_tm=np.ascontiguousarray(L4b[c]["rg1"].T),
            g2_tm=np.ascontiguousarray(L4b[c]["rg2"].T),
            ident=_IDENT))
    nc = build_L6()
    res = run_bass_kernel_spmd(nc, in_maps, core_ids=list(range(NCORES)))
    L6 = res.results
    st["L6"] = L6
    return st


def kernel(**inputs):
    st = run_pipeline(inputs, upto=99)
    out = np.zeros((2, 8192, 1024), np.float32)
    for c in range(NCORES):
        bb, qq = c // 4, c % 4
        out[bb, 2048 * qq:2048 * (qq + 1), :] = st["L6"][c]["out_fm"].T
    return out


# version 07
# speedup vs baseline: 1.4284x; 1.0268x over previous
import numpy as np
import ml_dtypes
from contextlib import ExitStack
import concourse.bass as bass
import concourse.mybir as mybir
from concourse.bass import IndirectOffsetOnAxis
from concourse.bass_utils import run_bass_kernel_spmd
from concourse.alu_op_type import AluOpType

F32 = mybir.dt.float32
BF16 = mybir.dt.bfloat16
U32 = mybir.dt.uint32
AF = mybir.ActivationFunctionType
OP = AluOpType

NCORES = 8
T = 2048
D = 1024
NCH = 8
RC = 1.5 * (2.0 ** 23)
BF = ml_dtypes.bfloat16


class Tok:
    __slots__ = ("eng", "cum", "dma", "key", "val")
    def __init__(self, eng, cum, dma, key=None, val=None):
        self.eng, self.cum, self.dma = eng, cum, dma
        self.key = key if key is not None else eng
        self.val = val if val is not None else cum

NDMASEM = 48

class TB:
    ENGS = ("tensor", "vector", "scalar", "gpsimd", "sync")

    def __init__(self, nc, ctx):
        self.nc, self.ctx = nc, ctx
        self.sems = {e: ctx.enter_context(nc.semaphore(f"S_{e}")) for e in self.ENGS}
        for i in range(NDMASEM):
            self.sems[f"dma{i}"] = ctx.enter_context(nc.semaphore(f"SD_{i}"))
        with nc.Block() as blk0:
            @blk0.gpsimd
            def _(g):
                for k in self.sems:
                    g.sem_clear(self.sems[k])
        self.cum = {e: 0 for e in self.ENGS}
        self.dmause = [0] * NDMASEM
        self.dmaidx = 0
        self.ops = {e: [] for e in self.ENGS}
        self.mem = {}
        self.width = {}

    def sbuf(self, name, shape, dt):
        t = self.ctx.enter_context(self.nc.sbuf_tensor(name, shape, dt))
        self.width[name] = shape[1] if len(shape) == 2 else int(np.prod(shape[1:]))
        return t

    def psum(self, name, shape, dt=F32):
        t = self.ctx.enter_context(self.nc.psum_tensor(name, shape, dt))
        self.width[name] = shape[1] if len(shape) == 2 else int(np.prod(shape[1:]))
        return t

    def dram(self, name, shape, dt, kind=None):
        t = (self.nc.dram_tensor(name, shape, dt, kind=kind) if kind
             else self.nc.dram_tensor(name, shape, dt))
        self.width[name] = shape[-1]
        return t

    def _region(self, ap):
        try:
            name = ap.name
            shp = ap.shape
            pb = ap.base_partition()
            if len(shp) == 2 and name in self.width:
                w = self.width[name]
                c0 = ap.offset - pb * w
                if 0 <= c0 and c0 + shp[1] <= w:
                    return (name, pb, shp[0], c0, shp[1], True)
            return (name, pb, shp[0], 0, 1 << 30, False)
        except Exception:
            return (getattr(ap, "name", "?"), 0, 1 << 30, 0, 1 << 30, False)

    @staticmethod
    def _olap(a, b):
        if a[0] != b[0]:
            return False
        if not (a[1] < b[1] + b[2] and b[1] < a[1] + a[2]):
            return False
        if a[5] and b[5]:
            return a[3] < b[3] + b[4] and b[3] < a[3] + a[4]
        return True

    def _deps_for(self, reads, writes):
        deps = []
        for ap in reads:
            r = self._region(ap)
            for rec in self.mem.get(r[0], ()):
                if rec[2] and self._olap(r, rec[0]):
                    deps.append((rec[1], False))
        for ap in writes:
            r = self._region(ap)
            for rec in self.mem.get(r[0], ()):
                if self._olap(r, rec[0]):
                    deps.append((rec[1], not rec[2]))
        return deps

    def _record(self, reads, writes, tok):
        for ap, isw in [(a, False) for a in reads] + [(a, True) for a in writes]:
            r = self._region(ap)
            lst = self.mem.setdefault(r[0], [])
            lst.append((r, tok, isw))
            if len(lst) > 240:
                old, keep = lst[:120], lst[120:]
                best = {}
                for rr, tk, _ in old:
                    if tk.key not in best or tk.val > best[tk.key].val:
                        best[tk.key] = tk
                name = r[0]
                lst[:] = [((name, 0, 1 << 30, 0, 1 << 30, False), tk, True)
                          for tk in best.values()] + keep

    def _needs(self, eng, alldeps, extra_need=None):
        need = dict(extra_need or {})
        for d, is_war in alldeps:
            if d.eng == eng and not d.dma:
                if is_war:
                    continue
                if eng == "tensor":
                    continue
            if d.key in need:
                need[d.key] = max(need[d.key], d.val)
            else:
                need[d.key] = d.val
        return need

    def op(self, eng, fn, reads=(), writes=(), deps=(), inc=1):
        alldeps = [(d, False) for d in deps if d is not None] + \
                  self._deps_for(reads, writes)
        self.cum[eng] += inc
        tok = Tok(eng, self.cum[eng], False)
        need = self._needs(eng, alldeps)
        self.ops[eng].append((fn, need, (eng, inc)))
        self._record(reads, writes, tok)
        return tok

    def _dmasem(self, queue):
        if queue == "gpsimd":
            self.gidx = getattr(self, "gidx", 0)
            j = self.gidx % 4
            self.gidx += 1
        else:
            j = 4 + self.dmaidx % (NDMASEM - 4)
            self.dmaidx += 1
        return j

    def dma(self, queue, out, in_, reads=(), writes=(), deps=(), **kw):
        if "single_packet" not in kw:
            try:
                nb = min(out.nbytes, in_.nbytes)
            except Exception:
                nb = 1 << 30
            if nb <= 12288:
                kw["single_packet"] = True
        alldeps = [(d, False) for d in deps if d is not None] + \
                  self._deps_for(list(reads) + [in_], list(writes) + [out])
        j = self._dmasem(queue)
        self.dmause[j] += 1
        key = f"dma{j}"
        tok = Tok(queue, 0, True, key, 16 * self.dmause[j])
        extra = {key: 16 * (self.dmause[j] - 1)} if self.dmause[j] > 1 else None
        need = self._needs(queue, alldeps, extra)
        sem = self.sems[key]
        self.ops[queue].append(
            (lambda e, out=out, in_=in_, kw=kw: e.dma_start(out=out, in_=in_, **kw),
             need, (key, 16)))
        self._record(list(reads) + [in_], list(writes) + [out], tok)
        return tok

    def idma(self, out, in_, offset_ap, side, bounds, reads=(), writes=(), deps=()):
        alldeps = [(d, False) for d in deps if d is not None] + \
                  self._deps_for(list(reads) + [in_, offset_ap], list(writes) + [out])
        j = self._dmasem("gpsimd")
        self.dmause[j] += 1
        key = f"dma{j}"
        tok = Tok("gpsimd", 0, True, key, 16 * self.dmause[j])
        extra = {key: 16 * (self.dmause[j] - 1)} if self.dmause[j] > 1 else None
        need = self._needs("gpsimd", alldeps, extra)
        ioff = IndirectOffsetOnAxis(ap=offset_ap, axis=0)
        if side == "out":
            fn = lambda e: e.indirect_dma_start(out=out, out_offset=ioff, in_=in_,
                                                in_offset=None, bounds_check=bounds,
                                                oob_is_err=False)
        else:
            fn = lambda e: e.indirect_dma_start(out=out, out_offset=None, in_=in_,
                                                in_offset=ioff, bounds_check=bounds,
                                                oob_is_err=False)
        self.ops["gpsimd"].append((fn, need, (key, 16)))
        self._record(list(reads) + [in_, offset_ap], list(writes) + [out], tok)
        return tok

    def emit(self):
        block = self.ctx.enter_context(self.nc.Block())
        for eng in self.ENGS:
            ops = self.ops[eng]
            if not ops:
                continue
            def make(eng, ops):
                def body(proxy):
                    water = {}
                    for fn, need, (inckey, inc) in ops:
                        for sk in sorted(need):
                            if need[sk] > water.get(sk, 0):
                                proxy.wait_ge(self.sems[sk], need[sk])
                                water[sk] = need[sk]
                        fn(proxy).then_inc(self.sems[inckey], inc)
                return body
            getattr(block, eng)(make(eng, list(ops)))

    def finalize(self):
        fin = {e: self.cum[e] for e in self.ENGS if self.cum[e] > 0 and e != "gpsimd"}
        for j in range(NDMASEM):
            if self.dmause[j]:
                fin[f"dma{j}"] = 16 * self.dmause[j]
        def fn(proxy):
            for k in sorted(fin):
                proxy.wait_ge(self.sems[k], fin[k])
            return proxy.nop()
        self.cum["gpsimd"] += 1
        self.ops["gpsimd"].append((fn, {}, ("gpsimd", 1)))
        self.emit()



def h_round_clip(b, out_ap, in_ap, inv_scale_col, lo, hi, extra_reads=()):
    b.op("vector", lambda v: v.tensor_scalar(
        out=out_ap, in0=in_ap, scalar1=inv_scale_col, scalar2=RC,
        op0=OP.mult, op1=OP.add),
        reads=[in_ap, inv_scale_col] + list(extra_reads), writes=[out_ap])
    b.op("vector", lambda v: v.tensor_scalar(
        out=out_ap, in0=out_ap, scalar1=-RC, scalar2=float(hi),
        op0=OP.add, op1=OP.min), reads=[out_ap], writes=[out_ap])
    return b.op("vector", lambda v: v.tensor_scalar(
        out=out_ap, in0=out_ap, scalar1=float(lo), scalar2=None,
        op0=OP.max), reads=[out_ap], writes=[out_ap])


def h_bcast_row(b, ps, dst, src_row, ones_row, n):
    for cs in range(0, n, 512):
        w = min(512, n - cs)
        b.op("tensor", lambda t, cs=cs, w=w: t.matmul(
            ps[:, 0:w], ones_row[:], src_row[0:1, cs:cs + w], start=True, stop=True),
            reads=[ones_row[:], src_row[0:1, cs:cs + w]], writes=[ps[:, 0:w]])
        b.op("vector", lambda v, cs=cs, w=w: v.tensor_copy(dst[:, cs:cs + w], ps[:, 0:w]),
             reads=[ps[:, 0:w]], writes=[dst[:, cs:cs + w]])



def build_L1():
    nc = bass.Bass("TRN2", target_bir_lowering=False)
    with ExitStack() as ctx:
        b = TB(nc, ctx)
        x_in = b.dram("x_fm", [D, T], F32, kind="ExternalInput").ap()
        w_in = b.dram("wqkd_fm", [D, 3 * D], F32, kind="ExternalInput").ap()
        ident_in = b.dram("ident", [128, 128], F32, kind="ExternalInput").ap()
        xn_out = b.dram("xn_fm", [D, T], F32, kind="ExternalOutput").ap()
        qkd_out = b.dram("qkd_bf", [3 * D, T], BF16, kind="ExternalOutput").ap()
        mx_out = b.dram("maxabs_xn", [1, 1], F32, kind="ExternalOutput").ap()

        x = [b.sbuf(f"x{i}", [128, T], F32) for i in range(NCH)]
        xnb = [b.sbuf(f"xnb{i}", [128, T], BF16) for i in range(NCH)]
        sq = b.sbuf("sq", [128, 512], F32)
        wcol = [b.sbuf(f"wcol{i}", [128, D], BF16) for i in range(2)]
        ones_col = b.sbuf("ones_col", [128, 1], F32)
        ones_row = b.sbuf("ones_row", [1, 128], F32)
        sx_row = b.sbuf("sx_row", [1, T], F32)
        sq_row = b.sbuf("sq_row", [1, T], F32)
        mubc = b.sbuf("mubc", [128, T], F32)
        rstdbc = b.sbuf("rstdbc", [128, T], F32)
        mxcol = b.sbuf("mxcol", [128, NCH], F32)
        mx1 = b.sbuf("mx1", [128, 1], F32)
        mxr = b.sbuf("mxr", [1, 128], F32)
        ident = b.sbuf("identsb", [128, 128], F32)
        otile = [b.sbuf(f"ot{i}", [128, T], BF16) for i in range(2)]
        ps_stat_t = b.psum("ps_stat_t", [128, 512], F32)
        ps_stat2_t = b.psum("ps_stat2_t", [128, 512], F32)
        ps_stat = ps_stat_t[0:1, :]
        ps_stat2 = ps_stat2_t[0:1, :]
        ps_bc = b.psum("ps_bc", [128, 512], F32)
        ps_tr = b.psum("ps_tr", [128, 512], F32)
        ps_mm = [b.psum(f"ps_mm{i}", [128, 512], F32) for i in range(2)]

        for i in range(NCH):
            b.dma("sync", x[i][:], x_in[128 * i:128 * (i + 1), :])
        b.dma("sync", ident[:], ident_in)
        b.op("vector", lambda v: v.memset(ones_col[:], 1.0), writes=[ones_col[:]])
        b.op("vector", lambda v: v.memset(ones_row[:], 1.0), writes=[ones_row[:]])

        for cs in range(4):
            sl = slice(512 * cs, 512 * (cs + 1))
            for i in range(NCH):
                b.op("tensor", lambda t, i=i, sl=sl: t.matmul(
                    ps_stat, ones_col[:], x[i][:, sl],
                    start=(i == 0), stop=(i == NCH - 1)),
                    reads=[x[i][:, sl], ones_col[:]], writes=[ps_stat])
            b.op("vector", lambda v, sl=sl: v.tensor_copy(sx_row[0:1, sl], ps_stat),
                 reads=[ps_stat], writes=[sx_row[0:1, sl]])
            for i in range(NCH):
                b.op("scalar", lambda s, i=i, sl=sl: s.activation(
                    out=sq[:, :], in_=x[i][:, sl], func=AF.Square),
                    reads=[x[i][:, sl]], writes=[sq[:, :]])
                b.op("tensor", lambda t, i=i: t.matmul(
                    ps_stat2, ones_col[:], sq[:, :],
                    start=(i == 0), stop=(i == NCH - 1)),
                    reads=[sq[:, :], ones_col[:]], writes=[ps_stat2])
            b.op("vector", lambda v, sl=sl: v.tensor_copy(sq_row[0:1, sl], ps_stat2),
                 reads=[ps_stat2], writes=[sq_row[0:1, sl]])
        b.op("vector", lambda v: v.tensor_scalar(out=sx_row[:], in0=sx_row[:],
             scalar1=1.0 / D, scalar2=None, op0=OP.mult),
             reads=[sx_row[:]], writes=[sx_row[:]])
        h_bcast_row(b, ps_bc, mubc, sx_row, ones_row, T)
        b.op("vector", lambda v: v.tensor_tensor(out=sx_row[:], in0=sx_row[:],
             in1=sx_row[:], op=OP.mult), reads=[sx_row[:]], writes=[sx_row[:]])
        b.op("vector", lambda v: v.tensor_scalar(out=sq_row[:], in0=sq_row[:],
             scalar1=1.0 / D, scalar2=None, op0=OP.mult),
             reads=[sq_row[:]], writes=[sq_row[:]])
        b.op("vector", lambda v: v.tensor_tensor(out=sq_row[:], in0=sq_row[:],
             in1=sx_row[:], op=OP.subtract), reads=[sq_row[:], sx_row[:]], writes=[sq_row[:]])
        b.op("vector", lambda v: v.tensor_scalar(out=sq_row[:], in0=sq_row[:],
             scalar1=1e-5, scalar2=None, op0=OP.add), reads=[sq_row[:]], writes=[sq_row[:]])
        b.op("scalar", lambda s: s.activation(out=sq_row[:], in_=sq_row[:],
             func=AF.Sqrt), reads=[sq_row[:]], writes=[sq_row[:]])
        b.op("vector", lambda v: v.reciprocal(out=sq_row[:], in_=sq_row[:]),
             reads=[sq_row[:]], writes=[sq_row[:]])
        h_bcast_row(b, ps_bc, rstdbc, sq_row, ones_row, T)

        for i in range(NCH):
            b.op("vector", lambda v, i=i: v.tensor_tensor(out=x[i][:], in0=x[i][:],
                 in1=mubc[:], op=OP.subtract), reads=[x[i][:], mubc[:]], writes=[x[i][:]])
            b.op("vector", lambda v, i=i: v.tensor_tensor(out=x[i][:], in0=x[i][:],
                 in1=rstdbc[:], op=OP.mult), reads=[x[i][:], rstdbc[:]], writes=[x[i][:]])
            b.op("vector", lambda v, i=i: v.reduce_max(out=mxcol[:, i:i + 1], in_=x[i][:],
                 axis=mybir.AxisListType.X, apply_absolute_value=True),
                 reads=[x[i][:]], writes=[mxcol[:, i:i + 1]])
            b.op("vector", lambda v, i=i: v.tensor_copy(xnb[i][:], x[i][:]),
                 reads=[x[i][:]], writes=[xnb[i][:]])
            b.dma("sync", xn_out[128 * i:128 * (i + 1), :], x[i][:])
        b.op("vector", lambda v: v.reduce_max(out=mx1[:], in_=mxcol[:],
             axis=mybir.AxisListType.X), reads=[mxcol[:]], writes=[mx1[:]])
        b.op("tensor", lambda t: t.transpose(ps_tr[0:1, 0:128], mx1[:], ident[:]),
             reads=[mx1[:], ident[:]], writes=[ps_tr[0:1, 0:128]])
        b.op("vector", lambda v: v.tensor_copy(mxr[:], ps_tr[0:1, 0:128]),
             reads=[ps_tr[0:1, 0:128]], writes=[mxr[:]])
        b.op("vector", lambda v: v.reduce_max(out=mxr[0:1, 0:1], in_=mxr[:],
             axis=mybir.AxisListType.X), reads=[mxr[:]], writes=[mxr[:]])
        b.dma("sync", mx_out, mxr[0:1, 0:1])

        for w in range(3):
            for ot in range(8):
                wc = wcol[(w * 8 + ot) % 2]
                col0 = D * w + 128 * ot
                src = w_in[:, col0:col0 + 128].rearrange("(c p) j -> p c j", p=128)
                b.dma("gpsimd", wc[:].rearrange("p (c j) -> p c j", c=NCH), src)
                obuf = otile[(w * 8 + ot) % 2]
                for cs in range(4):
                    ps = ps_mm[cs % 2]
                    sl = slice(512 * cs, 512 * (cs + 1))
                    for i in range(NCH):
                        b.op("tensor", lambda t, i=i, sl=sl, ps=ps, wc=wc: t.matmul(
                            ps[:, :], wc[:, 128 * i:128 * (i + 1)],
                            xnb[i][:, sl], start=(i == 0), stop=(i == NCH - 1)),
                            reads=[wc[:, 128 * i:128 * (i + 1)], xnb[i][:, sl]],
                            writes=[ps[:, :]])
                    b.op("scalar", lambda s, sl=sl, ps=ps, obuf=obuf: s.activation(
                        out=obuf[:, sl], in_=ps[:, :], func=AF.Copy),
                        reads=[ps[:, :]], writes=[obuf[:, sl]])
                b.dma("sync", qkd_out[col0:col0 + 128, :], obuf[:])
        b.finalize()
    return nc



S_FULL = 8192

import os
L2_PARTS = os.environ.get("L2_PARTS", "AB")

def build_L2():
    nc = bass.Bass("TRN2", target_bir_lowering=False)
    with ExitStack() as ctx:
        b = TB(nc, ctx)
        qh_in = b.dram("qh_bf", [256, S_FULL], BF16, kind="ExternalInput").ap()
        kh_in = b.dram("kh_bf", [256, S_FULL], BF16, kind="ExternalInput").ap()
        xn_in = b.dram("xn_fm", [D, T], F32, kind="ExternalInput").ap()
        mx_in = b.dram("maxxn_all", [1, NCORES], F32, kind="ExternalInput").ap()
        clip_in = b.dram("clip_x", [1, 1], F32, kind="ExternalInput").ap()
        wg_in = b.dram("wg_t", [D, 2816], F32, kind="ExternalInput").ap()
        wu_in = b.dram("wu_t", [D, 2816], F32, kind="ExternalInput").ap()
        ident_in = b.dram("ident", [128, 128], F32, kind="ExternalInput").ap()
        diag_out = b.dram("attn_diag", [512, 64], F32, kind="ExternalOutput").ap()
        h_out = b.dram("h_fm", [2816, T], F32, kind="ExternalOutput").ap()
        mxh_out = b.dram("maxh", [1, 1], F32, kind="ExternalOutput").ap()
        wq_dram = b.dram("wq_scratch", [2 * D, 2816], BF16).ap()

        qsc = b.sbuf("qsc", [64, S_FULL], BF16)
        ksc = b.sbuf("ksc", [64, S_FULL], BF16)
        xq = [b.sbuf(f"xq{i}", [128, T], BF16) for i in range(NCH)]
        xbuf = [b.sbuf(f"xb{i}", [128, T], F32) for i in range(2)]
        wstr = [b.sbuf(f"wstr{i}", [128, 2816], F32) for i in range(2)]
        wacc = b.sbuf("wacc", [128, 2816], F32)
        rbc = b.sbuf("rbc", [128, 2816], F32)
        wqtile = [b.sbuf(f"wqt{i}", [128, 2816], BF16) for i in range(2)]
        wcol = [b.sbuf(f"wcol{i}", [128, D], BF16) for i in range(2)]
        ident = b.sbuf("identsb", [128, 128], F32)
        ones_col = b.sbuf("ones_col", [128, 1], F32)
        ones_row = b.sbuf("ones_row", [1, 128], F32)
        rrow = b.sbuf("rrow", [1, 2816], F32)
        scale_g = b.sbuf("scale_g", [128, 22], F32)
        scale_u = b.sbuf("scale_u", [128, 22], F32)
        comb_g = b.sbuf("comb_g", [128, 22], F32)
        comb_u = b.sbuf("comb_u", [128, 22], F32)
        r22 = b.sbuf("r22", [22, 128], F32)
        sc1 = b.sbuf("sc1", [1, NCORES], F32)
        sxcol = b.sbuf("sxcol", [128, 1], F32)
        invsx = b.sbuf("invsx", [128, 1], F32)
        gsil = b.sbuf("gsil", [128, T], F32)
        htile = b.sbuf("htile", [128, T], F32)
        mxh_cols = b.sbuf("mxh_cols", [128, 22], F32)
        mx1 = b.sbuf("mx1", [128, 1], F32)
        mxr = b.sbuf("mxr", [1, 128], F32)
        dscr = b.sbuf("dscr", [64, 128], F32)
        escr = b.sbuf("escr", [128, 128], F32)
        escr2 = b.sbuf("escr2", [128, 128], F32)
        eth = [b.sbuf(f"eth{i}", [128, 512], F32) for i in range(2)]
        accb = b.sbuf("accb", [128, 18], F32)
        dcol = b.sbuf("dcol", [128, 1], F32)
        dent = b.sbuf("dent", [128, 1], F32)
        dat = b.sbuf("dat", [128, 64], F32)
        scr_dram = b.dram("scr22", [22, 128], F32).ap()

        ps_mm = [b.psum(f"ps_mm{i}", [128, 512], F32) for i in range(2)]
        ps_bc = b.psum("ps_bc", [128, 512], F32)
        ps_tr = b.psum("ps_tr", [128, 512], F32)
        ps_sc = [b.psum(f"ps_sc{i}", [128, 512], F32) for i in range(2)]
        ps_dg = b.psum("ps_dg", [128, 512], F32)

        b.dma("sync", ident[:], ident_in)
        b.op("vector", lambda v: v.memset(ones_col[:], 1.0), writes=[ones_col[:]])
        b.op("vector", lambda v: v.memset(ones_row[:], 1.0), writes=[ones_row[:]])


        gmax1 = b.sbuf("gmax1", [1, 1], F32)
        snap = b.sbuf("snap", [1, NCORES], F32)
        b.dma("sync", sc1[0:1, 0:NCORES], mx_in)
        b.op("vector", lambda v: v.tensor_copy(snap[:], sc1[0:1, 0:NCORES]),
             reads=[sc1[:]], writes=[snap[:]])
        b.op("vector", lambda v: v.reduce_max(out=gmax1[:], in_=sc1[0:1, 0:NCORES],
             axis=mybir.AxisListType.X), reads=[sc1[:]], writes=[gmax1[:]])
        cl = b.sbuf("cl", [1, 1], F32)
        b.dma("sync", cl[:], clip_in)
        b.op("scalar", lambda s: s.activation(out=cl[:], in_=cl[:], func=AF.Sigmoid),
             reads=[cl[:]], writes=[cl[:]])
        b.op("vector", lambda v: v.tensor_tensor(out=cl[:], in0=cl[:], in1=gmax1[:],
             op=OP.mult), reads=[cl[:], gmax1[:]], writes=[cl[:]])
        b.op("vector", lambda v: v.tensor_scalar(out=cl[:], in0=cl[:], scalar1=1.0 / 127.0,
             scalar2=None, op0=OP.mult), reads=[cl[:]], writes=[cl[:]])
        b.op("tensor", lambda t: t.matmul(ps_bc[:, 0:1], ones_row[:], cl[:],
             start=True, stop=True), reads=[ones_row[:], cl[:]], writes=[ps_bc[:, 0:1]])
        b.op("vector", lambda v: v.tensor_copy(sxcol[:], ps_bc[:, 0:1]),
             reads=[ps_bc[:, 0:1]], writes=[sxcol[:]])
        b.op("vector", lambda v: v.reciprocal(out=invsx[:], in_=sxcol[:]),
             reads=[sxcol[:]], writes=[invsx[:]])

        for i in (range(NCH) if "A" in L2_PARTS else range(0)):
            xb = xbuf[i % 2]
            b.dma("sync", xb[:], xn_in[128 * i:128 * (i + 1), :])
            b.op("vector", lambda v, xb=xb: v.tensor_scalar(
                out=xb[:], in0=xb[:], scalar1=invsx[:], scalar2=RC,
                op0=OP.mult, op1=OP.add), reads=[xb[:], invsx[:]], writes=[xb[:]])
            b.op("vector", lambda v, xb=xb: v.tensor_scalar(
                out=xb[:], in0=xb[:], scalar1=-RC, scalar2=127.0,
                op0=OP.add, op1=OP.min), reads=[xb[:]], writes=[xb[:]])
            b.op("vector", lambda v, xb=xb, i=i: v.tensor_scalar(
                out=xq[i][:], in0=xb[:], scalar1=-128.0, scalar2=None,
                op0=OP.max), reads=[xb[:]], writes=[xq[i][:]])

        def wquant(w_in_ap, scale_cols, wq_row0):
            for i in range(NCH):
                ws = wstr[i % 2]
                b.dma("sync", ws[:], w_in_ap[128 * i:128 * (i + 1), :])
                b.op("scalar", lambda s, ws=ws: s.activation(out=ws[:], in_=ws[:],
                     func=AF.Abs), reads=[ws[:]], writes=[ws[:]])
                if i == 0:
                    b.op("vector", lambda v, ws=ws: v.tensor_copy(wacc[:], ws[:]),
                         reads=[ws[:]], writes=[wacc[:]])
                else:
                    b.op("vector", lambda v, ws=ws: v.tensor_tensor(out=wacc[:], in0=wacc[:],
                         in1=ws[:], op=OP.max), reads=[wacc[:], ws[:]], writes=[wacc[:]])
            for blk in range(22):
                b.op("tensor", lambda t, blk=blk: t.transpose(
                    ps_tr[:, 0:128], wacc[:, 128 * blk:128 * (blk + 1)], ident[:]),
                    reads=[wacc[:], ident[:]], writes=[ps_tr[:, 0:128]])
                b.op("vector", lambda v, blk=blk: v.reduce_max(
                    out=scale_cols[:, blk:blk + 1], in_=ps_tr[:, 0:128],
                    axis=mybir.AxisListType.X),
                    reads=[ps_tr[:, 0:128]], writes=[scale_cols[:, blk:blk + 1]])
            rcols = b.sbuf(f"rcols{wq_row0}", [128, 22], F32)
            b.op("vector", lambda v, rcols=rcols: v.reciprocal(out=rcols[:], in_=scale_cols[:]),
                 reads=[scale_cols[:]], writes=[rcols[:]])
            b.op("vector", lambda v, rcols=rcols: v.tensor_scalar(out=rcols[:], in0=rcols[:],
                 scalar1=127.0, scalar2=None, op0=OP.mult), reads=[rcols[:]], writes=[rcols[:]])
            b.op("tensor", lambda t, rcols=rcols: t.transpose(ps_tr[0:22, 0:128], rcols[:], ident[:]),
                 reads=[rcols[:], ident[:]], writes=[ps_tr[0:22, 0:128]])
            b.op("vector", lambda v: v.tensor_copy(r22[:], ps_tr[0:22, 0:128]),
                 reads=[ps_tr[:, :]], writes=[r22[:]])
            b.dma("sync", scr_dram, r22[:])
            b.dma("sync", rrow[:], scr_dram.rearrange("a b -> (a b)").unsqueeze(0))
            h_bcast_row(b, ps_bc, rbc, rrow, ones_row, 2816)
            for i in range(NCH):
                ws = wstr[i % 2]
                wqt = wqtile[i % 2]
                b.dma("sync", ws[:], w_in_ap[128 * i:128 * (i + 1), :])
                b.op("vector", lambda v, ws=ws: v.tensor_tensor(out=ws[:], in0=ws[:],
                     in1=rbc[:], op=OP.mult), reads=[ws[:], rbc[:]], writes=[ws[:]])
                b.op("vector", lambda v, ws=ws: v.tensor_scalar(out=ws[:], in0=ws[:],
                     scalar1=RC, scalar2=-RC, op0=OP.add, op1=OP.add),
                     reads=[ws[:]], writes=[ws[:]])
                b.op("vector", lambda v, ws=ws, wqt=wqt: v.tensor_scalar(out=wqt[:], in0=ws[:],
                     scalar1=127.0, scalar2=-127.0, op0=OP.min, op1=OP.max),
                     reads=[ws[:]], writes=[wqt[:]])
                b.dma("sync", wq_dram[wq_row0 + 128 * i: wq_row0 + 128 * (i + 1), :], wqt[:])
        if "A" in L2_PARTS:
            wquant(wg_in, scale_g, 0)
            wquant(wu_in, scale_u, D)
        for sc_, cb_ in ([(scale_g, comb_g), (scale_u, comb_u)] if "A" in L2_PARTS else []):
            b.op("vector", lambda v, sc_=sc_, cb_=cb_: v.tensor_scalar(
                out=cb_[:], in0=sc_[:], scalar1=sxcol[:], scalar2=1.0 / 127.0,
                op0=OP.mult, op1=OP.mult), reads=[sc_[:], sxcol[:]], writes=[cb_[:]])

        for it in (range(22) if "A" in L2_PARTS else range(0)):
            for w in range(2):
                wc = wcol[w]
                col0 = 128 * it
                src = wq_dram[D * w:D * w + D, col0:col0 + 128].rearrange(
                    "(c p) j -> p c j", p=128)
                b.dma("sync", wc[:].rearrange("p (c j) -> p c j", c=NCH), src)
                for cs in range(4):
                    ps = ps_mm[cs % 2]
                    sl = slice(512 * cs, 512 * (cs + 1))
                    for i in range(NCH):
                        b.op("tensor", lambda t, i=i, sl=sl, ps=ps, wc=wc: t.matmul(
                            ps[:, :], wc[:, 128 * i:128 * (i + 1)], xq[i][:, sl],
                            start=(i == 0), stop=(i == NCH - 1)),
                            reads=[wc[:, 128 * i:128 * (i + 1)], xq[i][:, sl]],
                            writes=[ps[:, :]])
                    if w == 0:
                        b.op("scalar", lambda s, sl=sl, ps=ps, it=it: s.activation(
                            out=gsil[:, sl], in_=ps[:, :], func=AF.Silu,
                            scale=comb_g[:, it:it + 1]),
                            reads=[ps[:, :], comb_g[:, it:it + 1]], writes=[gsil[:, sl]])
                    else:
                        b.op("vector", lambda v, sl=sl, ps=ps, it=it: v.scalar_tensor_tensor(
                            out=htile[:, sl], in0=ps[:, :], scalar=comb_u[:, it:it + 1],
                            in1=gsil[:, sl], op0=OP.mult, op1=OP.mult),
                            reads=[ps[:, :], comb_u[:, it:it + 1], gsil[:, sl]],
                            writes=[htile[:, sl]])
            b.op("vector", lambda v, it=it: v.reduce_max(out=mxh_cols[:, it:it + 1],
                 in_=htile[:], axis=mybir.AxisListType.X, apply_absolute_value=True),
                 reads=[htile[:]], writes=[mxh_cols[:, it:it + 1]])
            b.dma("sync", h_out[128 * it:128 * (it + 1), :], htile[:])
        if "A" not in L2_PARTS:
            b.op("vector", lambda v: v.memset(mxh_cols[:], 0.0), writes=[mxh_cols[:]])
            b.op("vector", lambda v: v.memset(dat[:], 0.0), writes=[dat[:]])
            for u in range(4):
                b.dma("sync", diag_out[128 * u:128 * (u + 1), :], dat[:])
            for it in range(22):
                b.op("vector", lambda v: v.memset(htile[:], 0.0), writes=[htile[:]])
                b.dma("sync", h_out[128 * it:128 * (it + 1), :], htile[:])
        b.op("vector", lambda v: v.reduce_max(out=mx1[:], in_=mxh_cols[:],
             axis=mybir.AxisListType.X), reads=[mxh_cols[:]], writes=[mx1[:]])
        b.op("tensor", lambda t: t.transpose(ps_tr[0:1, 0:128], mx1[:], ident[:]),
             reads=[mx1[:], ident[:]], writes=[ps_tr[0:1, 0:128]])
        b.op("vector", lambda v: v.tensor_copy(mxr[:], ps_tr[0:1, 0:128]),
             reads=[ps_tr[:, :]], writes=[mxr[:]])
        b.op("vector", lambda v: v.reduce_max(out=mxr[0:1, 0:1], in_=mxr[:],
             axis=mybir.AxisListType.X), reads=[mxr[:]], writes=[mxr[:]])
        b.dma("sync", mxh_out, mxr[0:1, 0:1])

        NU = int(os.environ.get("L2B_UNITS", "4"))
        NR = int(os.environ.get("L2B_ROWS", "64"))
        SKIPREG = os.environ.get("L2B_SKIPREG", "") == "1"
        SKIPDIAG = os.environ.get("L2B_SKIPDIAG", "") == "1"
        tri_in = b.dram("trimask", [128, 128], F32, kind="ExternalInput").ap()
        trimask = b.sbuf("trimask_sb", [128, 128], F32)
        b.dma("sync", trimask[:], tri_in)
        acc_all = b.sbuf("acc_all", [128, 17 * 64], F32)
        dcols = b.sbuf("dcols", [128, 64], F32)
        dent_all = b.sbuf("dent_all", [128, 64], F32)
        escrE = [escr, escr2]
        esum = [b.sbuf(f"esum{i}", [128, 128], F32) for i in range(2)]
        eth4 = [b.sbuf(f"ethb{i}", [128, 512], BF16) for i in range(4)]
        ps_dg2 = [ps_dg, ps_bc]
        ps_sc4 = ps_sc + ps_mm
        for u in (range(NU) if "B" in L2_PARTS else range(0)):
            b.dma("sync", qsc[:], qh_in[64 * u:64 * (u + 1), :])
            b.dma("sync", ksc[:], kh_in[64 * u:64 * (u + 1), :])
            qt, kt = qsc, ksc
            pb = 0
            for r in range(NR):
                par = r % 2
                rs = slice(128 * r, 128 * (r + 1))
                ksl = kt[pb:pb + 64, rs]
                pd = ps_dg2[par]
                a0 = 17 * r
                b.op("tensor", lambda t, ksl=ksl, qt=qt, pb=pb, rs=rs, pd=pd: t.matmul(
                    pd[:, 0:128], ksl, qt[pb:pb + 64, rs], start=True, stop=True),
                    reads=[ksl, qt[pb:pb + 64, rs]], writes=[pd[:, 0:128]])
                b.op("vector", lambda v, pd=pd, par=par: v.tensor_tensor(
                    out=esum[par][:], in0=pd[:, 0:128], in1=trimask[:], op=OP.add),
                    reads=[pd[:, 0:128], trimask[:]], writes=[esum[par][:]])
                b.op("scalar", lambda s, par=par, a0=a0: s.activation(
                    out=escrE[par][:], in_=esum[par][:], func=AF.Exp, scale=0.125,
                    accum_out=acc_all[:, a0:a0 + 1]),
                    reads=[esum[par][:]], writes=[escrE[par][:], acc_all[:, a0:a0 + 1]])
                b.op("vector", lambda v, par=par: v.tensor_tensor(
                    out=esum[par][:], in0=escrE[par][:], in1=ident[:], op=OP.mult),
                    reads=[escrE[par][:], ident[:]], writes=[esum[par][:]])
                b.op("vector", lambda v, par=par, r=r: v.reduce_sum(
                    out=dcols[:, r:r + 1], in_=esum[par][:], axis=mybir.AxisListType.X),
                    reads=[esum[par][:]], writes=[dcols[:, r:r + 1]])
                c0 = 128 * (r + 1)
                ntile = 0
                while c0 < S_FULL and not SKIPREG:
                    cw = min(512, S_FULL - c0)
                    ps = ps_sc4[ntile % 4]
                    et = eth4[ntile % 4]
                    b.op("tensor", lambda t, ksl=ksl, qt=qt, pb=pb, c0=c0, cw=cw, ps=ps:
                         t.matmul(ps[:, 0:cw], ksl, qt[pb:pb + 64, c0:c0 + cw],
                                  start=True, stop=True),
                         reads=[ksl, qt[pb:pb + 64, c0:c0 + cw]], writes=[ps[:, 0:cw]])
                    b.op("scalar", lambda s, ps=ps, cw=cw, et=et: s.activation(
                         out=et[:, 0:cw], in_=ps[:, 0:cw], func=AF.Exp, scale=0.125),
                         reads=[ps[:, 0:cw]], writes=[et[:, 0:cw]])
                    b.op("vector", lambda v, et=et, cw=cw, nt=ntile, a0=a0: v.reduce_sum(
                         out=acc_all[:, a0 + 1 + nt:a0 + 2 + nt], in_=et[:, 0:cw],
                         axis=mybir.AxisListType.X),
                         reads=[et[:, 0:cw]],
                         writes=[acc_all[:, a0 + 1 + ntile:a0 + 2 + ntile]])
                    c0 += cw
                    ntile += 1
            for r in range(NR):
                nt = max(0, (S_FULL - 128 * (r + 1) + 511) // 512)
                a0 = 17 * r
                b.op("vector", lambda v, r=r, a0=a0, nt=nt: v.reduce_sum(
                     out=dent_all[:, r:r + 1], in_=acc_all[:, a0:a0 + nt + 1],
                     axis=mybir.AxisListType.X),
                     reads=[acc_all[:, a0:a0 + nt + 1]], writes=[dent_all[:, r:r + 1]])
            b.op("vector", lambda v: v.reciprocal(out=dent_all[:], in_=dent_all[:]),
                 reads=[dent_all[:]], writes=[dent_all[:]])
            b.op("vector", lambda v: v.tensor_tensor(out=dat[:], in0=dcols[:],
                 in1=dent_all[:], op=OP.mult),
                 reads=[dcols[:], dent_all[:]], writes=[dat[:]])
            b.dma("sync", diag_out[128 * u:128 * (u + 1), :], dat[:])
        b.finalize()
    return nc



def build_L3a():
    nc = bass.Bass("TRN2", target_bir_lowering=False)
    with ExitStack() as ctx:
        b = TB(nc, ctx)
        x_in = b.dram("x_fm", [D, T], F32, kind="ExternalInput").ap()
        xn_in = b.dram("xn_fm", [D, T], F32, kind="ExternalInput").ap()
        h_in = b.dram("h_fm", [2816, T], F32, kind="ExternalInput").ap()
        mxh_in = b.dram("maxh_all", [1, NCORES], F32, kind="ExternalInput").ap()
        clip_in = b.dram("clip_down", [1, 1], F32, kind="ExternalInput").ap()
        diag_in = b.dram("diag16", [16, T], F32, kind="ExternalInput").ap()
        dvd_in = b.dram("dvdown_bf", [D, T], BF16, kind="ExternalInput").ap()
        wd_in = b.dram("wd_t", [2816, D], F32, kind="ExternalInput").ap()
        wvu_in = b.dram("wvu_fm", [D, D], F32, kind="ExternalInput").ap()
        m0_in = b.dram("mask0", [1, 128], F32, kind="ExternalInput").ap()
        m1_in = b.dram("mask1", [1, 128], F32, kind="ExternalInput").ap()
        ident_in = b.dram("ident", [128, 128], F32, kind="ExternalInput").ap()
        x1_out = b.dram("x1_fm", [D, T], F32, kind="ExternalOutput").ap()
        xn1_out = b.dram("xn1_fm", [D, T], F32, kind="ExternalOutput").ap()
        xr1_out = b.dram("xr1_bf", [D, T], BF16, kind="ExternalOutput").ap()
        mm_out = b.dram("minmax_xn1", [1, 2], F32, kind="ExternalOutput").ap()
        wdq_dram = b.dram("wdq_scr", [2816, D], BF16).ap()
        scr_dram = b.dram("scr8", [8, 128], F32).ap()

        st1 = [b.sbuf(f"st1_{i}", [128, T], F32) for i in range(2)]
        st2 = [b.sbuf(f"st2_{i}", [128, 512], F32) for i in range(2)]
        hqsb = [b.sbuf(f"hqsb{i}", [128, T], BF16) for i in range(22)]
        wvusb = [b.sbuf(f"wvusb{i}", [128, D], BF16) for i in range(8)]
        wstr = [b.sbuf("wstr0", [128, D], F32)] * 2
        wacc = b.sbuf("wacc", [128, D], F32)
        rbc = wacc
        wqt = [b.sbuf(f"wqt{i}", [128, D], BF16) for i in range(2)]
        wcol = [b.sbuf(f"wcol{i}", [128, 2816], BF16) for i in range(2)]
        vd = b.sbuf("vd", [128, 8 * 512], BF16)
        g0 = b.sbuf("g0", [128, 8 * 512], BF16)
        s0b = [b.sbuf(f"s0b{i}", [128, 512], F32) for i in range(2)]
        diagrow = b.sbuf("diagrow", [1, 1024], BF16)
        ident = b.sbuf("identsb", [128, 128], F32)
        ones_col = b.sbuf("ones_col", [128, 1], F32)
        ones_row = b.sbuf("ones_row", [1, 128], F32)
        m0 = b.sbuf("m0", [1, 128], BF16)
        m1 = b.sbuf("m1", [1, 128], BF16)
        cl = b.sbuf("cl", [1, 1], F32)
        sc1 = b.sbuf("sc1", [1, NCORES], F32)
        gmax1 = b.sbuf("gmax1", [1, 1], F32)
        invsd = b.sbuf("invsd", [128, 1], F32)
        sdcol = b.sbuf("sdcol", [128, 1], F32)
        scale_w = b.sbuf("scale_w", [128, 8], F32)
        comb_w = b.sbuf("comb_w", [128, 8], F32)
        r8 = b.sbuf("r8", [8, 128], F32)
        rrow = b.sbuf("rrow", [1, D], F32)
        sx_row = b.sbuf("sx_row", [1, T], F32)
        sq_row = b.sbuf("sq_row", [1, T], F32)
        mubc = b.sbuf("mubc", [128, T], F32)
        rstdbc = b.sbuf("rstdbc", [128, T], F32)
        sqs = st2[0]
        mncol = b.sbuf("mncol", [128, NCH], F32)
        mxcol = b.sbuf("mxcol", [128, NCH], F32)
        mm2 = b.sbuf("mm2", [1, 128], F32)
        mmo = b.sbuf("mmo", [1, 2], F32)
        nsc = scale_w
        ns_in = b.dram("norm1_scale_col", [128, 8], F32, kind="ExternalInput").ap()

        PS = [b.psum(f"PS{i}", [128, 512], F32) for i in range(8)]

        b.dma("sync", ident[:], ident_in)
        b.dma("gpsimd", m0[:], m0_in)
        b.dma("gpsimd", m1[:], m1_in)
        b.op("vector", lambda v: v.memset(ones_col[:], 1.0), writes=[ones_col[:]])
        b.op("vector", lambda v: v.memset(ones_row[:], 1.0), writes=[ones_row[:]])

        b.dma("sync", sc1[:], mxh_in)
        b.dma("sync", cl[:], clip_in)
        b.op("vector", lambda v: v.reduce_max(out=gmax1[:], in_=sc1[:],
             axis=mybir.AxisListType.X), reads=[sc1[:]], writes=[gmax1[:]])
        b.op("scalar", lambda s: s.activation(out=cl[:], in_=cl[:], func=AF.Sigmoid),
             reads=[cl[:]], writes=[cl[:]])
        b.op("vector", lambda v: v.tensor_tensor(out=cl[:], in0=cl[:], in1=gmax1[:],
             op=OP.mult), reads=[cl[:], gmax1[:]], writes=[cl[:]])
        b.op("vector", lambda v: v.tensor_scalar(out=cl[:], in0=cl[:], scalar1=1.0 / 127.0,
             scalar2=None, op0=OP.mult), reads=[cl[:]], writes=[cl[:]])
        b.op("tensor", lambda t: t.matmul(PS[0][:, 0:1], ones_row[:], cl[:],
             start=True, stop=True), reads=[ones_row[:], cl[:]], writes=[PS[0][:, 0:1]])
        b.op("vector", lambda v: v.tensor_copy(sdcol[:], PS[0][:, 0:1]),
             reads=[PS[0][:, 0:1]], writes=[sdcol[:]])
        b.op("vector", lambda v: v.reciprocal(out=invsd[:], in_=sdcol[:]),
             reads=[sdcol[:]], writes=[invsd[:]])

        for i in range(22):
            sb = st1[i % 2]
            b.dma("sync", sb[:], h_in[128 * i:128 * (i + 1), :])
            b.op("vector", lambda v, sb=sb: v.tensor_scalar(
                out=sb[:], in0=sb[:], scalar1=invsd[:], scalar2=RC,
                op0=OP.mult, op1=OP.add), reads=[sb[:], invsd[:]], writes=[sb[:]])
            b.op("vector", lambda v, sb=sb: v.tensor_scalar(
                out=sb[:], in0=sb[:], scalar1=-RC, scalar2=127.0,
                op0=OP.add, op1=OP.min), reads=[sb[:]], writes=[sb[:]])
            b.op("vector", lambda v, sb=sb, i=i: v.tensor_scalar(
                out=hqsb[i][:], in0=sb[:], scalar1=-128.0, scalar2=None,
                op0=OP.max), reads=[sb[:]], writes=[hqsb[i][:]])
        for rt in range(8):
            b.dma("gpsimd", wvusb[rt][:], wvu_in[128 * rt:128 * (rt + 1), :])

        for i in range(22):
            ws = wstr[i % 2]
            b.dma("sync", ws[:], wd_in[128 * i:128 * (i + 1), :])
            b.op("scalar", lambda s, ws=ws: s.activation(out=ws[:], in_=ws[:],
                 func=AF.Abs), reads=[ws[:]], writes=[ws[:]])
            if i == 0:
                b.op("vector", lambda v, ws=ws: v.tensor_copy(wacc[:], ws[:]),
                     reads=[ws[:]], writes=[wacc[:]])
            else:
                b.op("vector", lambda v, ws=ws: v.tensor_tensor(out=wacc[:], in0=wacc[:],
                     in1=ws[:], op=OP.max), reads=[wacc[:], ws[:]], writes=[wacc[:]])
        for blk in range(8):
            b.op("tensor", lambda t, blk=blk: t.transpose(
                PS[1][:, 0:128], wacc[:, 128 * blk:128 * (blk + 1)], ident[:]),
                reads=[wacc[:], ident[:]], writes=[PS[1][:, 0:128]])
            b.op("vector", lambda v, blk=blk: v.reduce_max(
                out=scale_w[:, blk:blk + 1], in_=PS[1][:, 0:128],
                axis=mybir.AxisListType.X),
                reads=[PS[1][:, 0:128]], writes=[scale_w[:, blk:blk + 1]])
        rcols = comb_w
        b.op("vector", lambda v: v.reciprocal(out=rcols[:], in_=scale_w[:]),
             reads=[scale_w[:]], writes=[rcols[:]])
        b.op("vector", lambda v: v.tensor_scalar(out=rcols[:], in0=rcols[:],
             scalar1=127.0, scalar2=None, op0=OP.mult), reads=[rcols[:]], writes=[rcols[:]])
        b.op("tensor", lambda t: t.transpose(PS[1][0:8, 0:128], rcols[:], ident[:]),
             reads=[rcols[:], ident[:]], writes=[PS[1][0:8, 0:128]])
        b.op("vector", lambda v: v.tensor_copy(r8[:], PS[1][0:8, 0:128]),
             reads=[PS[1][0:8, 0:128]], writes=[r8[:]])
        b.dma("sync", scr_dram, r8[:])
        b.dma("sync", rrow[:], scr_dram.rearrange("a b -> (a b)").unsqueeze(0))
        h_bcast_row(b, PS[2], rbc, rrow, ones_row, D)
        for i in range(22):
            ws = wstr[i % 2]
            wq_ = wqt[i % 2]
            b.dma("sync", ws[:], wd_in[128 * i:128 * (i + 1), :])
            b.op("vector", lambda v, ws=ws: v.tensor_tensor(out=ws[:], in0=ws[:],
                 in1=rbc[:], op=OP.mult), reads=[ws[:], rbc[:]], writes=[ws[:]])
            b.op("vector", lambda v, ws=ws: v.tensor_scalar(out=ws[:], in0=ws[:],
                 scalar1=RC, scalar2=-RC, op0=OP.add, op1=OP.add),
                 reads=[ws[:]], writes=[ws[:]])
            b.op("vector", lambda v, ws=ws, wq_=wq_: v.tensor_scalar(out=wq_[:], in0=ws[:],
                 scalar1=127.0, scalar2=-127.0, op0=OP.min, op1=OP.max),
                 reads=[ws[:]], writes=[wq_[:]])
            b.dma("sync", wdq_dram[128 * i:128 * (i + 1), :], wq_[:])
        b.op("vector", lambda v: v.tensor_scalar(
            out=comb_w[:], in0=scale_w[:], scalar1=sdcol[:], scalar2=1.0 / 127.0,
            op0=OP.mult, op1=OP.mult), reads=[scale_w[:], sdcol[:]], writes=[comb_w[:]])

        for cs in range(4):
            tsl = slice(512 * cs, 512 * (cs + 1))
            for ot in range(8):
                wc = wcol[ot % 2]
                src = wdq_dram[:, 128 * ot:128 * (ot + 1)].rearrange(
                    "(c p) j -> p c j", p=128)
                b.dma("sync", wc[:].rearrange("p (c j) -> p c j", c=22), src)
                ps = PS[ot % 2]
                for i in range(22):
                    b.op("tensor", lambda t, wc=wc, i=i, ps=ps, tsl=tsl: t.matmul(
                        ps[:, :], wc[:, 128 * i:128 * (i + 1)], hqsb[i][:, tsl],
                        start=(i == 0), stop=(i == 21)),
                        reads=[wc[:, 128 * i:128 * (i + 1)], hqsb[i][:, tsl]],
                        writes=[ps[:, :]])
                b.op("scalar", lambda s, ps=ps, ot=ot: s.activation(
                    out=g0[:, 512 * ot:512 * (ot + 1)], in_=ps[:, :], func=AF.Silu,
                    scale=comb_w[:, ot:ot + 1]),
                    reads=[ps[:, :], comb_w[:, ot:ot + 1]],
                    writes=[g0[:, 512 * ot:512 * (ot + 1)]])
            for rt in range(8):
                b.dma("gpsimd", diagrow[0:1, 0:512], diag_in[2 * rt:2 * rt + 1, tsl])
                b.dma("gpsimd", diagrow[0:1, 512:1024], diag_in[2 * rt + 1:2 * rt + 2, tsl])
                ps = PS[2]
                b.op("tensor", lambda t, ps=ps: t.matmul(ps[:, :], m0[:], diagrow[0:1, 0:512],
                     start=True, stop=False), reads=[m0[:], diagrow[:]], writes=[ps[:, :]])
                b.op("tensor", lambda t, ps=ps: t.matmul(ps[:, :], m1[:], diagrow[0:1, 512:1024],
                     start=False, stop=True), reads=[m1[:], diagrow[:]], writes=[ps[:, :]])
                dbuf = st1[0]
                b.dma("sync", dbuf[:, 0:512].bitcast(BF16) if False else vd[:, 512 * rt:512 * (rt + 1)],
                      dvd_in[128 * rt:128 * (rt + 1), tsl])
                b.op("vector", lambda v, rt=rt, ps=ps: v.tensor_tensor(
                    out=vd[:, 512 * rt:512 * (rt + 1)], in0=vd[:, 512 * rt:512 * (rt + 1)],
                    in1=ps[:, :], op=OP.mult),
                    reads=[vd[:, 512 * rt:512 * (rt + 1)], ps[:, :]],
                    writes=[vd[:, 512 * rt:512 * (rt + 1)]])
            for et in range(8):
                ps = PS[3 + et % 2]
                for rt in range(8):
                    wv = wvusb[rt]
                    b.op("tensor", lambda t, wv=wv, rt=rt, et=et, ps=ps: t.matmul(
                        ps[:, :], wv[:, 128 * et:128 * (et + 1)],
                        vd[:, 512 * rt:512 * (rt + 1)],
                        start=(rt == 0), stop=(rt == 7)),
                        reads=[wv[:, 128 * et:128 * (et + 1)],
                               vd[:, 512 * rt:512 * (rt + 1)]], writes=[ps[:, :]])
                xnb_ = st2[et % 2]
                s0c = s0b[et % 2]
                b.dma("sync", xnb_[:, 0:512], xn_in[128 * et:128 * (et + 1), tsl])
                b.op("vector", lambda v, ps=ps, s0c=s0c, xnb_=xnb_: v.scalar_tensor_tensor(
                    out=s0c[:], in0=ps[:, :], scalar=1.0 / 16.0,
                    in1=xnb_[:, 0:512], op0=OP.mult, op1=OP.add),
                    reads=[ps[:, :], xnb_[:, 0:512]], writes=[s0c[:]])
                b.op("vector", lambda v, et=et, s0c=s0c: v.tensor_tensor(
                    out=s0c[:], in0=s0c[:], in1=g0[:, 512 * et:512 * (et + 1)],
                    op=OP.mult),
                    reads=[s0c[:], g0[:, 512 * et:512 * (et + 1)]], writes=[s0c[:]])
                xb = st1[et % 2]
                b.dma("sync", xb[:, 0:512], x_in[128 * et:128 * (et + 1), tsl])
                b.op("vector", lambda v, s0c=s0c, xb=xb: v.tensor_tensor(
                    out=s0c[:], in0=s0c[:], in1=xb[:, 0:512], op=OP.add),
                    reads=[s0c[:], xb[:, 0:512]], writes=[s0c[:]])
                b.dma("sync", x1_out[128 * et:128 * (et + 1), tsl], s0c[:])

        for cs in range(4):
            sl = slice(512 * cs, 512 * (cs + 1))
            for i in range(NCH):
                xb = st1[i % 2]
                b.dma("sync", xb[:, 0:512], x1_out[128 * i:128 * (i + 1), sl])
                b.op("tensor", lambda t, i=i, xb=xb: t.matmul(
                    PS[6][0:1, :], ones_col[:], xb[:, 0:512],
                    start=(i == 0), stop=(i == NCH - 1)),
                    reads=[xb[:, 0:512], ones_col[:]], writes=[PS[6][0:1, :]])
                b.op("scalar", lambda s, xb=xb: s.activation(
                    out=sqs[:], in_=xb[:, 0:512], func=AF.Square),
                    reads=[xb[:, 0:512]], writes=[sqs[:]])
                b.op("tensor", lambda t, i=i: t.matmul(
                    PS[7][0:1, :], ones_col[:], sqs[:],
                    start=(i == 0), stop=(i == NCH - 1)),
                    reads=[sqs[:], ones_col[:]], writes=[PS[7][0:1, :]])
            b.op("vector", lambda v, sl=sl: v.tensor_copy(sx_row[0:1, sl], PS[6][0:1, :]),
                 reads=[PS[6][0:1, :]], writes=[sx_row[0:1, sl]])
            b.op("vector", lambda v, sl=sl: v.tensor_copy(sq_row[0:1, sl], PS[7][0:1, :]),
                 reads=[PS[7][0:1, :]], writes=[sq_row[0:1, sl]])
        b.op("vector", lambda v: v.tensor_scalar(out=sx_row[:], in0=sx_row[:],
             scalar1=1.0 / D, scalar2=None, op0=OP.mult),
             reads=[sx_row[:]], writes=[sx_row[:]])
        h_bcast_row(b, PS[0], mubc, sx_row, ones_row, T)
        b.op("vector", lambda v: v.tensor_tensor(out=sx_row[:], in0=sx_row[:],
             in1=sx_row[:], op=OP.mult), reads=[sx_row[:]], writes=[sx_row[:]])
        b.op("vector", lambda v: v.tensor_scalar(out=sq_row[:], in0=sq_row[:],
             scalar1=1.0 / D, scalar2=None, op0=OP.mult),
             reads=[sq_row[:]], writes=[sq_row[:]])
        b.op("vector", lambda v: v.tensor_tensor(out=sq_row[:], in0=sq_row[:],
             in1=sx_row[:], op=OP.subtract), reads=[sq_row[:], sx_row[:]], writes=[sq_row[:]])
        b.op("vector", lambda v: v.tensor_scalar(out=sq_row[:], in0=sq_row[:],
             scalar1=1e-5, scalar2=None, op0=OP.add), reads=[sq_row[:]], writes=[sq_row[:]])
        b.op("scalar", lambda s: s.activation(out=sq_row[:], in_=sq_row[:],
             func=AF.Sqrt), reads=[sq_row[:]], writes=[sq_row[:]])
        b.op("vector", lambda v: v.reciprocal(out=sq_row[:], in_=sq_row[:]),
             reads=[sq_row[:]], writes=[sq_row[:]])
        h_bcast_row(b, PS[1], rstdbc, sq_row, ones_row, T)

        for i in range(NCH):
            xb = st1[i % 2]
            b.dma("sync", xb[:], x1_out[128 * i:128 * (i + 1), :])
            b.op("vector", lambda v, xb=xb: v.tensor_tensor(out=xb[:], in0=xb[:],
                 in1=mubc[:], op=OP.subtract), reads=[xb[:], mubc[:]], writes=[xb[:]])
            b.op("vector", lambda v, xb=xb: v.tensor_tensor(out=xb[:], in0=xb[:],
                 in1=rstdbc[:], op=OP.mult), reads=[xb[:], rstdbc[:]], writes=[xb[:]])
            b.op("vector", lambda v, i=i, xb=xb: v.reduce_max(out=mxcol[:, i:i + 1],
                 in_=xb[:], axis=mybir.AxisListType.X),
                 reads=[xb[:]], writes=[mxcol[:, i:i + 1]])
            b.op("vector", lambda v, i=i, xb=xb: v.tensor_reduce(out=mncol[:, i:i + 1],
                 in_=xb[:], axis=mybir.AxisListType.X, op=OP.min),
                 reads=[xb[:]], writes=[mncol[:, i:i + 1]])
            b.dma("sync", xn1_out[128 * i:128 * (i + 1), :], xb[:])
            for cs in range(4):
                sl = slice(512 * cs, 512 * (cs + 1))
                b.op("scalar", lambda s, xb=xb, sl=sl: s.activation(
                    out=sqs[:], in_=xb[:, sl], func=AF.Square),
                    reads=[xb[:, sl]], writes=[sqs[:]])
                b.op("tensor", lambda t, i=i, cs=cs: t.matmul(
                    PS[2 + cs][0:1, :], ones_col[:], sqs[:],
                    start=(i == 0), stop=(i == NCH - 1)),
                    reads=[sqs[:], ones_col[:]], writes=[PS[2 + cs][0:1, :]])
        for cs in range(4):
            sl = slice(512 * cs, 512 * (cs + 1))
            b.op("vector", lambda v, sl=sl, cs=cs: v.tensor_copy(
                sx_row[0:1, sl], PS[2 + cs][0:1, :]),
                reads=[PS[2 + cs][0:1, :]], writes=[sx_row[0:1, sl]])
        b.op("vector", lambda v: v.tensor_scalar(out=sx_row[:], in0=sx_row[:],
             scalar1=1.0 / D, scalar2=None, op0=OP.mult), reads=[sx_row[:]], writes=[sx_row[:]])
        b.op("scalar", lambda s: s.activation(out=sx_row[:], in_=sx_row[:],
             func=AF.Sqrt), reads=[sx_row[:]], writes=[sx_row[:]])
        b.op("vector", lambda v: v.reciprocal(out=sx_row[:], in_=sx_row[:]),
             reads=[sx_row[:]], writes=[sx_row[:]])
        h_bcast_row(b, PS[0], rstdbc, sx_row, ones_row, T)
        b.op("vector", lambda v: v.reduce_max(out=mxcol[:, 0:1], in_=mxcol[:],
             axis=mybir.AxisListType.X), reads=[mxcol[:]], writes=[mxcol[:, 0:1]])
        b.op("vector", lambda v: v.tensor_reduce(out=mncol[:, 0:1], in_=mncol[:],
             axis=mybir.AxisListType.X, op=OP.min), reads=[mncol[:]], writes=[mncol[:, 0:1]])
        b.op("tensor", lambda t: t.transpose(PS[1][0:1, 0:128], mxcol[:, 0:1], ident[:]),
             reads=[mxcol[:], ident[:]], writes=[PS[1][0:1, 0:128]])
        b.op("vector", lambda v: v.tensor_copy(mm2[:], PS[1][0:1, 0:128]),
             reads=[PS[1][0:1, 0:128]], writes=[mm2[:]])
        b.op("vector", lambda v: v.reduce_max(out=mmo[0:1, 1:2], in_=mm2[:],
             axis=mybir.AxisListType.X), reads=[mm2[:]], writes=[mmo[:]])
        b.op("tensor", lambda t: t.transpose(PS[1][0:1, 0:128], mncol[:, 0:1], ident[:]),
             reads=[mncol[:], ident[:]], writes=[PS[1][0:1, 0:128]])
        b.op("vector", lambda v: v.tensor_copy(mm2[:], PS[1][0:1, 0:128]),
             reads=[PS[1][0:1, 0:128]], writes=[mm2[:]])
        b.op("vector", lambda v: v.tensor_reduce(out=mmo[0:1, 0:1], in_=mm2[:],
             axis=mybir.AxisListType.X, op=OP.min), reads=[mm2[:]], writes=[mmo[:]])
        b.dma("sync", mm_out, mmo[:])

        b.dma("sync", nsc[:], ns_in)
        qtile = hqsb[0]
        for i in range(NCH):
            xb = st1[i % 2]
            b.dma("sync", xb[:], xn1_out[128 * i:128 * (i + 1), :])
            b.op("vector", lambda v, xb=xb: v.tensor_tensor(out=xb[:], in0=xb[:],
                 in1=rstdbc[:], op=OP.mult), reads=[xb[:], rstdbc[:]], writes=[xb[:]])
            b.op("vector", lambda v, i=i, xb=xb: v.tensor_scalar(out=qtile[:], in0=xb[:],
                 scalar1=nsc[:, i:i + 1], scalar2=None, op0=OP.mult),
                 reads=[xb[:], nsc[:, i:i + 1]], writes=[qtile[:]])
            b.dma("sync", xr1_out[128 * i:128 * (i + 1), :], qtile[:])
        b.finalize()
    return nc



def build_L3b():
    nc = bass.Bass("TRN2", target_bir_lowering=False)
    with ExitStack() as ctx:
        b = TB(nc, ctx)
        xr_in = b.dram("xr1_bf", [D, T], BF16, kind="ExternalInput").ap()
        w_in = b.dram("wqkvg", [D, 4 * D], F32, kind="ExternalInput").ap()
        bq_in = b.dram("bq_col", [128, 8], F32, kind="ExternalInput").ap()
        bkvg_in = b.dram("bkvg_rows", [1, 3 * D], F32, kind="ExternalInput").ap()
        cbig_in = b.dram("CBIG", [128, T], F32, kind="ExternalInput").ap()
        sbig_in = b.dram("SBIG", [128, T], F32, kind="ExternalInput").ap()
        cosk_in = b.dram("cosK", [T, 32], F32, kind="ExternalInput").ap()
        sink_in = b.dram("sinK", [T, 32], F32, kind="ExternalInput").ap()
        pswap_in = b.dram("PSWAP", [128, 128], F32, kind="ExternalInput").ap()
        q_out = b.dram("qro_bf", [D, T], BF16, kind="ExternalOutput").ap()
        k_out = b.dram("k_tm", [T, D], BF16, kind="ExternalOutput").ap()
        v_out = b.dram("v_tm", [T, D], BF16, kind="ExternalOutput").ap()
        g_out = b.dram("g_tm", [T, D], BF16, kind="ExternalOutput").ap()
        ds_out = b.dram("ds_tot", [64, D], F32, kind="ExternalOutput").ap()
        dbg_kt = b.dram("dbg_kt", [128, D], F32, kind="ExternalOutput").ap()
        dbg_kr = b.dram("dbg_kr", [128, D], F32, kind="ExternalOutput").ap()
        z_out = b.dram("z_tot", [64, 16], F32, kind="ExternalOutput").ap()

        xr = [b.sbuf(f"xr{i}", [128, T], BF16) for i in range(NCH)]
        wq_sb = [b.sbuf(f"wqsb{i}", [128, D], BF16) for i in range(NCH)]
        wk_sb = [b.sbuf(f"wksb{i}", [128, D], BF16) for i in range(NCH)]
        wv_sb = [b.sbuf(f"wvsb{i}", [128, D], BF16) for i in range(NCH)]
        wg_sb = [b.sbuf(f"wgsb{i}", [128, D], BF16) for i in range(NCH)]
        bqc = b.sbuf("bqc", [128, 8], F32)
        bkbc = b.sbuf("bkbc", [128, D], F32)
        bvbc = b.sbuf("bvbc", [128, D], F32)
        bgbc = b.sbuf("bgbc", [128, D], F32)
        brow = b.sbuf("brow", [1, 3 * D], F32)
        cbig = b.sbuf("cbig", [128, T], F32)
        sbig = b.sbuf("sbig", [128, T], F32)
        cost = b.sbuf("cost", [128, 512], F32)
        sint = b.sbuf("sint", [128, 512], F32)
        pswap = b.sbuf("pswap", [128, 128], BF16)
        pswap32 = b.sbuf("pswap32", [128, 128], F32)
        ones_row = b.sbuf("ones_row", [1, 128], F32)
        ones_col = b.sbuf("ones_col", [128, 1], F32)
        ones_col_bf = b.sbuf("ones_col_bf", [128, 1], BF16)
        qt = b.sbuf("qt", [128, T], F32)
        qr = b.sbuf("qr", [128, T], F32)
        qtmp = b.sbuf("qtmp", [128, T], F32)
        qbf = b.sbuf("qbf", [128, T], BF16)
        kt = b.sbuf("kt", [128, D], F32)
        vt = b.sbuf("vt", [128, D], F32)
        gt = b.sbuf("gt", [128, D], F32)
        krot = b.sbuf("krot", [128, D], F32)
        ktmp = b.sbuf("ktmp", [128, D], F32)
        ktile = b.sbuf("ktile", [128, D], BF16)
        vtile = b.sbuf("vtile", [128, D], BF16)
        gtile = b.sbuf("gtile", [128, D], BF16)
        dssb = b.sbuf("dssb", [64, D], F32)
        zsb = b.sbuf("zsb", [64, 16], F32)
        PS = [b.psum(f"PS{i}", [128, 512], F32) for i in range(8)]

        for i in range(NCH):
            b.dma("sync", xr[i][:], xr_in[128 * i:128 * (i + 1), :])
            b.dma("gpsimd", wq_sb[i][:], w_in[128 * i:128 * (i + 1), 0:D])
            b.dma("gpsimd", wk_sb[i][:], w_in[128 * i:128 * (i + 1), D:2 * D])
            b.dma("gpsimd", wv_sb[i][:], w_in[128 * i:128 * (i + 1), 2 * D:3 * D])
            b.dma("gpsimd", wg_sb[i][:], w_in[128 * i:128 * (i + 1), 3 * D:4 * D])
        b.dma("sync", bqc[:], bq_in)
        b.dma("sync", brow[:], bkvg_in)
        b.dma("sync", cbig[:], cbig_in)
        b.dma("sync", sbig[:], sbig_in)
        b.dma("sync", cost[:].rearrange("p (t c) -> p t c", c=32),
              cosk_in.rearrange("(t p) c -> p t c", p=128))
        b.dma("sync", sint[:].rearrange("p (t c) -> p t c", c=32),
              sink_in.rearrange("(t p) c -> p t c", p=128))
        b.dma("gpsimd", pswap[:], pswap_in)
        b.dma("sync", pswap32[:], pswap_in)
        b.op("vector", lambda v: v.memset(ones_row[:], 1.0), writes=[ones_row[:]])
        b.op("vector", lambda v: v.memset(ones_col[:], 1.0), writes=[ones_col[:]])
        b.op("vector", lambda v: v.memset(ones_col_bf[:], 1.0), writes=[ones_col_bf[:]])
        h_bcast_row(b, PS[0], bkbc, brow[0:1, 0:D], ones_row, D)
        h_bcast_row(b, PS[0], bvbc, brow[0:1, D:2 * D], ones_row, D)
        h_bcast_row(b, PS[0], bgbc, brow[0:1, 2 * D:3 * D], ones_row, D)

        for ot in range(8):
            for cs in range(4):
                ps = PS[cs % 2]
                sl = slice(512 * cs, 512 * (cs + 1))
                for i in range(NCH):
                    b.op("tensor", lambda t, i=i, ot=ot, sl=sl, ps=ps: t.matmul(
                        ps[:, :], wq_sb[i][:, 128 * ot:128 * (ot + 1)], xr[i][:, sl],
                        start=(i == 0), stop=(i == NCH - 1)),
                        reads=[wq_sb[i][:], xr[i][:, sl]], writes=[ps[:, :]])
                b.op("vector", lambda v, sl=sl, ps=ps, ot=ot: v.tensor_scalar(
                    out=qt[:, sl], in0=ps[:, :], scalar1=bqc[:, ot:ot + 1], scalar2=None,
                    op0=OP.add), reads=[ps[:, :], bqc[:, ot:ot + 1]], writes=[qt[:, sl]])
            for cs in range(4):
                ps = PS[2 + cs % 2]
                sl = slice(512 * cs, 512 * (cs + 1))
                b.op("tensor", lambda t, sl=sl, ps=ps: t.matmul(
                    ps[:, :], pswap32[:], qt[:, sl], start=True, stop=True),
                    reads=[pswap32[:], qt[:, sl]], writes=[ps[:, :]])
                b.op("vector", lambda v, sl=sl: v.tensor_tensor(
                    out=qr[:, sl], in0=qt[:, sl], in1=cbig[:, sl], op=OP.mult),
                    reads=[qt[:, sl], cbig[:, sl]], writes=[qr[:, sl]])
                b.op("vector", lambda v, sl=sl, ps=ps: v.tensor_tensor(
                    out=qtmp[:, sl], in0=ps[:, :], in1=sbig[:, sl], op=OP.mult),
                    reads=[ps[:, :], sbig[:, sl]], writes=[qtmp[:, sl]])
                b.op("vector", lambda v, sl=sl: v.tensor_tensor(
                    out=qr[:, sl], in0=qr[:, sl], in1=qtmp[:, sl], op=OP.add),
                    reads=[qr[:, sl], qtmp[:, sl]], writes=[qr[:, sl]])
            b.op("vector", lambda v: v.tensor_scalar(out=qtmp[:], in0=qr[:],
                 scalar1=0.0, scalar2=None, op0=OP.min), reads=[qr[:]], writes=[qtmp[:]])
            b.op("scalar", lambda s: s.activation(out=qtmp[:], in_=qtmp[:],
                 func=AF.Exp), reads=[qtmp[:]], writes=[qtmp[:]])
            b.op("vector", lambda v: v.tensor_scalar(out=qr[:], in0=qr[:],
                 scalar1=0.0, scalar2=None, op0=OP.max), reads=[qr[:]], writes=[qr[:]])
            b.op("vector", lambda v: v.tensor_tensor(out=qr[:], in0=qr[:],
                 in1=qtmp[:], op=OP.add), reads=[qr[:], qtmp[:]], writes=[qr[:]])
            b.op("vector", lambda v: v.tensor_copy(qbf[:], qr[:]),
                 reads=[qr[:]], writes=[qbf[:]])
            b.dma("sync", q_out[128 * ot:128 * (ot + 1), :], qbf[:])

        for tt in range(16):
            tsl = slice(128 * tt, 128 * (tt + 1))
            for w_i, (wsb, dst) in enumerate([(wk_sb, kt), (wv_sb, vt), (wg_sb, gt)]):
                for half in range(2):
                    ps = PS[2 * w_i + half]
                    osl = slice(512 * half, 512 * (half + 1))
                    for i in range(NCH):
                        b.op("tensor", lambda t, i=i, ps=ps, osl=osl, tsl=tsl, wsb=wsb:
                             t.matmul(ps[:, :], xr[i][:, tsl], wsb[i][:, osl],
                                      start=(i == 0), stop=(i == NCH - 1)),
                             reads=[xr[i][:, tsl], wsb[i][:, osl]], writes=[ps[:, :]])
                    bbc = [bkbc, bvbc, bgbc][w_i]
                    b.op("vector", lambda v, ps=ps, osl=osl, dst=dst, bbc=bbc:
                         v.tensor_tensor(out=dst[:, osl], in0=ps[:, :],
                                         in1=bbc[:, osl], op=OP.add),
                         reads=[ps[:, :], bbc[:, osl]], writes=[dst[:, osl]])
            if tt == 15:
                b.dma("sync", dbg_kt, kt[:])
            c3 = cost[:, 32 * tt:32 * (tt + 1)].unsqueeze(1).broadcast_to([128, 16, 32])
            s3 = sint[:, 32 * tt:32 * (tt + 1)].unsqueeze(1).broadcast_to([128, 16, 32])
            ktv = kt[:].rearrange("p (h a c) -> p h a c", h=16, a=2)
            krv = krot[:].rearrange("p (h a c) -> p h a c", h=16, a=2)
            ktm = ktmp[:].rearrange("p (h a c) -> p h a c", h=16, a=2)
            fh_k, sh_k = ktv[:, :, 0:1, :], ktv[:, :, 1:2, :]
            fh_r, sh_r = krv[:, :, 0:1, :], krv[:, :, 1:2, :]
            fh_t, sh_t = ktm[:, :, 0:1, :], ktm[:, :, 1:2, :]
            c4 = cost[:, 32 * tt:32 * (tt + 1)].unsqueeze(1).unsqueeze(2).broadcast_to([128, 16, 1, 32])
            s4 = sint[:, 32 * tt:32 * (tt + 1)].unsqueeze(1).unsqueeze(2).broadcast_to([128, 16, 1, 32])
            b.op("vector", lambda v, c4=c4: v.tensor_tensor(out=fh_r, in0=fh_k, in1=c4,
                 op=OP.mult), reads=[kt[:], cost[:]], writes=[krot[:]])
            b.op("vector", lambda v, s4=s4: v.tensor_tensor(out=fh_t, in0=sh_k, in1=s4,
                 op=OP.mult), reads=[kt[:], sint[:]], writes=[ktmp[:]])
            b.op("vector", lambda v: v.tensor_tensor(out=fh_r, in0=fh_r, in1=fh_t,
                 op=OP.subtract), reads=[krot[:], ktmp[:]], writes=[krot[:]])
            b.op("vector", lambda v, c4=c4: v.tensor_tensor(out=sh_r, in0=sh_k, in1=c4,
                 op=OP.mult), reads=[kt[:], cost[:]], writes=[krot[:]])
            b.op("vector", lambda v, s4=s4: v.tensor_tensor(out=sh_t, in0=fh_k, in1=s4,
                 op=OP.mult), reads=[kt[:], sint[:]], writes=[ktmp[:]])
            b.op("vector", lambda v: v.tensor_tensor(out=sh_r, in0=sh_r, in1=sh_t,
                 op=OP.add), reads=[krot[:], ktmp[:]], writes=[krot[:]])
            b.op("vector", lambda v: v.tensor_scalar(out=ktmp[:], in0=krot[:],
                 scalar1=0.0, scalar2=None, op0=OP.min), reads=[krot[:]], writes=[ktmp[:]])
            b.op("scalar", lambda s: s.activation(out=ktmp[:], in_=ktmp[:],
                 func=AF.Exp), reads=[ktmp[:]], writes=[ktmp[:]])
            b.op("vector", lambda v: v.tensor_scalar(out=krot[:], in0=krot[:],
                 scalar1=0.0, scalar2=None, op0=OP.max), reads=[krot[:]], writes=[krot[:]])
            b.op("vector", lambda v: v.tensor_tensor(out=krot[:], in0=krot[:],
                 in1=ktmp[:], op=OP.add), reads=[krot[:], ktmp[:]], writes=[krot[:]])
            b.op("vector", lambda v: v.tensor_copy(ktile[:], krot[:]),
                 reads=[krot[:]], writes=[ktile[:]])
            if tt == 15:
                b.dma("sync", dbg_kr, krot[:])
            b.dma("sync", k_out[tsl, :], ktile[:])
            b.op("vector", lambda v: v.tensor_copy(vtile[:], vt[:]),
                 reads=[vt[:]], writes=[vtile[:]])
            b.dma("sync", v_out[tsl, :], vtile[:])
            b.op("scalar", lambda s: s.activation(out=gtile[:], in_=gt[:],
                 func=AF.Sigmoid), reads=[gt[:]], writes=[gtile[:]])
            b.dma("sync", g_out[tsl, :], gtile[:])
            for h in range(16):
                psd = PS[6] if h < 8 else PS[7]
                col = 64 * (h % 8)
                b.op("tensor", lambda t, h=h, psd=psd, col=col: t.matmul(
                    psd[0:64, col:col + 64], ktile[:, 64 * h:64 * (h + 1)],
                    vtile[:, 64 * h:64 * (h + 1)], start=True, stop=True),
                    reads=[ktile[:, 64 * h:64 * (h + 1)], vtile[:, 64 * h:64 * (h + 1)]],
                    writes=[psd[0:64, col:col + 64]])
                psz = PS[5]
                b.op("tensor", lambda t, h=h, psz=psz: t.matmul(
                    psz[0:64, h:h + 1], ktile[:, 64 * h:64 * (h + 1)],
                    ones_col_bf[:], start=True, stop=True),
                    reads=[ktile[:, 64 * h:64 * (h + 1)], ones_col_bf[:]],
                    writes=[psz[0:64, h:h + 1]])
            if tt == 0:
                b.op("vector", lambda v: v.tensor_copy(dssb[:, 0:512], PS[6][0:64, :]),
                     reads=[PS[6][0:64, :]], writes=[dssb[:, 0:512]])
                b.op("vector", lambda v: v.tensor_copy(dssb[:, 512:1024], PS[7][0:64, :]),
                     reads=[PS[7][0:64, :]], writes=[dssb[:, 512:1024]])
                b.op("vector", lambda v: v.tensor_copy(zsb[:], PS[5][0:64, 0:16]),
                     reads=[PS[5][0:64, 0:16]], writes=[zsb[:]])
            else:
                b.op("vector", lambda v: v.tensor_tensor(out=dssb[:, 0:512],
                     in0=dssb[:, 0:512], in1=PS[6][0:64, :], op=OP.add),
                     reads=[dssb[:, 0:512], PS[6][0:64, :]], writes=[dssb[:, 0:512]])
                b.op("vector", lambda v: v.tensor_tensor(out=dssb[:, 512:1024],
                     in0=dssb[:, 512:1024], in1=PS[7][0:64, :], op=OP.add),
                     reads=[dssb[:, 512:1024], PS[7][0:64, :]], writes=[dssb[:, 512:1024]])
                b.op("vector", lambda v: v.tensor_tensor(out=zsb[:],
                     in0=zsb[:], in1=PS[5][0:64, 0:16], op=OP.add),
                     reads=[zsb[:], PS[5][0:64, 0:16]], writes=[zsb[:]])
        b.dma("sync", ds_out, dssb[:])
        b.dma("sync", z_out, zsb[:])
        b.finalize()
    return nc



def build_L4a():
    nc = bass.Bass("TRN2", target_bir_lowering=False)
    with ExitStack() as ctx:
        b = TB(nc, ctx)
        q_in = b.dram("qro_bf", [D, T], BF16, kind="ExternalInput").ap()
        k_in = b.dram("k_tm", [T, D], BF16, kind="ExternalInput").ap()
        v_in = b.dram("v_tm", [T, D], BF16, kind="ExternalInput").ap()
        g_in = b.dram("g_tm", [T, D], BF16, kind="ExternalInput").ap()
        dsp_in = b.dram("ds_prev", [3, 64 * D], F32, kind="ExternalInput").ap()
        zp_in = b.dram("z_prev", [3, 64 * 16], F32, kind="ExternalInput").ap()
        xn1_in = b.dram("xn1_fm", [D, T], F32, kind="ExternalInput").ap()
        wo_in = b.dram("wo", [D, D], F32, kind="ExternalInput").ap()
        bo_in = b.dram("bo_col", [128, 8], F32, kind="ExternalInput").ap()
        ns2_in = b.dram("norm2_scale_col", [128, 8], F32, kind="ExternalInput").ap()
        ident_in = b.dram("ident", [128, 128], F32, kind="ExternalInput").ap()
        identb_in = b.dram("identb", [128, 128], BF16, kind="ExternalInput").ap()
        x2_out = b.dram("x2_fm", [D, T], F32, kind="ExternalOutput").ap()
        xr2_out = b.dram("xr2_bf", [D, T], BF16, kind="ExternalOutput").ap()
        at_dram = b.dram("attn_tm", [T, D], BF16).ap()
        dbg_S = b.dram("dbg_S", [64, D], F32, kind="ExternalOutput").ap()
        dbg_den = b.dram("dbg_den", [64, 16], F32, kind="ExternalOutput").ap()
        dbg_q = b.dram("dbg_q", [64, 2048], F32, kind="ExternalOutput").ap()
        dbg_att = b.dram("dbg_att", [64, D], F32, kind="ExternalOutput").ap()
        dbg_g = b.dram("dbg_g", [64, 2048], F32, kind="ExternalOutput").ap()
        dbg_num = b.dram("dbg_num", [64, D], F32, kind="ExternalOutput").ap()

        qsc = [b.sbuf(f"qsc{i}", [64, 2048], BF16) for i in range(2)]
        ksc = [b.sbuf(f"ksc{i}", [64, 2048], BF16) for i in range(2)]
        vsc = [b.sbuf(f"vsc{i}", [64, 2048], BF16) for i in range(2)]
        gsc = [b.sbuf(f"gsc{i}", [64, 2048], BF16) for i in range(2)]
        S_sb = b.sbuf("S_sb", [64, D], BF16)
        S32 = b.sbuf("S32", [64, D], F32)
        Z_sb = b.sbuf("Z_sb", [64, 16], BF16)
        Z32 = b.sbuf("Z32", [64, 16], F32)
        dent = b.sbuf("dent", [64, 16], F32)
        att = b.sbuf("att", [64, D], F32)
        attb = b.sbuf("attb", [64, D], BF16)
        atile = b.sbuf("atile", [128, D], BF16)
        afm = b.sbuf("afm", [128, 4096], BF16)
        wosb = [b.sbuf(f"wosb{i}", [128, D], BF16) for i in range(NCH)]
        boc = b.sbuf("boc", [128, 8], F32)
        ns2 = b.sbuf("ns2", [128, 8], F32)
        ident = b.sbuf("identsb", [128, 128], F32)
        identb = b.sbuf("identbb", [128, 128], BF16)
        ones_col = b.sbuf("ones_col", [128, 1], F32)
        x2t = b.sbuf("x2t", [128, 512], F32)
        sqs = b.sbuf("sqs", [128, 512], F32)
        sx_row = b.sbuf("sx_row", [1, T], F32)
        rstdbc = b.sbuf("rstdbc", [128, T], F32)
        st1 = [b.sbuf(f"st1_{i}", [128, T], F32) for i in range(2)]
        xrtile = b.sbuf("xrtile", [128, T], BF16)
        PS = [b.psum(f"PS{i}", [128, 512], F32) for i in range(7)]
        PSB = b.psum("PSB", [128, 256], BF16)
        onesb = b.sbuf("onesb", [64, 1], BF16)

        b.dma("sync", ident[:], ident_in)
        b.dma("sync", identb[:], identb_in)
        b.dma("sync", boc[:], bo_in)
        b.dma("sync", ns2[:], ns2_in)
        b.op("vector", lambda v: v.memset(ones_col[:], 1.0), writes=[ones_col[:]])
        b.op("vector", lambda v: v.memset(onesb[:], 1.0), writes=[onesb[:]])
        for i in range(NCH):
            b.dma("gpsimd", wosb[i][:], wo_in[128 * i:128 * (i + 1), :])

        sload = b.sbuf("sload", [64, D], F32)
        zload = b.sbuf("zload", [64, 16], F32)
        for pq in range(3):
            b.dma("sync", sload[:],
                  dsp_in[pq:pq + 1, :].rearrange("a (p f) -> p (a f)", p=64))
            b.dma("sync", zload[:],
                  zp_in[pq:pq + 1, :].rearrange("a (p f) -> p (a f)", p=64))
            if pq == 0:
                b.op("vector", lambda v: v.tensor_copy(S32[:], sload[:]),
                     reads=[sload[:]], writes=[S32[:]])
                b.op("vector", lambda v: v.tensor_copy(Z32[:], zload[:]),
                     reads=[zload[:]], writes=[Z32[:]])
            else:
                b.op("vector", lambda v: v.tensor_tensor(out=S32[:], in0=S32[:],
                     in1=sload[:], op=OP.add), reads=[S32[:], sload[:]], writes=[S32[:]])
                b.op("vector", lambda v: v.tensor_tensor(out=Z32[:], in0=Z32[:],
                     in1=zload[:], op=OP.add), reads=[Z32[:], zload[:]], writes=[Z32[:]])

        for tt in range(16):
            qb, kb, vb, gb = qsc[tt % 2], ksc[tt % 2], vsc[tt % 2], gsc[tt % 2]
            b.dma("sync", qb[:].rearrange("p (h t) -> p h t", h=16),
                  q_in[:, 128 * tt:128 * (tt + 1)].rearrange("(h d) t -> d h t", d=64))
            for buf, src in [(kb, k_in), (vb, v_in), (gb, g_in)]:
                b.dma("sync", buf[:].rearrange("p (a h d) -> p a h d", a=2, h=16),
                      src[128 * tt:128 * (tt + 1), :].rearrange(
                          "(a p) (h d) -> p a h d", a=2, h=16))
            for half in range(2):
                for h in range(16):
                    psd = PS[0] if h < 8 else PS[1]
                    col = 64 * (h % 8)
                    ks_ = kb[:, 1024 * half + 64 * h:1024 * half + 64 * (h + 1)]
                    vs_ = vb[:, 1024 * half + 64 * h:1024 * half + 64 * (h + 1)]
                    b.op("tensor", lambda t, psd=psd, col=col, ks_=ks_, vs_=vs_: t.matmul(
                        psd[0:64, col:col + 64], ks_, vs_, start=True, stop=True),
                        reads=[ks_, vs_], writes=[psd[0:64, col:col + 64]])
                    psz = PS[2]
                    b.op("tensor", lambda t, h=h, psz=psz, ks_=ks_: t.matmul(
                        psz[0:64, h:h + 1], ks_, onesb[:], start=True, stop=True),
                        reads=[ks_, onesb[:]], writes=[psz[0:64, h:h + 1]])
                b.op("vector", lambda v: v.tensor_tensor(out=S32[:, 0:512],
                     in0=S32[:, 0:512], in1=PS[0][0:64, :], op=OP.add),
                     reads=[S32[:], PS[0][0:64, :]], writes=[S32[:]])
                b.op("vector", lambda v: v.tensor_tensor(out=S32[:, 512:1024],
                     in0=S32[:, 512:1024], in1=PS[1][0:64, :], op=OP.add),
                     reads=[S32[:], PS[1][0:64, :]], writes=[S32[:]])
                b.op("vector", lambda v: v.tensor_tensor(out=Z32[:],
                     in0=Z32[:], in1=PS[2][0:64, 0:16], op=OP.add),
                     reads=[Z32[:], PS[2][0:64, 0:16]], writes=[Z32[:]])
                b.op("vector", lambda v: v.tensor_copy(S_sb[:], S32[:]),
                     reads=[S32[:]], writes=[S_sb[:]])
                b.op("vector", lambda v: v.tensor_copy(Z_sb[:], Z32[:]),
                     reads=[Z32[:]], writes=[Z_sb[:]])
                for h in range(16):
                    psn = PS[3] if h < 8 else PS[4]
                    col = 64 * (h % 8)
                    qs_ = qb[:, 128 * h + 64 * half:128 * h + 64 * (half + 1)]
                    b.op("tensor", lambda t, psn=psn, col=col, qs_=qs_, h=h: t.matmul(
                        psn[0:64, col:col + 64], qs_, S_sb[:, 64 * h:64 * (h + 1)],
                        start=True, stop=True),
                        reads=[qs_, S_sb[:, 64 * h:64 * (h + 1)]],
                        writes=[psn[0:64, col:col + 64]])
                    b.op("tensor", lambda t, qs_=qs_, h=h: t.matmul(
                        PS[5][0:64, h:h + 1], qs_, Z_sb[:, h:h + 1],
                        start=True, stop=True),
                        reads=[qs_, Z_sb[:, h:h + 1]], writes=[PS[5][0:64, h:h + 1]])
                b.op("vector", lambda v: v.tensor_scalar(out=dent[:], in0=PS[5][0:64, 0:16],
                     scalar1=1e-6, scalar2=None, op0=OP.add),
                     reads=[PS[5][0:64, 0:16]], writes=[dent[:]])
                b.op("vector", lambda v: v.reciprocal(out=dent[:], in_=dent[:]),
                     reads=[dent[:]], writes=[dent[:]])
                b.op("vector", lambda v, half=half, gb=gb: v.tensor_tensor(
                    out=att[:, 0:512], in0=PS[3][0:64, :],
                    in1=gb[:, 1024 * half:1024 * half + 512], op=OP.mult),
                    reads=[PS[3][0:64, :], gb[:]], writes=[att[:]])
                b.op("vector", lambda v, half=half, gb=gb: v.tensor_tensor(
                    out=att[:, 512:1024], in0=PS[4][0:64, :],
                    in1=gb[:, 1024 * half + 512:1024 * half + 1024], op=OP.mult),
                    reads=[PS[4][0:64, :], gb[:]], writes=[att[:]])
                a3 = att[:].rearrange("p (h e) -> p h e", h=16)
                r3 = dent[:].unsqueeze(2).broadcast_to([64, 16, 64])
                b.op("vector", lambda v, a3=a3, r3=r3: v.tensor_tensor(
                    out=a3, in0=a3, in1=r3, op=OP.mult),
                    reads=[att[:], dent[:]], writes=[att[:]])
                if tt == 0 and half == 0:
                    nm = b.sbuf("nm32", [64, D], F32)
                    b.op("vector", lambda v: v.tensor_copy(nm[:, 0:512], PS[3][0:64, :]),
                         reads=[PS[3][0:64, :]], writes=[nm[:]])
                    b.op("vector", lambda v: v.tensor_copy(nm[:, 512:1024], PS[4][0:64, :]),
                         reads=[PS[4][0:64, :]], writes=[nm[:]])
                    b.dma("sync", dbg_num, nm[:])
                    gd = b.sbuf("gd32", [64, 2048], F32)
                    b.op("vector", lambda v, gb=gb: v.tensor_copy(gd[:], gb[:]),
                         reads=[gb[:]], writes=[gd[:]])
                    b.dma("sync", dbg_g, gd[:])
                    b.dma("sync", dbg_att, att[:])
                    b.dma("sync", dbg_S, S32[:])
                    b.dma("sync", dbg_den, dent[:])
                    qd = b.sbuf("qd32", [64, 2048], F32)
                    b.op("vector", lambda v, qb=qb: v.tensor_copy(qd[:], qb[:]),
                         reads=[qb[:]], writes=[qd[:]])
                    b.dma("sync", dbg_q, qd[:])
                b.op("vector", lambda v: v.tensor_copy(attb[:], att[:]),
                     reads=[att[:]], writes=[attb[:]])
                b.dma("sync", at_dram[128 * tt + 64 * half:128 * tt + 64 * (half + 1), :],
                      attb[:])

        for cs in range(4):
            tsl = slice(512 * cs, 512 * (cs + 1))
            for q4 in range(4):
                tadr = 512 * cs + 128 * q4
                b.dma("sync", atile[:], at_dram[tadr:tadr + 128, :])
                for ch in range(NCH):
                    b.op("tensor", lambda t, ch=ch: t.transpose(
                        PSB[:, 0:128], atile[:, 128 * ch:128 * (ch + 1)],
                        identb[:]), reads=[atile[:], identb[:]],
                        writes=[PSB[:, 0:128]])
                    b.op("vector", lambda v, ch=ch, q4=q4: v.tensor_copy(
                        afm[:, 512 * ch + 128 * q4: 512 * ch + 128 * (q4 + 1)],
                        PSB[:, 0:128]),
                        reads=[PSB[:, 0:128]], writes=[afm[:]])
            for ft in range(8):
                ps = PS[6]
                for ch in range(NCH):
                    b.op("tensor", lambda t, ft=ft, ch=ch, ps=ps: t.matmul(
                        ps[:, :], wosb[ch][:, 128 * ft:128 * (ft + 1)],
                        afm[:, 512 * ch:512 * (ch + 1)],
                        start=(ch == 0), stop=(ch == NCH - 1)),
                        reads=[wosb[ch][:], afm[:]], writes=[ps[:, :]])
                xb = st1[ft % 2]
                b.dma("sync", xb[:, 0:512], xn1_in[128 * ft:128 * (ft + 1), tsl])
                b.op("vector", lambda v, ps=ps, ft=ft: v.tensor_scalar(
                    out=x2t[:], in0=ps[:, :], scalar1=boc[:, ft:ft + 1], scalar2=None,
                    op0=OP.add), reads=[ps[:, :], boc[:, ft:ft + 1]], writes=[x2t[:]])
                b.op("vector", lambda v, xb=xb: v.tensor_tensor(out=x2t[:], in0=x2t[:],
                     in1=xb[:, 0:512], op=OP.add), reads=[x2t[:], xb[:, 0:512]],
                     writes=[x2t[:]])
                b.dma("sync", x2_out[128 * ft:128 * (ft + 1), tsl], x2t[:])
                b.op("scalar", lambda s: s.activation(out=sqs[:], in_=x2t[:],
                     func=AF.Square), reads=[x2t[:]], writes=[sqs[:]])
                b.op("tensor", lambda t, ft=ft, cs=cs: t.matmul(
                    PS[5][0:1, :], ones_col[:], sqs[:],
                    start=(ft == 0), stop=(ft == NCH - 1)),
                    reads=[sqs[:], ones_col[:]], writes=[PS[5][0:1, :]])
            b.op("vector", lambda v, tsl=tsl: v.tensor_copy(sx_row[0:1, tsl], PS[5][0:1, :]),
                 reads=[PS[5][0:1, :]], writes=[sx_row[0:1, tsl]])
        b.op("vector", lambda v: v.tensor_scalar(out=sx_row[:], in0=sx_row[:],
             scalar1=1.0 / D, scalar2=None, op0=OP.mult), reads=[sx_row[:]], writes=[sx_row[:]])
        b.op("scalar", lambda s: s.activation(out=sx_row[:], in_=sx_row[:],
             func=AF.Sqrt), reads=[sx_row[:]], writes=[sx_row[:]])
        b.op("vector", lambda v: v.reciprocal(out=sx_row[:], in_=sx_row[:]),
             reads=[sx_row[:]], writes=[sx_row[:]])
        ones_row = b.sbuf("ones_row", [1, 128], F32)
        b.op("vector", lambda v: v.memset(ones_row[:], 1.0), writes=[ones_row[:]])
        h_bcast_row(b, PS[0], rstdbc, sx_row, ones_row, T)
        for i in range(NCH):
            xb = st1[i % 2]
            b.dma("sync", xb[:], x2_out[128 * i:128 * (i + 1), :])
            b.op("vector", lambda v, xb=xb: v.tensor_tensor(out=xb[:], in0=xb[:],
                 in1=rstdbc[:], op=OP.mult), reads=[xb[:], rstdbc[:]], writes=[xb[:]])
            b.op("vector", lambda v, i=i, xb=xb: v.tensor_scalar(out=xrtile[:], in0=xb[:],
                 scalar1=ns2[:, i:i + 1], scalar2=None, op0=OP.mult),
                 reads=[xb[:], ns2[:, i:i + 1]], writes=[xrtile[:]])
            b.dma("sync", xr2_out[128 * i:128 * (i + 1), :], xrtile[:])
        b.finalize()
    return nc



def build_L4b():
    nc = bass.Bass("TRN2", target_bir_lowering=False)
    with ExitStack() as ctx:
        b = TB(nc, ctx)
        xr2_in = b.dram("xr2_bf", [D, T], BF16, kind="ExternalInput").ap()
        xn1_in = b.dram("xn1_fm", [D, T], F32, kind="ExternalInput").ap()
        mins_in = b.dram("mins", [1, NCORES], F32, kind="ExternalInput").ap()
        maxs_in = b.dram("maxs", [1, NCORES], F32, kind="ExternalInput").ap()
        rw_in = b.dram("router_w", [D, 8], F32, kind="ExternalInput").ap()
        rb_in = b.dram("router_b", [1, 8], F32, kind="ExternalInput").ap()
        w1_in = b.dram("w1t_ffn", [D, 4096], F32, kind="ExternalInput").ap()
        b1_in = b.dram("b1_col", [128, 32], F32, kind="ExternalInput").ap()
        iota_in = b.dram("iota8", [1, 8], F32, kind="ExternalInput").ap()
        ident_in = b.dram("ident", [128, 128], F32, kind="ExternalInput").ap()
        ridx1_o = b.dram("ridx1", [16, 128], F32, kind="ExternalOutput").ap()
        ridx2_o = b.dram("ridx2", [16, 128], F32, kind="ExternalOutput").ap()
        rg1_o = b.dram("rg1", [16, 128], F32, kind="ExternalOutput").ap()
        rg2_o = b.dram("rg2", [16, 128], F32, kind="ExternalOutput").ap()
        rr1_o = b.dram("rr1", [1, T], F32, kind="ExternalOutput").ap()
        rr2_o = b.dram("rr2", [1, T], F32, kind="ExternalOutput").ap()
        cnt_o = b.dram("cnt_row", [1, 8], F32, kind="ExternalOutput").ap()
        h2_o = b.dram("h2_bf", [4096, T], BF16, kind="ExternalOutput").ap()
        mmh2_o = b.dram("minmax_h2", [1, 2], F32, kind="ExternalOutput").ap()
        wq1_dram = b.dram("wq1_scr", [D, 4096], BF16).ap()

        xcb = [b.sbuf(f"xcb{i}", [128, D], BF16) for i in range(2)]
        rw_sb = [b.sbuf(f"rwsb{i}", [128, 8], BF16) for i in range(NCH)]
        rbbc = b.sbuf("rbbc", [128, 8], F32)
        bc8 = b.sbuf("bc8", [128, 8], F32)
        lg = b.sbuf("lg", [128, 8], F32)
        m1 = b.sbuf("m1c", [128, 1], F32)
        m2 = b.sbuf("m2c", [128, 1], F32)
        negm = b.sbuf("negm", [128, 1], F32)
        ssum = b.sbuf("ssum", [128, 1], F32)
        eq1 = b.sbuf("eq1", [128, 8], F32)
        tsc1 = b.sbuf("tsc1", [128, 8], F32)
        idx1c = b.sbuf("idx1c", [128, 1], F32)
        idx2c = b.sbuf("idx2c", [128, 1], F32)
        ind1 = b.sbuf("ind1", [128, 8], F32)
        ind2 = b.sbuf("ind2", [128, 8], F32)
        p2 = b.sbuf("p2", [128, 8], F32)
        idx1a = b.sbuf("idx1a", [128, 16], F32)
        idx2a = b.sbuf("idx2a", [128, 16], F32)
        g1a = b.sbuf("g1a", [128, 16], F32)
        g2a = b.sbuf("g2a", [128, 16], F32)
        tr16 = b.sbuf("tr16", [16, 128], F32)
        ind1T = b.sbuf("ind1T", [8, T], F32)
        ind2T = b.sbuf("ind2T", [8, T], F32)
        zrow = b.sbuf("zrow", [8, T], F32)
        excl1 = b.sbuf("excl1", [8, T], F32)
        excl2 = b.sbuf("excl2", [8, T], F32)
        prod = b.sbuf("prod", [8, T], F32)
        r1row = b.sbuf("r1row", [1, T], F32)
        r2row = b.sbuf("r2row", [1, T], F32)
        cnt8 = b.sbuf("cnt8", [8, 1], F32)
        cntr = b.sbuf("cntr", [1, 8], F32)
        ident = b.sbuf("identsb", [128, 128], F32)
        ones_row = b.sbuf("ones_row", [1, 128], F32)
        ones_col = b.sbuf("ones_col", [128, 1], F32)
        sc1 = b.sbuf("sc1", [1, NCORES], F32)
        gmin1 = b.sbuf("gmin1", [1, 1], F32)
        gmax1 = b.sbuf("gmax1", [1, 1], F32)
        scl = b.sbuf("scl", [1, 1], F32)
        zpt = b.sbuf("zpt", [1, 1], F32)
        invs_col = b.sbuf("invs_col", [128, 1], F32)
        zp_col = b.sbuf("zp_col", [128, 1], F32)
        comb_col = b.sbuf("comb_col", [128, 1], F32)
        wmaxc = b.sbuf("wmaxc", [128, 1], F32)
        wrow16 = b.sbuf("wrow16", [1, 128], F32)
        wst = [b.sbuf(f"wst{i}", [128, 4096], F32) for i in range(2)]
        wqt = [b.sbuf(f"wqt4b{i}", [128, 4096], BF16) for i in range(2)]
        xqz = [b.sbuf(f"xqz{i}", [128, T], BF16) for i in range(NCH)]
        stx = [b.sbuf(f"stx{i}", [128, T], F32) for i in range(2)]
        wcol = [b.sbuf(f"wcol4b{i}", [128, D], BF16) for i in range(2)]
        h2t = b.sbuf("h2t", [128, T], F32)
        h2b = b.sbuf("h2b", [128, T], BF16)
        mxc = b.sbuf("mxc", [128, 32], F32)
        mnc = b.sbuf("mnc", [128, 32], F32)
        mmr = b.sbuf("mmr", [1, 128], F32)
        mmo = b.sbuf("mmo", [1, 2], F32)
        b1c = b.sbuf("b1c", [128, 32], F32)
        PS = [b.psum(f"PS{i}", [128, 512], F32) for i in range(8)]

        b.dma("sync", ident[:], ident_in)
        b.dma("sync", b1c[:], b1_in)
        b.op("vector", lambda v: v.memset(ones_row[:], 1.0), writes=[ones_row[:]])
        b.op("vector", lambda v: v.memset(ones_col[:], 1.0), writes=[ones_col[:]])
        b.op("vector", lambda v: v.memset(zrow[:], 0.0), writes=[zrow[:]])
        for i in range(NCH):
            b.dma("gpsimd", rw_sb[i][:], rw_in[128 * i:128 * (i + 1), :])
        iot = b.sbuf("iot", [1, 8], F32)
        b.dma("sync", iot[:], iota_in)
        rbr = b.sbuf("rbr", [1, 8], F32)
        b.dma("sync", rbr[:], rb_in)
        h_bcast_row(b, PS[0], rbbc, rbr, ones_row, 8)
        h_bcast_row(b, PS[0], bc8, iot, ones_row, 8)

        for tt in range(16):
            xb = xcb[tt % 2]
            b.dma("sync", xb[:].rearrange("p (c t) -> p c t", c=NCH),
                  xr2_in[:, 128 * tt:128 * (tt + 1)].rearrange("(c p) t -> p c t", p=128))
            ps = PS[1]
            for i in range(NCH):
                b.op("tensor", lambda t, i=i, ps=ps, xb=xb: t.matmul(
                    ps[:, 0:8], xb[:, 128 * i:128 * (i + 1)], rw_sb[i][:],
                    start=(i == 0), stop=(i == NCH - 1)),
                    reads=[xb[:], rw_sb[i][:]], writes=[ps[:, 0:8]])
            b.op("vector", lambda v, ps=ps: v.tensor_tensor(out=lg[:], in0=ps[:, 0:8],
                 in1=rbbc[:], op=OP.add), reads=[ps[:, 0:8], rbbc[:]], writes=[lg[:]])
            b.op("vector", lambda v: v.reduce_max(out=m1[:], in_=lg[:],
                 axis=mybir.AxisListType.X), reads=[lg[:]], writes=[m1[:]])
            b.op("vector", lambda v: v.tensor_scalar(out=negm[:], in0=m1[:],
                 scalar1=-1.0, scalar2=None, op0=OP.mult), reads=[m1[:]], writes=[negm[:]])
            b.op("scalar", lambda s: s.activation(out=lg[:], in_=lg[:], func=AF.Exp,
                 bias=negm[:]), reads=[lg[:], negm[:]], writes=[lg[:]])
            b.op("vector", lambda v: v.reduce_sum(out=ssum[:], in_=lg[:],
                 axis=mybir.AxisListType.X), reads=[lg[:]], writes=[ssum[:]])
            b.op("vector", lambda v: v.reciprocal(out=ssum[:], in_=ssum[:]),
                 reads=[ssum[:]], writes=[ssum[:]])
            b.op("vector", lambda v: v.tensor_scalar(out=lg[:], in0=lg[:],
                 scalar1=ssum[:], scalar2=None, op0=OP.mult),
                 reads=[lg[:], ssum[:]], writes=[lg[:]])
            b.op("vector", lambda v: v.reduce_max(out=m1[:], in_=lg[:],
                 axis=mybir.AxisListType.X), reads=[lg[:]], writes=[m1[:]])
            b.op("vector", lambda v: v.tensor_scalar(out=eq1[:], in0=lg[:],
                 scalar1=m1[:], scalar2=None, op0=OP.is_ge),
                 reads=[lg[:], m1[:]], writes=[eq1[:]])
            b.op("vector", lambda v: v.scalar_tensor_tensor(out=tsc1[:], in0=eq1[:],
                 scalar=-9.0, in1=bc8[:], op0=OP.mult, op1=OP.mult),
                 reads=[eq1[:], bc8[:]], writes=[tsc1[:]])
            b.op("vector", lambda v: v.tensor_tensor(out=tsc1[:], in0=eq1[:],
                 in1=bc8[:], op=OP.mult), reads=[eq1[:], bc8[:]], writes=[tsc1[:]])
            b.op("vector", lambda v: v.tensor_scalar(out=p2[:], in0=eq1[:],
                 scalar1=-9.0, scalar2=9.0, op0=OP.mult, op1=OP.add),
                 reads=[eq1[:]], writes=[p2[:]])
            b.op("vector", lambda v: v.tensor_tensor(out=tsc1[:], in0=tsc1[:],
                 in1=p2[:], op=OP.add), reads=[tsc1[:], p2[:]], writes=[tsc1[:]])
            b.op("vector", lambda v: v.tensor_reduce(out=idx1c[:], in_=tsc1[:],
                 axis=mybir.AxisListType.X, op=OP.min), reads=[tsc1[:]], writes=[idx1c[:]])
            b.op("vector", lambda v: v.tensor_scalar(out=ind1[:], in0=bc8[:],
                 scalar1=idx1c[:], scalar2=None, op0=OP.is_equal),
                 reads=[bc8[:], idx1c[:]], writes=[ind1[:]])
            b.op("vector", lambda v: v.scalar_tensor_tensor(out=p2[:], in0=ind1[:],
                 scalar=-1e30, in1=lg[:], op0=OP.mult, op1=OP.add),
                 reads=[ind1[:], lg[:]], writes=[p2[:]])
            b.op("vector", lambda v: v.reduce_max(out=m2[:], in_=p2[:],
                 axis=mybir.AxisListType.X), reads=[p2[:]], writes=[m2[:]])
            b.op("vector", lambda v: v.tensor_scalar(out=eq1[:], in0=p2[:],
                 scalar1=m2[:], scalar2=None, op0=OP.is_ge),
                 reads=[p2[:], m2[:]], writes=[eq1[:]])
            b.op("vector", lambda v: v.tensor_tensor(out=tsc1[:], in0=eq1[:],
                 in1=bc8[:], op=OP.mult), reads=[eq1[:], bc8[:]], writes=[tsc1[:]])
            b.op("vector", lambda v: v.tensor_scalar(out=p2[:], in0=eq1[:],
                 scalar1=-9.0, scalar2=9.0, op0=OP.mult, op1=OP.add),
                 reads=[eq1[:]], writes=[p2[:]])
            b.op("vector", lambda v: v.tensor_tensor(out=tsc1[:], in0=tsc1[:],
                 in1=p2[:], op=OP.add), reads=[tsc1[:], p2[:]], writes=[tsc1[:]])
            b.op("vector", lambda v: v.tensor_reduce(out=idx2c[:], in_=tsc1[:],
                 axis=mybir.AxisListType.X, op=OP.min), reads=[tsc1[:]], writes=[idx2c[:]])
            b.op("vector", lambda v: v.tensor_scalar(out=ind2[:], in0=bc8[:],
                 scalar1=idx2c[:], scalar2=None, op0=OP.is_equal),
                 reads=[bc8[:], idx2c[:]], writes=[ind2[:]])
            b.op("vector", lambda v: v.tensor_tensor(out=ssum[:], in0=m1[:],
                 in1=m2[:], op=OP.add), reads=[m1[:], m2[:]], writes=[ssum[:]])
            b.op("vector", lambda v: v.reciprocal(out=ssum[:], in_=ssum[:]),
                 reads=[ssum[:]], writes=[ssum[:]])
            b.op("vector", lambda v, tt=tt: v.tensor_tensor(out=g1a[:, tt:tt + 1],
                 in0=m1[:], in1=ssum[:], op=OP.mult),
                 reads=[m1[:], ssum[:]], writes=[g1a[:, tt:tt + 1]])
            b.op("vector", lambda v, tt=tt: v.tensor_tensor(out=g2a[:, tt:tt + 1],
                 in0=m2[:], in1=ssum[:], op=OP.mult),
                 reads=[m2[:], ssum[:]], writes=[g2a[:, tt:tt + 1]])
            b.op("vector", lambda v, tt=tt: v.tensor_copy(idx1a[:, tt:tt + 1], idx1c[:]),
                 reads=[idx1c[:]], writes=[idx1a[:, tt:tt + 1]])
            b.op("vector", lambda v, tt=tt: v.tensor_copy(idx2a[:, tt:tt + 1], idx2c[:]),
                 reads=[idx2c[:]], writes=[idx2a[:, tt:tt + 1]])
            for ind, dst in [(ind1, ind1T), (ind2, ind2T)]:
                b.op("tensor", lambda t, ind=ind: t.transpose(
                    PS[2][0:8, 0:128], ind[:], ident[:]),
                    reads=[ind[:], ident[:]], writes=[PS[2][0:8, 0:128]])
                b.op("vector", lambda v, dst=dst, tt=tt: v.tensor_copy(
                    dst[:, 128 * tt:128 * (tt + 1)], PS[2][0:8, 0:128]),
                    reads=[PS[2][0:8, 0:128]], writes=[dst[:, 128 * tt:128 * (tt + 1)]])
        b.op("vector", lambda v: v.tensor_tensor_scan(out=excl1[:], data0=ind1T[:],
             data1=zrow[:], initial=0.0, op0=OP.add, op1=OP.add),
             reads=[ind1T[:], zrow[:]], writes=[excl1[:]])
        b.op("vector", lambda v: v.tensor_tensor_scan(out=excl2[:], data0=ind2T[:],
             data1=zrow[:], initial=0.0, op0=OP.add, op1=OP.add),
             reads=[ind2T[:], zrow[:]], writes=[excl2[:]])
        b.op("vector", lambda v: v.tensor_tensor(out=cnt8[:], in0=excl1[:, T - 1:T],
             in1=excl2[:, T - 1:T], op=OP.add),
             reads=[excl1[:], excl2[:]], writes=[cnt8[:]])
        b.op("tensor", lambda t: t.transpose(PS[2][0:1, 0:8], cnt8[:, 0:1],
             ident[:, 0:8] if False else ident[0:8, 0:8]),
             reads=[cnt8[:], ident[:]], writes=[PS[2][0:1, 0:8]])
        b.op("vector", lambda v: v.tensor_copy(cntr[:], PS[2][0:1, 0:8]),
             reads=[PS[2][0:1, 0:8]], writes=[cntr[:]])
        b.dma("sync", cnt_o, cntr[:])
        b.op("vector", lambda v: v.tensor_tensor(out=excl1[:], in0=excl1[:],
             in1=ind1T[:], op=OP.subtract), reads=[excl1[:], ind1T[:]], writes=[excl1[:]])
        b.op("vector", lambda v: v.tensor_tensor(out=excl2[:], in0=excl2[:],
             in1=ind2T[:], op=OP.subtract), reads=[excl2[:], ind2T[:]], writes=[excl2[:]])
        b.op("vector", lambda v: v.tensor_tensor(out=prod[:], in0=excl1[:],
             in1=excl2[:], op=OP.add), reads=[excl1[:], excl2[:]], writes=[prod[:]])
        for indT, rrow, rout in [(ind1T, r1row, rr1_o), (ind2T, r2row, rr2_o)]:
            sel = b.sbuf(f"sel_{rrow.name if hasattr(rrow,'name') else id(rrow)}", [8, T], F32)
            b.op("vector", lambda v, indT=indT, sel=sel: v.tensor_tensor(out=sel[:],
                 in0=indT[:], in1=prod[:], op=OP.mult),
                 reads=[indT[:], prod[:]], writes=[sel[:]])
            for cs in range(4):
                b.op("tensor", lambda t, sel=sel, cs=cs: t.matmul(
                    PS[3][0:1, :], ones_col[0:8, 0:1], sel[:, 512 * cs:512 * (cs + 1)],
                    start=True, stop=True),
                    reads=[sel[:], ones_col[:]], writes=[PS[3][0:1, :]])
                b.op("vector", lambda v, rrow=rrow, cs=cs: v.tensor_copy(
                    rrow[0:1, 512 * cs:512 * (cs + 1)], PS[3][0:1, :]),
                    reads=[PS[3][0:1, :]], writes=[rrow[0:1, 512 * cs:512 * (cs + 1)]])
            b.dma("sync", rout, rrow[:])
        for arr, out in [(idx1a, ridx1_o), (idx2a, ridx2_o), (g1a, rg1_o), (g2a, rg2_o)]:
            b.op("tensor", lambda t, arr=arr: t.transpose(PS[2][0:16, 0:128], arr[:],
                 ident[:]), reads=[arr[:], ident[:]], writes=[PS[2][0:16, 0:128]])
            b.op("vector", lambda v: v.tensor_copy(tr16[:], PS[2][0:16, 0:128]),
                 reads=[PS[2][0:16, 0:128]], writes=[tr16[:]])
            b.dma("sync", out, tr16[:])

        b.dma("sync", sc1[:], mins_in)
        b.op("vector", lambda v: v.tensor_reduce(out=gmin1[:], in_=sc1[:],
             axis=mybir.AxisListType.X, op=OP.min), reads=[sc1[:]], writes=[gmin1[:]])
        b.dma("sync", sc1[:], maxs_in)
        b.op("vector", lambda v: v.reduce_max(out=gmax1[:], in_=sc1[:],
             axis=mybir.AxisListType.X), reads=[sc1[:]], writes=[gmax1[:]])
        b.op("vector", lambda v: v.tensor_tensor(out=scl[:], in0=gmax1[:],
             in1=gmin1[:], op=OP.subtract), reads=[gmax1[:], gmin1[:]], writes=[scl[:]])
        b.op("vector", lambda v: v.tensor_scalar(out=scl[:], in0=scl[:],
             scalar1=1.0 / 255.0, scalar2=1e-8, op0=OP.mult, op1=OP.max),
             reads=[scl[:]], writes=[scl[:]])
        b.op("vector", lambda v: v.reciprocal(out=zpt[:], in_=scl[:]),
             reads=[scl[:]], writes=[zpt[:]])
        b.op("vector", lambda v: v.tensor_tensor(out=gmin1[:], in0=gmin1[:],
             in1=zpt[:], op=OP.mult), reads=[gmin1[:], zpt[:]], writes=[gmin1[:]])
        b.op("vector", lambda v: v.tensor_scalar(out=gmin1[:], in0=gmin1[:],
             scalar1=-1.0, scalar2=RC, op0=OP.mult, op1=OP.add),
             reads=[gmin1[:]], writes=[gmin1[:]])
        b.op("vector", lambda v: v.tensor_scalar(out=gmin1[:], in0=gmin1[:],
             scalar1=-RC, scalar2=-128.0, op0=OP.add, op1=OP.add),
             reads=[gmin1[:]], writes=[gmin1[:]])
        b.op("vector", lambda v: v.tensor_scalar(out=gmin1[:], in0=gmin1[:],
             scalar1=127.0, scalar2=-128.0, op0=OP.min, op1=OP.max),
             reads=[gmin1[:]], writes=[gmin1[:]])
        b.op("tensor", lambda t: t.matmul(PS[0][:, 0:1], ones_row[:], zpt[:],
             start=True, stop=True), reads=[ones_row[:], zpt[:]], writes=[PS[0][:, 0:1]])
        b.op("vector", lambda v: v.tensor_copy(invs_col[:], PS[0][:, 0:1]),
             reads=[PS[0][:, 0:1]], writes=[invs_col[:]])
        b.op("tensor", lambda t: t.matmul(PS[0][:, 0:1], ones_row[:], gmin1[:],
             start=True, stop=True), reads=[ones_row[:], gmin1[:]], writes=[PS[0][:, 0:1]])
        b.op("vector", lambda v: v.tensor_copy(zp_col[:], PS[0][:, 0:1]),
             reads=[PS[0][:, 0:1]], writes=[zp_col[:]])
        for i in range(NCH):
            xb = stx[i % 2]
            b.dma("sync", xb[:], xn1_in[128 * i:128 * (i + 1), :])
            b.op("vector", lambda v, xb=xb: v.tensor_scalar(out=xb[:], in0=xb[:],
                 scalar1=invs_col[:], scalar2=RC, op0=OP.mult, op1=OP.add),
                 reads=[xb[:], invs_col[:]], writes=[xb[:]])
            b.op("vector", lambda v, xb=xb: v.tensor_scalar(out=xb[:], in0=xb[:],
                 scalar1=-RC, scalar2=zp_col[:], op0=OP.add, op1=OP.add),
                 reads=[xb[:], zp_col[:]], writes=[xb[:]])
            b.op("vector", lambda v, xb=xb: v.tensor_scalar(out=xb[:], in0=xb[:],
                 scalar1=127.0, scalar2=-128.0, op0=OP.min, op1=OP.max),
                 reads=[xb[:]], writes=[xb[:]])
            b.op("vector", lambda v, xb=xb, i=i: v.tensor_scalar(out=xqz[i][:], in0=xb[:],
                 scalar1=zp_col[:], scalar2=None, op0=OP.subtract),
                 reads=[xb[:], zp_col[:]], writes=[xqz[i][:]])
        for i in range(NCH):
            ws = wst[i % 2]
            b.dma("sync", ws[:], w1_in[128 * i:128 * (i + 1), :])
            b.op("scalar", lambda s, ws=ws: s.activation(out=ws[:], in_=ws[:],
                 func=AF.Abs), reads=[ws[:]], writes=[ws[:]])
            if i == 0:
                b.op("vector", lambda v, ws=ws: v.reduce_max(out=wmaxc[:], in_=ws[:],
                     axis=mybir.AxisListType.X), reads=[ws[:]], writes=[wmaxc[:]])
            else:
                b.op("vector", lambda v, ws=ws: v.reduce_max(out=m1[:], in_=ws[:],
                     axis=mybir.AxisListType.X), reads=[ws[:]], writes=[m1[:]])
                b.op("vector", lambda v: v.tensor_tensor(out=wmaxc[:], in0=wmaxc[:],
                     in1=m1[:], op=OP.max), reads=[wmaxc[:], m1[:]], writes=[wmaxc[:]])
        b.op("tensor", lambda t: t.transpose(PS[0][0:1, 0:128], wmaxc[:], ident[:]),
             reads=[wmaxc[:], ident[:]], writes=[PS[0][0:1, 0:128]])
        b.op("vector", lambda v: v.tensor_copy(wrow16[:], PS[0][0:1, 0:128]),
             reads=[PS[0][0:1, 0:128]], writes=[wrow16[:]])
        b.op("vector", lambda v: v.reduce_max(out=wrow16[0:1, 0:1], in_=wrow16[:],
             axis=mybir.AxisListType.X), reads=[wrow16[:]], writes=[wrow16[:]])
        b.op("vector", lambda v: v.tensor_scalar(out=wrow16[0:1, 0:1],
             in0=wrow16[0:1, 0:1], scalar1=1.0 / 127.0, scalar2=1e-8,
             op0=OP.mult, op1=OP.max), reads=[wrow16[:]], writes=[wrow16[:]])
        b.op("vector", lambda v: v.tensor_tensor(out=scl[:], in0=scl[:],
             in1=wrow16[0:1, 0:1], op=OP.mult), reads=[scl[:], wrow16[:]], writes=[scl[:]])
        b.op("tensor", lambda t: t.matmul(PS[0][:, 0:1], ones_row[:], scl[:],
             start=True, stop=True), reads=[ones_row[:], scl[:]], writes=[PS[0][:, 0:1]])
        b.op("vector", lambda v: v.tensor_copy(comb_col[:], PS[0][:, 0:1]),
             reads=[PS[0][:, 0:1]], writes=[comb_col[:]])
        b.op("vector", lambda v: v.reciprocal(out=m2[:], in_=wrow16[0:1, 0:1].broadcast_to([1, 1])),
             reads=[wrow16[:]], writes=[m2[:]]) if False else None
        invsw = b.sbuf("invsw", [1, 1], F32)
        b.op("vector", lambda v: v.reciprocal(out=invsw[:], in_=wrow16[0:1, 0:1]),
             reads=[wrow16[:]], writes=[invsw[:]])
        invswc = b.sbuf("invswc", [128, 1], F32)
        b.op("tensor", lambda t: t.matmul(PS[0][:, 0:1], ones_row[:], invsw[:],
             start=True, stop=True), reads=[ones_row[:], invsw[:]], writes=[PS[0][:, 0:1]])
        b.op("vector", lambda v: v.tensor_copy(invswc[:], PS[0][:, 0:1]),
             reads=[PS[0][:, 0:1]], writes=[invswc[:]])
        for i in range(NCH):
            ws = wst[i % 2]
            wq_ = wqt[i % 2]
            b.dma("sync", ws[:], w1_in[128 * i:128 * (i + 1), :])
            b.op("vector", lambda v, ws=ws: v.tensor_scalar(out=ws[:], in0=ws[:],
                 scalar1=invswc[:], scalar2=RC, op0=OP.mult, op1=OP.add),
                 reads=[ws[:], invswc[:]], writes=[ws[:]])
            b.op("vector", lambda v, ws=ws: v.tensor_scalar(out=ws[:], in0=ws[:],
                 scalar1=-RC, scalar2=127.0, op0=OP.add, op1=OP.min),
                 reads=[ws[:]], writes=[ws[:]])
            b.op("vector", lambda v, ws=ws, wq_=wq_: v.tensor_scalar(out=wq_[:], in0=ws[:],
                 scalar1=-127.0, scalar2=None, op0=OP.max),
                 reads=[ws[:]], writes=[wq_[:]])
            b.dma("sync", wq1_dram[128 * i:128 * (i + 1), :], wq_[:])
        for ot in range(32):
            wc = wcol[ot % 2]
            src2 = wq1_dram[:, 128 * ot:128 * (ot + 1)].rearrange("(c p) j -> p c j", p=128)
            b.dma("sync", wc[:].rearrange("p (c j) -> p c j", c=NCH), src2)
            for cs in range(4):
                ps = PS[4 + cs % 2]
                sl = slice(512 * cs, 512 * (cs + 1))
                for i in range(NCH):
                    b.op("tensor", lambda t, i=i, sl=sl, ps=ps, wc=wc: t.matmul(
                        ps[:, :], wc[:, 128 * i:128 * (i + 1)], xqz[i][:, sl],
                        start=(i == 0), stop=(i == NCH - 1)),
                        reads=[wc[:, 128 * i:128 * (i + 1)], xqz[i][:, sl]],
                        writes=[ps[:, :]])
                b.op("scalar", lambda s, sl=sl, ps=ps, ot=ot: s.activation(
                    out=h2t[:, sl], in_=ps[:, :], func=AF.Gelu,
                    scale=comb_col[:], bias=b1c[:, ot:ot + 1]),
                    reads=[ps[:, :], comb_col[:], b1c[:, ot:ot + 1]], writes=[h2t[:, sl]])
            b.op("vector", lambda v: v.tensor_copy(h2b[:], h2t[:]),
                 reads=[h2t[:]], writes=[h2b[:]])
            b.op("vector", lambda v, ot=ot: v.reduce_max(out=mxc[:, ot:ot + 1],
                 in_=h2b[:], axis=mybir.AxisListType.X), reads=[h2b[:]],
                 writes=[mxc[:, ot:ot + 1]])
            b.op("vector", lambda v, ot=ot: v.tensor_reduce(out=mnc[:, ot:ot + 1],
                 in_=h2b[:], axis=mybir.AxisListType.X, op=OP.min), reads=[h2b[:]],
                 writes=[mnc[:, ot:ot + 1]])
            b.dma("sync", h2_o[128 * ot:128 * (ot + 1), :], h2b[:])
        b.op("vector", lambda v: v.reduce_max(out=mxc[:, 0:1], in_=mxc[:],
             axis=mybir.AxisListType.X), reads=[mxc[:]], writes=[mxc[:, 0:1]])
        b.op("vector", lambda v: v.tensor_reduce(out=mnc[:, 0:1], in_=mnc[:],
             axis=mybir.AxisListType.X, op=OP.min), reads=[mnc[:]], writes=[mnc[:, 0:1]])
        b.op("tensor", lambda t: t.transpose(PS[0][0:1, 0:128], mxc[:, 0:1], ident[:]),
             reads=[mxc[:], ident[:]], writes=[PS[0][0:1, 0:128]])
        b.op("vector", lambda v: v.tensor_copy(mmr[:], PS[0][0:1, 0:128]),
             reads=[PS[0][0:1, 0:128]], writes=[mmr[:]])
        b.op("vector", lambda v: v.reduce_max(out=mmo[0:1, 1:2], in_=mmr[:],
             axis=mybir.AxisListType.X), reads=[mmr[:]], writes=[mmo[:]])
        b.op("tensor", lambda t: t.transpose(PS[0][0:1, 0:128], mnc[:, 0:1], ident[:]),
             reads=[mnc[:], ident[:]], writes=[PS[0][0:1, 0:128]])
        b.op("vector", lambda v: v.tensor_copy(mmr[:], PS[0][0:1, 0:128]),
             reads=[PS[0][0:1, 0:128]], writes=[mmr[:]])
        b.op("vector", lambda v: v.tensor_reduce(out=mmo[0:1, 0:1], in_=mmr[:],
             axis=mybir.AxisListType.X, op=OP.min), reads=[mmr[:]], writes=[mmo[:]])
        b.dma("sync", mmh2_o, mmo[:])

        cmp_part = b.dram("cmp_part", [16384, D], BF16, kind="ExternalOutput").ap()
        rr_scr = b.dram("rr_scr", [2, T], F32).ap()
        b.dma("sync", rr_scr[0:1, :], r1row[:])
        b.dma("sync", rr_scr[1:2, :], r2row[:])
        rr1cf = b.sbuf("rr1cf", [128, 16], F32)
        rr2cf = b.sbuf("rr2cf", [128, 16], F32)
        b.dma("sync", rr1cf[:], rr_scr[0:1, :].rearrange("a (j p) -> p (a j)", p=128),
              allow_slow_non_contiguous=True)
        b.dma("sync", rr2cf[:], rr_scr[1:2, :].rearrange("a (j p) -> p (a j)", p=128),
              allow_slow_non_contiguous=True)
        dd1 = b.sbuf("dd1", [128, 16], F32)
        dd2 = b.sbuf("dd2", [128, 16], F32)
        di1 = b.sbuf("di1", [128, 16], U32)
        di2 = b.sbuf("di2", [128, 16], U32)
        b.op("vector", lambda v: v.scalar_tensor_tensor(out=dd1[:], in0=idx1a[:],
             scalar=2048.0, in1=rr1cf[:], op0=OP.mult, op1=OP.add),
             reads=[idx1a[:], rr1cf[:]], writes=[dd1[:]])
        b.op("vector", lambda v: v.scalar_tensor_tensor(out=dd2[:], in0=idx2a[:],
             scalar=2048.0, in1=rr2cf[:], op0=OP.mult, op1=OP.add),
             reads=[idx2a[:], rr2cf[:]], writes=[dd2[:]])
        b.op("vector", lambda v: v.tensor_copy(di1[:], dd1[:]),
             reads=[dd1[:]], writes=[di1[:]])
        b.op("vector", lambda v: v.tensor_copy(di2[:], dd2[:]),
             reads=[dd2[:]], writes=[di2[:]])
        identb_in4 = b.dram("identb", [128, 128], BF16, kind="ExternalInput").ap()
        identb4 = b.sbuf("identb4", [128, 128], BF16)
        b.dma("sync", identb4[:], identb_in4)
        xtm = b.sbuf("xtm", [128, D], BF16)
        for tt in range(16):
            xb = xcb[tt % 2]
            b.dma("sync", xb[:].rearrange("p (c t) -> p c t", c=NCH),
                  xr2_in[:, 128 * tt:128 * (tt + 1)].rearrange("(c p) t -> p c t", p=128))
            for ch in range(NCH):
                pbank = PS[ch % 2]
                b.op("tensor", lambda t, ch=ch, xb=xb, pbank=pbank: t.matmul(
                    pbank[:, 0:128], xb[:, 128 * ch:128 * (ch + 1)], identb4[:],
                    start=True, stop=True),
                    reads=[xb[:], identb4[:]], writes=[pbank[:, 0:128]])
                b.op("vector", lambda v, ch=ch, pbank=pbank: v.tensor_copy(
                    xtm[:, 128 * ch:128 * (ch + 1)], pbank[:, 0:128]),
                    reads=[pbank[:, 0:128]], writes=[xtm[:, 128 * ch:128 * (ch + 1)]])
            b.idma(cmp_part, xtm[:], di1[:, tt:tt + 1], "out", 16383,
                   reads=[xtm[:]], writes=[cmp_part])
            b.idma(cmp_part, xtm[:], di2[:, tt:tt + 1], "out", 16383,
                   reads=[xtm[:]], writes=[cmp_part])
        b.finalize()
    return nc



NCAP = 5120

def build_L5():
    nc = bass.Bass("TRN2", target_bir_lowering=False)
    with ExitStack() as ctx:
        b = TB(nc, ctx)
        cmpmy_in = b.dram("cmp_my", [8 * 2048, D], BF16, kind="ExternalInput").ap()
        cnt_in = b.dram("cnts", [8, 8], F32, kind="ExternalInput").ap()
        w1_in = b.dram("w1e", [D, 4096], F32, kind="ExternalInput").ap()
        w2_in = b.dram("w2e", [4096, D], F32, kind="ExternalInput").ap()
        b1_in = b.dram("b1e_col", [128, 32], F32, kind="ExternalInput").ap()
        b2_in = b.dram("b2e_col", [128, 8], F32, kind="ExternalInput").ap()
        my_in = b.dram("myexp_col", [128, 1], F32, kind="ExternalInput").ap()
        ltri_in = b.dram("ltri8", [8, 8], F32, kind="ExternalInput").ap()
        ident_in = b.dram("ident", [128, 128], F32, kind="ExternalInput").ap()
        identb_in = b.dram("identb", [128, 128], BF16, kind="ExternalInput").ap()
        cmp_out = b.dram("cmp_out", [NCAP, D], BF16, kind="ExternalOutput").ap()

        myc = b.sbuf("myc", [128, 1], F32)
        cnt8 = b.sbuf("cnt8sb", [8, 8], F32)
        ltri = b.sbuf("ltri", [8, 8], F32)
        ident = b.sbuf("identsb", [128, 128], F32)
        identb = b.sbuf("identbb", [128, 128], BF16)
        ones_row = b.sbuf("ones_row", [1, 128], F32)
        prefc = b.sbuf("prefc", [8, 1], F32)
        NGT = NCAP // 128
        riota = b.sbuf("riota", [128, NGT], F32)
        stcols = b.sbuf("stcols", [128, 8], F32)
        cnc = b.sbuf("cnc", [128, 8], F32)
        totc = b.sbuf("totc", [128, 1], F32)
        geq = b.sbuf("geq", [128, NGT], F32)
        aci = b.sbuf("aci", [128, NGT], F32)
        ass = b.sbuf("ass", [128, NGT], F32)
        lrb = b.sbuf("lrb", [128, NGT], F32)
        srcb = b.sbuf("srcb", [128, NGT], F32)
        vld = b.sbuf("vld", [128, NGT], F32)
        t2b = b.sbuf("t2b", [128, NGT], F32)
        didxg = b.sbuf("didxg", [128, NGT], U32)
        xtile = [b.sbuf(f"xtile{i}", [128, D], BF16) for i in range(2)]
        wcol = [b.sbuf(f"wcol5{i}", [128, 1024], BF16) for i in range(2)]
        wcol2 = [b.sbuf(f"wcol5b{i}", [128, 4096], BF16) for i in range(2)]
        b1c = b.sbuf("b1c5", [128, 32], F32)
        b2c = b.sbuf("b2c5", [128, 8], F32)
        o2bs2 = [b.sbuf(f"o2bs2_{i}", [128, 4096], BF16) for i in range(2)]
        otile = b.sbuf("otile", [128, D], BF16)
        PS = [b.psum(f"PS{i}", [128, 512], F32) for i in range(7)]
        PSB = b.psum("PSB", [128, 256], BF16)

        b.dma("sync", ident[:], ident_in)
        b.dma("sync", identb[:], identb_in)
        b.dma("sync", myc[:], my_in)
        b.dma("sync", cnt8[:], cnt_in)
        b.dma("sync", ltri[:], ltri_in)
        b.dma("sync", b1c[:], b1_in)
        b.dma("sync", b2c[:], b2_in)
        b.op("vector", lambda v: v.memset(ones_row[:], 1.0), writes=[ones_row[:]])
        mysel = b.sbuf("mysel", [8, 8], F32)
        iot8 = b.sbuf("iot8", [1, 8], F32)
        b.op("gpsimd", lambda g: g.iota(iot8[:], pattern=[[1, 8]], base=0,
             channel_multiplier=0, allow_small_or_imprecise_dtypes=True),
             writes=[iot8[:]])
        iot8bc = b.sbuf("iot8bc", [8, 8], F32)
        b.op("tensor", lambda t: t.matmul(PS[0][0:8, 0:8], ones_row[0:1, 0:8], iot8[:],
             start=True, stop=True), reads=[ones_row[:], iot8[:]], writes=[PS[0][0:8, 0:8]])
        b.op("vector", lambda v: v.tensor_copy(iot8bc[:], PS[0][0:8, 0:8]),
             reads=[PS[0][0:8, 0:8]], writes=[iot8bc[:]])
        b.op("vector", lambda v: v.tensor_scalar(out=iot8bc[:], in0=iot8bc[:],
             scalar1=myc[0:8, 0:1], scalar2=None, op0=OP.is_equal),
             reads=[iot8bc[:], myc[:]], writes=[iot8bc[:]])
        b.op("vector", lambda v: v.tensor_tensor(out=mysel[:], in0=cnt8[:],
             in1=iot8bc[:], op=OP.mult), reads=[cnt8[:], iot8bc[:]], writes=[mysel[:]])
        mycnt = b.sbuf("mycnt", [8, 1], F32)
        b.op("vector", lambda v: v.reduce_sum(out=mycnt[:], in_=mysel[:],
             axis=mybir.AxisListType.X), reads=[mysel[:]], writes=[mycnt[:]])
        b.op("tensor", lambda t: t.matmul(PS[0][0:8, 0:1], ltri[:], mycnt[:],
             start=True, stop=True), reads=[ltri[:], mycnt[:]], writes=[PS[0][0:8, 0:1]])
        b.op("vector", lambda v: v.tensor_copy(prefc[:], PS[0][0:8, 0:1]),
             reads=[PS[0][0:8, 0:1]], writes=[prefc[:]])
        prefrow = b.sbuf("prefrow", [1, 8], F32)
        cntrow = b.sbuf("cntrow", [1, 8], F32)
        b.op("tensor", lambda t: t.transpose(PS[0][0:1, 0:8], prefc[:], ident[0:8, 0:8]),
             reads=[prefc[:], ident[:]], writes=[PS[0][0:1, 0:8]])
        b.op("vector", lambda v: v.tensor_copy(prefrow[:], PS[0][0:1, 0:8]),
             reads=[PS[0][0:1, 0:8]], writes=[prefrow[:]])
        b.op("tensor", lambda t: t.transpose(PS[0][0:1, 0:8], mycnt[:], ident[0:8, 0:8]),
             reads=[mycnt[:], ident[:]], writes=[PS[0][0:1, 0:8]])
        b.op("vector", lambda v: v.tensor_copy(cntrow[:], PS[0][0:1, 0:8]),
             reads=[PS[0][0:1, 0:8]], writes=[cntrow[:]])
        for c in range(8):
            h_bcast_row(b, PS[1], stcols[:, c:c + 1], prefrow[0:1, c:c + 1], ones_row, 1)
            h_bcast_row(b, PS[2], cnc[:, c:c + 1], cntrow[0:1, c:c + 1], ones_row, 1)
        tot1 = b.sbuf("tot1", [1, 1], F32)
        b.op("vector", lambda v: v.reduce_sum(out=tot1[:], in_=cntrow[:],
             axis=mybir.AxisListType.X), reads=[cntrow[:]], writes=[tot1[:]])
        h_bcast_row(b, PS[1], totc[:], tot1[:], ones_row, 1)
        b.op("gpsimd", lambda g: g.iota(riota[:], pattern=[[128, NGT]], base=0,
             channel_multiplier=1, allow_small_or_imprecise_dtypes=True),
             writes=[riota[:]])
        b.op("vector", lambda v: v.memset(aci[:], 0.0), writes=[aci[:]])
        b.op("vector", lambda v: v.memset(ass[:], 0.0), writes=[ass[:]])
        for c in range(1, 8):
            b.op("vector", lambda v, c=c: v.tensor_scalar(out=geq[:], in0=riota[:],
                 scalar1=stcols[:, c:c + 1], scalar2=None, op0=OP.is_ge),
                 reads=[riota[:], stcols[:, c:c + 1]], writes=[geq[:]])
            b.op("vector", lambda v: v.tensor_tensor(out=aci[:], in0=aci[:],
                 in1=geq[:], op=OP.add), reads=[aci[:], geq[:]], writes=[aci[:]])
            b.op("vector", lambda v, c=c: v.scalar_tensor_tensor(out=ass[:],
                 in0=geq[:], scalar=cnc[:, c - 1:c], in1=ass[:],
                 op0=OP.mult, op1=OP.add),
                 reads=[geq[:], cnc[:, c - 1:c], ass[:]], writes=[ass[:]])
        b.op("vector", lambda v: v.tensor_tensor(out=lrb[:], in0=riota[:],
             in1=ass[:], op=OP.subtract), reads=[riota[:], ass[:]], writes=[lrb[:]])
        b.op("vector", lambda v: v.scalar_tensor_tensor(out=srcb[:], in0=aci[:],
             scalar=2048.0, in1=lrb[:], op0=OP.mult, op1=OP.add),
             reads=[aci[:], lrb[:]], writes=[srcb[:]])
        b.op("vector", lambda v: v.tensor_scalar(out=vld[:], in0=riota[:],
             scalar1=totc[:, 0:1], scalar2=None, op0=OP.is_lt),
             reads=[riota[:], totc[:]], writes=[vld[:]])
        b.op("vector", lambda v: v.tensor_tensor(out=srcb[:], in0=srcb[:],
             in1=vld[:], op=OP.mult), reads=[srcb[:], vld[:]], writes=[srcb[:]])
        b.op("vector", lambda v: v.tensor_scalar(out=t2b[:], in0=vld[:],
             scalar1=-1.0e9, scalar2=1.0e9, op0=OP.mult, op1=OP.add),
             reads=[vld[:]], writes=[t2b[:]])
        b.op("vector", lambda v: v.tensor_tensor(out=srcb[:], in0=srcb[:],
             in1=t2b[:], op=OP.add), reads=[srcb[:], t2b[:]], writes=[srcb[:]])
        b.op("vector", lambda v: v.tensor_copy(didxg[:], srcb[:]),
             reads=[srcb[:]], writes=[didxg[:]])

        w1p_dram = b.dram("w1p_scr", [128, 32, 8, 128], BF16).ap()
        w2p_dram = b.dram("w2p_scr", [128, 8, 32, 128], BF16).ap()
        cst = [b.sbuf(f"cst{i}", [128, 4096], F32) for i in range(2)]
        cbf = [b.sbuf(f"cbf{i}", [128, 4096], BF16) for i in range(2)]
        for i in range(8):
            cs_ = cst[i % 2]
            cb_ = cbf[i % 2]
            b.dma("sync", cs_[:], w1_in[128 * i:128 * (i + 1), :])
            b.op("vector", lambda v, cs_=cs_, cb_=cb_: v.tensor_copy(cb_[:], cs_[:]),
                 reads=[cs_[:]], writes=[cb_[:]])
            b.dma("sync", w1p_dram[:, :, i, :],
                  cb_[:].rearrange("p (ht j) -> p ht j", ht=32))
        for i in range(32):
            cs_ = cst[i % 2]
            cb_ = cbf[i % 2]
            b.dma("sync", cs_[:, 0:D], w2_in[128 * i:128 * (i + 1), :])
            b.op("vector", lambda v, cs_=cs_, cb_=cb_: v.tensor_copy(
                 cb_[:, 0:D], cs_[:, 0:D]), reads=[cs_[:, 0:D]], writes=[cb_[:, 0:D]])
            b.dma("sync", w2p_dram[:, :, i, :],
                  cb_[:, 0:D].rearrange("p (ft j) -> p ft j", ft=8))

        xgp = [b.sbuf(f"xgp{i}", [128, 4096], BF16) for i in range(2)]
        hstp = [b.sbuf(f"hstp{i}", [128, 32 * 512], BF16) for i in range(2)]
        for pp in range(5):
            for g2 in range(2):
                grp = 2 * pp + g2
                r0g = 512 * grp
                for q4 in range(4):
                    j = 4 * grp + q4
                    xb = xtile[q4 % 2]
                    b.idma(xb[:], cmpmy_in, didxg[:, j:j + 1], "in", 8 * 2048 - 1,
                           reads=[cmpmy_in], writes=[xb[:]])
                    for ch in range(NCH):
                        b.op("tensor", lambda t, ch=ch, xb=xb: t.transpose(
                            PSB[:, 0:128], xb[:, 128 * ch:128 * (ch + 1)], identb[:]),
                            reads=[xb[:], identb[:]], writes=[PSB[:, 0:128]])
                        b.op("vector", lambda v, ch=ch, q4=q4, g2=g2: v.tensor_copy(
                            xgp[g2][:, 512 * ch + 128 * q4:512 * ch + 128 * (q4 + 1)],
                            PSB[:, 0:128]), reads=[PSB[:, 0:128]], writes=[xgp[g2][:]])
            for ht in range(32):
                wc = wcol[ht % 2]
                b.dma("sync", wc[:].rearrange("p (c j) -> p c j", c=NCH),
                      w1p_dram[:, ht, :, :])
                for g2 in range(2):
                    ps = PS[2 * (ht % 2) + g2]
                    for ch in range(NCH):
                        b.op("tensor", lambda t, ch=ch, ps=ps, wc=wc, g2=g2: t.matmul(
                            ps[:, :], wc[:, 128 * ch:128 * (ch + 1)],
                            xgp[g2][:, 512 * ch:512 * (ch + 1)],
                            start=(ch == 0), stop=(ch == NCH - 1)),
                            reads=[wc[:, 128 * ch:128 * (ch + 1)], xgp[g2][:]],
                            writes=[ps[:, :]])
                    b.op("scalar", lambda s, ps=ps, ht=ht, g2=g2: s.activation(
                        out=hstp[g2][:, 512 * ht:512 * (ht + 1)], in_=ps[:, :],
                        func=AF.Gelu, bias=b1c[:, ht:ht + 1]),
                        reads=[ps[:, :], b1c[:, ht:ht + 1]],
                        writes=[hstp[g2][:, 512 * ht:512 * (ht + 1)]])
            for ft in range(8):
                wc2 = wcol2[ft % 2]
                b.dma("sync", wc2[:].rearrange("p (c j) -> p c j", c=32),
                      w2p_dram[:, ft, :, :])
                for g2 in range(2):
                    ps = PS[2 * (ft % 2) + g2]
                    for ch in range(32):
                        b.op("tensor", lambda t, ch=ch, ps=ps, wc2=wc2, g2=g2: t.matmul(
                            ps[:, :], wc2[:, 128 * ch:128 * (ch + 1)],
                            hstp[g2][:, 512 * ch:512 * (ch + 1)],
                            start=(ch == 0), stop=(ch == 31)),
                            reads=[wc2[:, 128 * ch:128 * (ch + 1)], hstp[g2][:]],
                            writes=[ps[:, :]])
                    b.op("vector", lambda v, ps=ps, ft=ft, g2=g2: v.tensor_scalar(
                        out=o2bs2[g2][:, 512 * ft:512 * (ft + 1)], in0=ps[:, :],
                        scalar1=b2c[:, ft:ft + 1], scalar2=None, op0=OP.add),
                        reads=[ps[:, :], b2c[:, ft:ft + 1]],
                        writes=[o2bs2[g2][:, 512 * ft:512 * (ft + 1)]])
            for g2 in range(2):
                r0g = 512 * (2 * pp + g2)
                for q4 in range(4):
                    for ft in range(8):
                        b.op("tensor", lambda t, q4=q4, ft=ft, g2=g2: t.transpose(
                            PSB[:, 0:128],
                            o2bs2[g2][:, 512 * ft + 128 * q4:512 * ft + 128 * (q4 + 1)],
                            identb[:]), reads=[o2bs2[g2][:], identb[:]],
                            writes=[PSB[:, 0:128]])
                        b.op("vector", lambda v, ft=ft: v.tensor_copy(
                            otile[:, 128 * ft:128 * (ft + 1)], PSB[:, 0:128]),
                            reads=[PSB[:, 0:128]], writes=[otile[:, 128 * ft:128 * (ft + 1)]])
                    b.dma("sync", cmp_out[r0g + 128 * q4:r0g + 128 * (q4 + 1), :],
                          otile[:])
        b.finalize()
    return nc


def build_L6():
    nc = bass.Bass("TRN2", target_bir_lowering=False)
    with ExitStack() as ctx:
        b = TB(nc, ctx)
        x1_in = b.dram("x1_fm", [D, T], F32, kind="ExternalInput").ap()
        x2_in = b.dram("x2_fm", [D, T], F32, kind="ExternalInput").ap()
        h2_in = b.dram("h2_bf", [4096, T], BF16, kind="ExternalInput").ap()
        mins_in = b.dram("mins2", [1, NCORES], F32, kind="ExternalInput").ap()
        maxs_in = b.dram("maxs2", [1, NCORES], F32, kind="ExternalInput").ap()
        w2_in = b.dram("w2_ffn", [4096, D], F32, kind="ExternalInput").ap()
        b2f_in = b.dram("b2f_col", [128, 8], F32, kind="ExternalInput").ap()
        cmp_in = b.dram("cmp_all", [8 * NCAP, D], BF16, kind="ExternalInput").ap()
        cnt_in = b.dram("cnts", [8, 8], F32, kind="ExternalInput").ap()
        pm_in = b.dram("premask_col", [8, 1], F32, kind="ExternalInput").ap()
        idx1_in = b.dram("idx1_tm", [128, 16], F32, kind="ExternalInput").ap()
        idx2_in = b.dram("idx2_tm", [128, 16], F32, kind="ExternalInput").ap()
        rr1_in = b.dram("rr1_tm", [128, 16], F32, kind="ExternalInput").ap()
        rr2_in = b.dram("rr2_tm", [128, 16], F32, kind="ExternalInput").ap()
        g1_in = b.dram("g1_tm", [128, 16], F32, kind="ExternalInput").ap()
        g2_in = b.dram("g2_tm", [128, 16], F32, kind="ExternalInput").ap()
        ident_in = b.dram("ident", [128, 128], F32, kind="ExternalInput").ap()
        out_o = b.dram("out_fm", [D, T], F32, kind="ExternalOutput").ap()
        wq2_dram = b.dram("wq2_scr", [4096, D], BF16).ap()
        moe_dram = b.dram("moe_scr", [T, D], F32).ap()

        ident = b.sbuf("identsb", [128, 128], F32)
        ones_row = b.sbuf("ones_row", [1, 128], F32)
        sc1 = b.sbuf("sc1", [1, NCORES], F32)
        gmin1 = b.sbuf("gmin1", [1, 1], F32)
        gmax1 = b.sbuf("gmax1", [1, 1], F32)
        scl = b.sbuf("scl", [1, 1], F32)
        zpt = b.sbuf("zpt", [1, 1], F32)
        invs_col = b.sbuf("invs_col", [128, 1], F32)
        zp_col = b.sbuf("zp_col", [128, 1], F32)
        comb_col = b.sbuf("comb_col", [128, 1], F32)
        invsw = b.sbuf("invsw", [1, 1], F32)
        invswc = b.sbuf("invswc", [128, 1], F32)
        wmaxc = b.sbuf("wmaxc", [128, 1], F32)
        wrow16 = b.sbuf("wrow16", [1, 128], F32)
        m1 = b.sbuf("m1c", [128, 1], F32)
        cnt8 = b.sbuf("cnt8sb", [8, 8], F32)
        pmc = b.sbuf("pmc", [8, 1], F32)
        strow = b.sbuf("strow", [1, 8], F32)
        stecol = b.sbuf("stecol", [128, 8], F32)
        idxa = [b.sbuf(f"idxa{k}", [128, 16], F32) for k in range(2)]
        rra = [b.sbuf(f"rra{k}", [128, 16], F32) for k in range(2)]
        ga = [b.sbuf(f"ga{k}", [128, 16], F32) for k in range(2)]
        ssel = b.sbuf("ssel", [128, 16], F32)
        eqt = b.sbuf("eqt", [128, 16], F32)
        grk = b.sbuf("grk", [128, 16], F32)
        kpt = b.sbuf("kpt", [128, 16], F32)
        tmq = b.sbuf("tmq", [128, 16], F32)
        didx = [b.sbuf(f"didx{k}", [128, 16], U32) for k in range(2)]
        geff = [b.sbuf(f"geff{k}", [128, 16], F32) for k in range(2)]
        e1t = b.sbuf("e1t", [128, D], BF16)
        e2t = b.sbuf("e2t", [128, D], BF16)
        m1t = b.sbuf("m1t", [128, D], F32)
        hb = [b.sbuf("hb0", [128, T], BF16)] * 2
        ft32 = b.sbuf("ft32", [128, T], F32)
        xq2z = [b.sbuf(f"xq2z{i}", [128, T], BF16) for i in range(32)]
        wst = [b.sbuf("wst0", [128, D], F32)] * 2
        wqt = [b.sbuf(f"wqt{i}", [128, D], BF16) for i in range(2)]
        wqcol = [b.sbuf("wqcol0", [128, 4096], BF16)] * 2
        b2c = b.sbuf("b2c6", [128, 8], F32)
        g2ch = b.sbuf("g2ch", [128, T], F32)
        moech = b.sbuf("moech", [128, T], F32)
        x1ch = b.sbuf("x1ch", [128, T], F32)
        x2ch = b.sbuf("x2ch", [128, T], F32)
        outch = ft32
        mtile = [b.sbuf(f"mtile{i}", [128, 128], F32) for i in range(2)]
        PS = [b.psum(f"PS{i}", [128, 512], F32) for i in range(8)]
        RC = 1.5 * 2 ** 23

        b.dma("sync", ident[:], ident_in)
        b.dma("sync", cnt8[:], cnt_in)
        b.dma("sync", pmc[:], pm_in)
        b.dma("sync", b2c[:], b2f_in)
        for k, (ii, rr, gg) in enumerate([(idx1_in, rr1_in, g1_in),
                                          (idx2_in, rr2_in, g2_in)]):
            b.dma("sync", idxa[k][:], ii)
            b.dma("sync", rra[k][:], rr)
            b.dma("sync", ga[k][:], gg)
        b.op("vector", lambda v: v.memset(ones_row[:], 1.0), writes=[ones_row[:]])

        b.dma("sync", sc1[:], mins_in)
        b.op("vector", lambda v: v.tensor_reduce(out=gmin1[:], in_=sc1[:],
             axis=mybir.AxisListType.X, op=OP.min), reads=[sc1[:]], writes=[gmin1[:]])
        b.dma("sync", sc1[:], maxs_in)
        b.op("vector", lambda v: v.reduce_max(out=gmax1[:], in_=sc1[:],
             axis=mybir.AxisListType.X), reads=[sc1[:]], writes=[gmax1[:]])
        b.op("vector", lambda v: v.tensor_tensor(out=scl[:], in0=gmax1[:],
             in1=gmin1[:], op=OP.subtract), reads=[gmax1[:], gmin1[:]], writes=[scl[:]])
        b.op("vector", lambda v: v.tensor_scalar(out=scl[:], in0=scl[:],
             scalar1=1.0 / 255.0, scalar2=1e-8, op0=OP.mult, op1=OP.max),
             reads=[scl[:]], writes=[scl[:]])
        b.op("vector", lambda v: v.reciprocal(out=zpt[:], in_=scl[:]),
             reads=[scl[:]], writes=[zpt[:]])
        b.op("vector", lambda v: v.tensor_tensor(out=gmin1[:], in0=gmin1[:],
             in1=zpt[:], op=OP.mult), reads=[gmin1[:], zpt[:]], writes=[gmin1[:]])
        b.op("vector", lambda v: v.tensor_scalar(out=gmin1[:], in0=gmin1[:],
             scalar1=-1.0, scalar2=RC, op0=OP.mult, op1=OP.add),
             reads=[gmin1[:]], writes=[gmin1[:]])
        b.op("vector", lambda v: v.tensor_scalar(out=gmin1[:], in0=gmin1[:],
             scalar1=-RC, scalar2=-128.0, op0=OP.add, op1=OP.add),
             reads=[gmin1[:]], writes=[gmin1[:]])
        b.op("vector", lambda v: v.tensor_scalar(out=gmin1[:], in0=gmin1[:],
             scalar1=127.0, scalar2=-128.0, op0=OP.min, op1=OP.max),
             reads=[gmin1[:]], writes=[gmin1[:]])
        b.op("tensor", lambda t: t.matmul(PS[0][:, 0:1], ones_row[:], zpt[:],
             start=True, stop=True), reads=[ones_row[:], zpt[:]], writes=[PS[0][:, 0:1]])
        b.op("vector", lambda v: v.tensor_copy(invs_col[:], PS[0][:, 0:1]),
             reads=[PS[0][:, 0:1]], writes=[invs_col[:]])
        b.op("tensor", lambda t: t.matmul(PS[0][:, 0:1], ones_row[:], gmin1[:],
             start=True, stop=True), reads=[ones_row[:], gmin1[:]], writes=[PS[0][:, 0:1]])
        b.op("vector", lambda v: v.tensor_copy(zp_col[:], PS[0][:, 0:1]),
             reads=[PS[0][:, 0:1]], writes=[zp_col[:]])

        b.op("tensor", lambda t: t.matmul(PS[0][0:1, 0:8], pmc[:], cnt8[:],
             start=True, stop=True), reads=[pmc[:], cnt8[:]], writes=[PS[0][0:1, 0:8]])
        b.op("vector", lambda v: v.tensor_copy(strow[:], PS[0][0:1, 0:8]),
             reads=[PS[0][0:1, 0:8]], writes=[strow[:]])
        for e in range(8):
            h_bcast_row(b, PS[1], stecol[:, e:e + 1], strow[0:1, e:e + 1], ones_row, 1)
        for k in range(2):
            b.op("vector", lambda v, k=k: v.memset(ssel[:], 0.0), writes=[ssel[:]])
            for e in range(8):
                b.op("vector", lambda v, k=k, e=e: v.tensor_scalar(out=eqt[:],
                     in0=idxa[k][:], scalar1=float(e), scalar2=None, op0=OP.is_equal),
                     reads=[idxa[k][:]], writes=[eqt[:]])
                b.op("vector", lambda v, e=e: v.scalar_tensor_tensor(out=ssel[:],
                     in0=eqt[:], scalar=stecol[:, e:e + 1], in1=ssel[:],
                     op0=OP.mult, op1=OP.add),
                     reads=[eqt[:], stecol[:, e:e + 1], ssel[:]], writes=[ssel[:]])
            b.op("vector", lambda v, k=k: v.tensor_tensor(out=grk[:], in0=rra[k][:],
                 in1=ssel[:], op=OP.add), reads=[rra[k][:], ssel[:]], writes=[grk[:]])
            b.op("vector", lambda v: v.tensor_scalar(out=kpt[:], in0=grk[:],
                 scalar1=float(NCAP), scalar2=None, op0=OP.is_lt),
                 reads=[grk[:]], writes=[kpt[:]])
            b.op("vector", lambda v, k=k: v.scalar_tensor_tensor(out=tmq[:],
                 in0=idxa[k][:], scalar=float(NCAP), in1=grk[:],
                 op0=OP.mult, op1=OP.add),
                 reads=[idxa[k][:], grk[:]], writes=[tmq[:]])
            b.op("vector", lambda v: v.tensor_tensor(out=tmq[:], in0=tmq[:],
                 in1=kpt[:], op=OP.mult), reads=[tmq[:], kpt[:]], writes=[tmq[:]])
            b.op("vector", lambda v: v.tensor_scalar(out=eqt[:], in0=kpt[:],
                 scalar1=-1.0e9, scalar2=1.0e9, op0=OP.mult, op1=OP.add),
                 reads=[kpt[:]], writes=[eqt[:]])
            b.op("vector", lambda v: v.tensor_tensor(out=tmq[:], in0=tmq[:],
                 in1=eqt[:], op=OP.add), reads=[tmq[:], eqt[:]], writes=[tmq[:]])
            b.op("vector", lambda v, k=k: v.tensor_copy(didx[k][:], tmq[:]),
                 reads=[tmq[:]], writes=[didx[k][:]])
            b.op("vector", lambda v, k=k: v.tensor_tensor(out=geff[k][:],
                 in0=ga[k][:], in1=kpt[:], op=OP.mult),
                 reads=[ga[k][:], kpt[:]], writes=[geff[k][:]])

        b.op("vector", lambda v: v.memset(e1t[:], 0.0), writes=[e1t[:]])
        b.op("vector", lambda v: v.memset(e2t[:], 0.0), writes=[e2t[:]])
        for j in range(16):
            b.idma(e1t[:], cmp_in, didx[0][:, j:j + 1], "in", 8 * NCAP - 1,
                   reads=[cmp_in], writes=[e1t[:]])
            b.idma(e2t[:], cmp_in, didx[1][:, j:j + 1], "in", 8 * NCAP - 1,
                   reads=[cmp_in], writes=[e2t[:]])
            b.op("vector", lambda v, j=j: v.tensor_scalar(out=m1t[:], in0=e1t[:],
                 scalar1=geff[0][:, j:j + 1], scalar2=None, op0=OP.mult),
                 reads=[e1t[:], geff[0][:, j:j + 1]], writes=[m1t[:]])
            b.op("vector", lambda v, j=j: v.scalar_tensor_tensor(out=m1t[:],
                 in0=e2t[:], scalar=geff[1][:, j:j + 1], in1=m1t[:],
                 op0=OP.mult, op1=OP.add),
                 reads=[e2t[:], geff[1][:, j:j + 1], m1t[:]], writes=[m1t[:]])
            b.dma("sync", moe_dram[128 * j:128 * (j + 1), :], m1t[:])

        for i in range(32):
            xb = hb[i % 2]
            b.dma("sync", xb[:], h2_in[128 * i:128 * (i + 1), :])
            b.op("vector", lambda v, xb=xb: v.tensor_scalar(out=ft32[:], in0=xb[:],
                 scalar1=invs_col[:], scalar2=RC, op0=OP.mult, op1=OP.add),
                 reads=[xb[:], invs_col[:]], writes=[ft32[:]])
            b.op("vector", lambda v: v.tensor_scalar(out=ft32[:], in0=ft32[:],
                 scalar1=-RC, scalar2=zp_col[:], op0=OP.add, op1=OP.add),
                 reads=[ft32[:], zp_col[:]], writes=[ft32[:]])
            b.op("vector", lambda v: v.tensor_scalar(out=ft32[:], in0=ft32[:],
                 scalar1=127.0, scalar2=-128.0, op0=OP.min, op1=OP.max),
                 reads=[ft32[:]], writes=[ft32[:]])
            b.op("vector", lambda v, i=i: v.tensor_scalar(out=xq2z[i][:], in0=ft32[:],
                 scalar1=zp_col[:], scalar2=None, op0=OP.subtract),
                 reads=[ft32[:], zp_col[:]], writes=[xq2z[i][:]])

        for i in range(32):
            ws = wst[i % 2]
            b.dma("sync", ws[:], w2_in[128 * i:128 * (i + 1), :])
            b.op("scalar", lambda s, ws=ws: s.activation(out=ws[:], in_=ws[:],
                 func=AF.Abs), reads=[ws[:]], writes=[ws[:]])
            if i == 0:
                b.op("vector", lambda v, ws=ws: v.reduce_max(out=wmaxc[:], in_=ws[:],
                     axis=mybir.AxisListType.X), reads=[ws[:]], writes=[wmaxc[:]])
            else:
                b.op("vector", lambda v, ws=ws: v.reduce_max(out=m1[:], in_=ws[:],
                     axis=mybir.AxisListType.X), reads=[ws[:]], writes=[m1[:]])
                b.op("vector", lambda v: v.tensor_tensor(out=wmaxc[:], in0=wmaxc[:],
                     in1=m1[:], op=OP.max), reads=[wmaxc[:], m1[:]], writes=[wmaxc[:]])
        b.op("tensor", lambda t: t.transpose(PS[0][0:1, 0:128], wmaxc[:], ident[:]),
             reads=[wmaxc[:], ident[:]], writes=[PS[0][0:1, 0:128]])
        b.op("vector", lambda v: v.tensor_copy(wrow16[:], PS[0][0:1, 0:128]),
             reads=[PS[0][0:1, 0:128]], writes=[wrow16[:]])
        b.op("vector", lambda v: v.reduce_max(out=wrow16[0:1, 0:1], in_=wrow16[:],
             axis=mybir.AxisListType.X), reads=[wrow16[:]], writes=[wrow16[:]])
        b.op("vector", lambda v: v.tensor_scalar(out=wrow16[0:1, 0:1],
             in0=wrow16[0:1, 0:1], scalar1=1.0 / 127.0, scalar2=1e-8,
             op0=OP.mult, op1=OP.max), reads=[wrow16[:]], writes=[wrow16[:]])
        b.op("vector", lambda v: v.tensor_tensor(out=scl[:], in0=scl[:],
             in1=wrow16[0:1, 0:1], op=OP.mult), reads=[scl[:], wrow16[:]], writes=[scl[:]])
        b.op("tensor", lambda t: t.matmul(PS[0][:, 0:1], ones_row[:], scl[:],
             start=True, stop=True), reads=[ones_row[:], scl[:]], writes=[PS[0][:, 0:1]])
        b.op("vector", lambda v: v.tensor_copy(comb_col[:], PS[0][:, 0:1]),
             reads=[PS[0][:, 0:1]], writes=[comb_col[:]])
        b.op("vector", lambda v: v.reciprocal(out=invsw[:], in_=wrow16[0:1, 0:1]),
             reads=[wrow16[:]], writes=[invsw[:]])
        b.op("tensor", lambda t: t.matmul(PS[0][:, 0:1], ones_row[:], invsw[:],
             start=True, stop=True), reads=[ones_row[:], invsw[:]], writes=[PS[0][:, 0:1]])
        b.op("vector", lambda v: v.tensor_copy(invswc[:], PS[0][:, 0:1]),
             reads=[PS[0][:, 0:1]], writes=[invswc[:]])
        for i in range(32):
            ws = wst[i % 2]
            wq_ = wqt[i % 2]
            b.dma("sync", ws[:], w2_in[128 * i:128 * (i + 1), :])
            b.op("vector", lambda v, ws=ws: v.tensor_scalar(out=ws[:], in0=ws[:],
                 scalar1=invswc[:], scalar2=RC, op0=OP.mult, op1=OP.add),
                 reads=[ws[:], invswc[:]], writes=[ws[:]])
            b.op("vector", lambda v, ws=ws: v.tensor_scalar(out=ws[:], in0=ws[:],
                 scalar1=-RC, scalar2=127.0, op0=OP.add, op1=OP.min),
                 reads=[ws[:]], writes=[ws[:]])
            b.op("vector", lambda v, ws=ws, wq_=wq_: v.tensor_scalar(out=wq_[:], in0=ws[:],
                 scalar1=-127.0, scalar2=None, op0=OP.max),
                 reads=[ws[:]], writes=[wq_[:]])
            b.dma("sync", wq2_dram[128 * i:128 * (i + 1), :], wq_[:])

        for of in range(8):
            wc = wqcol[of % 2]
            b.dma("sync", wc[:].rearrange("p (c j) -> p c j", c=32),
                  wq2_dram[:, 128 * of:128 * (of + 1)].rearrange("(c p) j -> p c j", p=128))
            for cs in range(4):
                ps = PS[4 + cs % 2]
                sl = slice(512 * cs, 512 * (cs + 1))
                for i in range(32):
                    b.op("tensor", lambda t, i=i, sl=sl, ps=ps, wc=wc: t.matmul(
                        ps[:, :], wc[:, 128 * i:128 * (i + 1)], xq2z[i][:, sl],
                        start=(i == 0), stop=(i == 31)),
                        reads=[wc[:, 128 * i:128 * (i + 1)], xq2z[i][:, sl]],
                        writes=[ps[:, :]])
                b.op("scalar", lambda s, sl=sl, ps=ps, of=of: s.activation(
                    out=g2ch[:, sl], in_=ps[:, :], func=AF.Silu,
                    scale=comb_col[:], bias=b2c[:, of:of + 1]),
                    reads=[ps[:, :], comb_col[:], b2c[:, of:of + 1]],
                    writes=[g2ch[:, sl]])
            for j in range(16):
                mt = mtile[j % 2]
                b.dma("sync", mt[:],
                      moe_dram[128 * j:128 * (j + 1), 128 * of:128 * (of + 1)])
                b.op("tensor", lambda t, mt=mt: t.transpose(PS[2][:, 0:128], mt[:],
                     ident[:]), reads=[mt[:], ident[:]], writes=[PS[2][:, 0:128]])
                b.op("vector", lambda v, j=j: v.tensor_copy(
                    moech[:, 128 * j:128 * (j + 1)], PS[2][:, 0:128]),
                    reads=[PS[2][:, 0:128]], writes=[moech[:, 128 * j:128 * (j + 1)]])
            b.dma("sync", x1ch[:], x1_in[128 * of:128 * (of + 1), :])
            b.dma("sync", x2ch[:], x2_in[128 * of:128 * (of + 1), :])
            b.op("vector", lambda v: v.tensor_tensor(out=x2ch[:], in0=x2ch[:],
                 in1=moech[:], op=OP.add), reads=[x2ch[:], moech[:]], writes=[x2ch[:]])
            b.op("vector", lambda v: v.tensor_tensor(out=outch[:], in0=x2ch[:],
                 in1=g2ch[:], op=OP.mult), reads=[x2ch[:], g2ch[:]], writes=[outch[:]])
            b.op("vector", lambda v: v.tensor_tensor(out=outch[:], in0=outch[:],
                 in1=x1ch[:], op=OP.add), reads=[outch[:], x1ch[:]], writes=[outch[:]])
            b.dma("sync", out_o[128 * of:128 * (of + 1), :], outch[:])
        b.finalize()
    return nc



def _splits(full):
    out = []
    for c in range(NCORES):
        bb, qq = c // 4, c % 4
        out.append(full[bb, 2048 * qq:2048 * (qq + 1)])
    return out

_IDENT = np.eye(128, dtype=np.float32)
_TRIMASK = np.where(np.arange(128)[None, :] >= np.arange(128)[:, None],
                    0.0, -1.0e4).astype(np.float32)

def run_pipeline(inputs, upto=99, keep=None):
    st = {}
    x = np.asarray(inputs["L0_attn_inputs"])
    WQ = np.asarray(inputs["L0_attn_WQ"]).transpose(1, 0, 2).reshape(D, D)
    WK = np.asarray(inputs["L0_attn_WK"]).transpose(1, 0, 2).reshape(D, D)
    WVd = np.asarray(inputs["L0_attn_WV_down"]).transpose(1, 0, 2).reshape(D, D)
    wqkd = np.ascontiguousarray(np.concatenate([WQ, WK, WVd], axis=1))
    in_maps = []
    for c in range(NCORES):
        bb, qq = c // 4, c % 4
        xc = np.ascontiguousarray(x[bb, 2048 * qq:2048 * (qq + 1), :].T)
        in_maps.append(dict(x_fm=xc, wqkd_fm=wqkd, ident=_IDENT))
    nc = build_L1()
    res = run_bass_kernel_spmd(nc, in_maps, core_ids=list(range(NCORES)))
    L1 = res.results
    st["L1"] = L1
    if upto <= 1:
        return st

    maxxn_all = np.concatenate([L1[c]["maxabs_xn"] for c in range(NCORES)], axis=1)
    wg_t = np.ascontiguousarray(np.asarray(inputs["L0_ffn_w_gate"]).T)
    wu_t = np.ascontiguousarray(np.asarray(inputs["L0_ffn_w_up"]).T)
    clip_x = np.asarray(inputs["L0_ffn_clip_x"]).reshape(1, 1).astype(np.float32)
    in_maps = []
    for c in range(NCORES):
        bu, hu = c // 4, 4 * (c % 4)
        r0 = 64 * hu
        qh = np.concatenate(
            [L1[4 * bu + qq]["qkd_bf"][r0:r0 + 256, :] for qq in range(4)], axis=1)
        kh = np.concatenate(
            [L1[4 * bu + qq]["qkd_bf"][D + r0:D + r0 + 256, :] for qq in range(4)], axis=1)
        in_maps.append(dict(qh_bf=np.ascontiguousarray(qh),
                            kh_bf=np.ascontiguousarray(kh),
                            xn_fm=L1[c]["xn_fm"], maxxn_all=maxxn_all,
                            clip_x=clip_x, wg_t=wg_t, wu_t=wu_t, ident=_IDENT,
                            trimask=_TRIMASK))
    nc = build_L2()
    res = run_bass_kernel_spmd(nc, in_maps, core_ids=list(range(NCORES)))
    L2 = res.results
    st["L2"] = L2
    if upto <= 2:
        return st

    diag_full = []
    for bb in range(2):
        rows = np.zeros((16, 8192), np.float32)
        for k in range(4):
            arr = L2[4 * bb + k]["attn_diag"].reshape(4, 128, 64)
            for u in range(4):
                rows[4 * k + u] = arr[u].T.reshape(8192)
        diag_full.append(rows)
    maxh_all = np.concatenate([L2[c]["maxh"] for c in range(NCORES)], axis=1)
    wd_t = np.ascontiguousarray(np.asarray(inputs["L0_ffn_w_down"]).T)
    wvu_fm = np.ascontiguousarray(
        np.asarray(inputs["L0_attn_WV_up"]).reshape(D, D))
    clip_down = np.asarray(inputs["L0_ffn_clip_down"]).reshape(1, 1).astype(np.float32)
    m0 = np.zeros((1, 128), np.float32); m0[0, 0:64] = 1.0
    m1 = np.zeros((1, 128), np.float32); m1[0, 64:128] = 1.0
    ns_col = np.ascontiguousarray(
        np.asarray(inputs["L1_attn_norm1_scale"]).reshape(8, 128).T).astype(np.float32)
    in_maps = []
    for c in range(NCORES):
        bb, qq = c // 4, c % 4
        in_maps.append(dict(
            x_fm=np.ascontiguousarray(x[bb, 2048 * qq:2048 * (qq + 1), :].T),
            xn_fm=L1[c]["xn_fm"], h_fm=L2[c]["h_fm"], maxh_all=maxh_all,
            clip_down=clip_down,
            diag16=np.ascontiguousarray(diag_full[bb][:, 2048 * qq:2048 * (qq + 1)]),
            dvdown_bf=np.ascontiguousarray(L1[c]["qkd_bf"][2 * D:3 * D, :]),
            wd_t=wd_t, wvu_fm=wvu_fm, mask0=m0, mask1=m1, ident=_IDENT,
            norm1_scale_col=ns_col))
    nc = build_L3a()
    res = run_bass_kernel_spmd(nc, in_maps, core_ids=list(range(NCORES)))
    L3a = res.results
    st["L3a"] = L3a
    if upto <= 3:
        return st

    A = lambda k: np.asarray(inputs[f"L1_attn_{k}"]).astype(np.float32)
    wqkvg = np.ascontiguousarray(np.concatenate(
        [A("Wq"), A("Wk"), A("Wv"), A("Wg")], axis=1))
    bq_col = np.ascontiguousarray(A("bq").reshape(8, 128).T)
    bkvg_rows = np.concatenate([A("bk"), A("bv"), A("bg")]).reshape(1, 3 * D)
    jj = np.arange(32, dtype=np.float64)
    inv = 1.0 / 10000.0 ** (jj / 32.0)
    pswap_m = np.zeros((128, 128), np.float32)
    for p in range(128):
        pswap_m[(p // 64) * 64 + ((p % 64) + 32) % 64, p] = 1.0
    in_maps = []
    for c in range(NCORES):
        bb, qq = c // 4, c % 4
        s = np.arange(2048 * qq, 2048 * (qq + 1), dtype=np.float64)
        ang = (s[:, None] * inv[None, :]).astype(np.float32)
        cosk = np.cos(ang).astype(np.float32)
        sink = np.sin(ang).astype(np.float32)
        cb = np.zeros((128, T), np.float32)
        sb = np.zeros((128, T), np.float32)
        for p in range(128):
            j = p % 32
            cb[p] = cosk[:, j]
            sb[p] = (-sink[:, j]) if (p % 64) < 32 else sink[:, j]
        in_maps.append(dict(
            xr1_bf=L3a[c]["xr1_bf"], wqkvg=wqkvg, bq_col=bq_col,
            bkvg_rows=bkvg_rows, CBIG=cb, SBIG=sb, cosK=cosk, sinK=sink,
            PSWAP=pswap_m))
    nc = build_L3b()
    res = run_bass_kernel_spmd(nc, in_maps, core_ids=list(range(NCORES)))
    L3b = res.results
    st["L3b"] = L3b
    if upto <= 4:
        return st

    wo = np.ascontiguousarray(A("Wo"))
    bo_col = np.ascontiguousarray(A("bo").reshape(8, 128).T)
    ns2_col = np.ascontiguousarray(A("norm2_scale").reshape(8, 128).T)
    identb = np.eye(128, dtype=ml_dtypes.bfloat16)
    in_maps = []
    for c in range(NCORES):
        bb, qq = c // 4, c % 4
        dsp = np.zeros((3, 64 * D), np.float32)
        zp = np.zeros((3, 64 * 16), np.float32)
        for j, q2 in enumerate(range(qq)):
            dsp[j] = L3b[4 * bb + q2]["ds_tot"].reshape(-1)
            zp[j] = L3b[4 * bb + q2]["z_tot"].reshape(-1)
        in_maps.append(dict(
            qro_bf=L3b[c]["qro_bf"], k_tm=L3b[c]["k_tm"], v_tm=L3b[c]["v_tm"],
            g_tm=L3b[c]["g_tm"], ds_prev=dsp, z_prev=zp,
            xn1_fm=L3a[c]["xn1_fm"], wo=wo, bo_col=bo_col,
            norm2_scale_col=ns2_col, ident=_IDENT, identb=identb))
    nc = build_L4a()
    res = run_bass_kernel_spmd(nc, in_maps, core_ids=list(range(NCORES)))
    L4a = res.results
    st["L4a"] = L4a
    if upto <= 5:
        return st

    mins = np.concatenate([L3a[c]["minmax_xn1"][:, 0:1] for c in range(NCORES)], axis=1)
    maxs = np.concatenate([L3a[c]["minmax_xn1"][:, 1:2] for c in range(NCORES)], axis=1)
    w1t = np.ascontiguousarray(np.asarray(inputs["L1_ffn_W1"]).T)
    b1_col = np.ascontiguousarray(np.asarray(inputs["L1_ffn_b1"]).reshape(32, 128).T)
    iota8 = np.arange(8, dtype=np.float32).reshape(1, 8)
    rb = A("router_b").reshape(1, 8)
    rw = np.ascontiguousarray(A("router_W"))
    in_maps = []
    for c in range(NCORES):
        in_maps.append(dict(
            xr2_bf=L4a[c]["xr2_bf"], xn1_fm=L3a[c]["xn1_fm"], mins=mins, maxs=maxs,
            router_w=rw, router_b=rb, w1t_ffn=w1t, b1_col=b1_col, iota8=iota8,
            ident=_IDENT, identb=identb))
    nc = build_L4b()
    res = run_bass_kernel_spmd(nc, in_maps, core_ids=list(range(NCORES)))
    L4b = res.results
    st["L4b"] = L4b
    if upto <= 6:
        return st

    cnts = np.concatenate([L4b[c]["cnt_row"] for c in range(NCORES)], axis=0)
    ltri8 = np.triu(np.ones((8, 8), np.float32), k=1)
    W1e = np.asarray(inputs["L1_attn_W1"]).astype(np.float32)
    W2e = np.asarray(inputs["L1_attn_W2"]).astype(np.float32)
    b1e = np.asarray(inputs["L1_attn_b1"]).astype(np.float32)
    b2e = np.asarray(inputs["L1_attn_b2"]).astype(np.float32)
    in_maps = []
    for c in range(NCORES):
        cmp_my = np.concatenate(
            [L4b[s]["cmp_part"][2048 * c:2048 * (c + 1)] for s in range(NCORES)], axis=0)
        in_maps.append(dict(
            cmp_my=np.ascontiguousarray(cmp_my), cnts=cnts,
            w1e=np.ascontiguousarray(W1e[c]), w2e=np.ascontiguousarray(W2e[c]),
            b1e_col=np.ascontiguousarray(b1e[c].reshape(32, 128).T),
            b2e_col=np.ascontiguousarray(b2e[c].reshape(8, 128).T),
            myexp_col=np.full((128, 1), float(c), np.float32),
            ltri8=ltri8, ident=_IDENT, identb=identb))
    nc = build_L5()
    res = run_bass_kernel_spmd(nc, in_maps, core_ids=list(range(NCORES)))
    L5 = res.results
    st["L5"] = L5
    if upto <= 7:
        return st

    mins2 = np.concatenate([L4b[c]["minmax_h2"][:, 0:1] for c in range(NCORES)], axis=1)
    maxs2 = np.concatenate([L4b[c]["minmax_h2"][:, 1:2] for c in range(NCORES)], axis=1)
    cmp_all = np.ascontiguousarray(
        np.concatenate([L5[e]["cmp_out"] for e in range(NCORES)], axis=0))
    w2f = np.ascontiguousarray(np.asarray(inputs["L1_ffn_W2"]).astype(np.float32).T)
    b2f_col = np.ascontiguousarray(
        np.asarray(inputs["L1_ffn_b2"]).astype(np.float32).reshape(8, 128).T)
    in_maps = []
    for c in range(NCORES):
        pm = np.zeros((8, 1), np.float32)
        pm[:c] = 1.0
        in_maps.append(dict(
            x1_fm=L3a[c]["x1_fm"], x2_fm=L4a[c]["x2_fm"], h2_bf=L4b[c]["h2_bf"],
            mins2=mins2, maxs2=maxs2, w2_ffn=w2f, b2f_col=b2f_col,
            cmp_all=cmp_all, cnts=cnts, premask_col=pm,
            idx1_tm=np.ascontiguousarray(L4b[c]["ridx1"].T),
            idx2_tm=np.ascontiguousarray(L4b[c]["ridx2"].T),
            rr1_tm=np.ascontiguousarray(L4b[c]["rr1"].reshape(16, 128).T),
            rr2_tm=np.ascontiguousarray(L4b[c]["rr2"].reshape(16, 128).T),
            g1_tm=np.ascontiguousarray(L4b[c]["rg1"].T),
            g2_tm=np.ascontiguousarray(L4b[c]["rg2"].T),
            ident=_IDENT))
    nc = build_L6()
    res = run_bass_kernel_spmd(nc, in_maps, core_ids=list(range(NCORES)))
    L6 = res.results
    st["L6"] = L6
    return st


def kernel(**inputs):
    st = run_pipeline(inputs, upto=99)
    out = np.zeros((2, 8192, 1024), np.float32)
    for c in range(NCORES):
        bb, qq = c // 4, c % 4
        out[bb, 2048 * qq:2048 * (qq + 1), :] = st["L6"][c]["out_fm"].T
    return out
